# version 02
# baseline (speedup 1.0000x reference)
import numpy as np
import ml_dtypes
import concourse.bass as bass
import concourse.mybir as mybir
from concourse.bass_utils import run_bass_kernel_spmd
from contextlib import ExitStack

F32 = mybir.dt.float32
BF16 = mybir.dt.bfloat16
I32 = mybir.dt.int32
AF = mybir.ActivationFunctionType
ALU = mybir.AluOpType

D = 2048
HID = 5632
NH = 16
DKV = 128
SCALE = DKV ** -0.5
EPS_LN = 1e-5
EPS_RMS = 1e-5
EPS_Q = 1e-8
C = 64
NKD = D // 128
NKH = HID // 128
TT = 512
NGRP = TT // 128
NPAIR = TT // 128


class Prog:
    ENGINES = ("sync", "gpsimd", "scalar", "vector", "tensor")

    def __init__(self):
        self.ops = {e: [] for e in self.ENGINES}
        self.cnt = {}

    def val(self, sem):
        return self.cnt.get(sem, 0)

    def emit(self, engine, fn, *args, incs=(), waits=(), **kwargs):
        for s, v in waits:
            if v > 0:
                assert v <= self.val(s), \
                    f"wait {s}>={v} but only {self.val(s)} emitted ({engine}.{fn})"
                self.ops[engine].append(("wait", s, v))
        self.ops[engine].append(("op", fn, args, kwargs, list(incs)))
        for s, n in incs:
            self.cnt[s] = self.cnt.get(s, 0) + n

    def wait(self, engine, sem, value):
        self.ops[engine].append(("wait", sem, value))

    def replay(self, engine_name, eng, sems):
        last_wait = {}
        for item in self.ops[engine_name]:
            if item[0] == "wait":
                _, s, v = item
                if last_wait.get(s, -1) < v:
                    eng.wait_ge(sems[s], v)
                    last_wait[s] = v
            else:
                _, fn, args, kwargs, incs = item
                try:
                    r = getattr(eng, fn)(*args, **kwargs)
                except Exception:
                    shapes = [getattr(a, 'shape', a) for a in args]
                    print(f"REPLAY FAIL {engine_name}.{fn} args={shapes} kw={list(kwargs)}")
                    raise
                for s, n in incs:
                    r.then_inc(sems[s], n)

    def n_ops(self):
        return {e: len(v) for e, v in self.ops.items()}


def build(T_core, dbg=False):
    n_tiles = T_core // TT
    assert T_core % TT == 0
    nc = bass.Bass("TRN2", target_bir_lowering=False, debug=False,
                   detect_race_conditions=False)

    x_d = nc.dram_tensor("x", [T_core, D], F32, kind="ExternalInput").ap()
    wq_d = nc.dram_tensor("wq", [D, D], F32, kind="ExternalInput").ap()
    wk_d = nc.dram_tensor("wk", [D, D], F32, kind="ExternalInput").ap()
    wi_d = nc.dram_tensor("wi", [D, D], F32, kind="ExternalInput").ap()
    wo_d = nc.dram_tensor("wo", [D, D], F32, kind="ExternalInput").ap()
    gw_d = nc.dram_tensor("gwt", [D], F32, kind="ExternalInput").ap()
    w1_d = nc.dram_tensor("w1t", [D, HID], F32, kind="ExternalInput").ap()
    w2_d = nc.dram_tensor("w2t", [D, HID], F32, kind="ExternalInput").ap()
    w3_d = nc.dram_tensor("w3t", [HID, D], F32, kind="ExternalInput").ap()
    out_d = nc.dram_tensor("out", [T_core, D], F32, kind="ExternalOutput").ap()
    dbg_d = {}
    if dbg:
        for nm, shp, dt_ in (("xf", [128, NKD * TT], BF16), ("xl", [128, NKD * TT], BF16),
                             ("qt0", [128, NH * TT], BF16), ("sig0", [128, NH * TT], BF16),
                             ("vft0", [128, NH * TT], BF16), ("sgn0", [128, 8 * TT], F32),
                             ("qt1", [128, NH * TT], BF16), ("sig1", [128, NH * TT], BF16),
                             ("os0", [128, NH * TT], BF16), ("sf0", [128, NH * DKV], F32),
                             ("orms", [128, NH * TT], BF16), ("y0", [128, NGRP * D], F32),
                             ("hid0", [128, NKH * TT], BF16),
                             ("w1q", [128, HID], BF16), ("w3q", [128, D], BF16),
                             ("w1full", [NKD * 128, HID], BF16),
                             ("tsc1", [1, 4], F32), ("tsc2", [1, 4], F32),
                             ("tsc3", [1, 4], F32), ("grow1", [1, TT], F32)):
            dbg_d[nm] = nc.dram_tensor("dbg_" + nm, shp, dt_, kind="ExternalOutput").ap()
    wsc = {
        "q": nc.dram_tensor("wq_s", [NKD, 128, D], BF16, kind="Internal").ap(),
        "k": nc.dram_tensor("wk_s", [NKD, 128, D], BF16, kind="Internal").ap(),
        "i": nc.dram_tensor("wi_s", [NKD, 128, D], BF16, kind="Internal").ap(),
        "o": nc.dram_tensor("wo_s", [NKD, 128, D], BF16, kind="Internal").ap(),
        "1": nc.dram_tensor("w1_s", [NKD, 128, HID], BF16, kind="Internal").ap(),
        "2": nc.dram_tensor("w2_s", [NKD, 128, HID], BF16, kind="Internal").ap(),
        "3": nc.dram_tensor("w3_s", [NKH, 128, D], BF16, kind="Internal").ap(),
    }
    gsc_d = nc.dram_tensor("gsc", [128, 4], F32, kind="Internal").ap()

    with ExitStack() as ctx:
        def sb(name, shape, dtype):
            return ctx.enter_context(nc.sbuf_tensor(name, shape, dtype)).ap()

        xt = sb("xt", [128, NGRP * D], F32)
        bigbuf = sb("bigbuf", [128, 3 * NH * TT], BF16)
        qt = bigbuf[:, 0:NH * TT]
        sig = bigbuf[:, NH * TT:2 * NH * TT]
        vft = bigbuf[:, 2 * NH * TT:3 * NH * TT]
        hid_b = bigbuf[:, 0:NKH * TT]
        xf = sb("xf", [128, NKD * TT], BF16)
        xl = sb("xl", [128, NKD * TT], BF16)
        sgn = sb("sgn", [128, 8 * TT], F32)
        o_s = sb("o_s", [128, NH * TT], BF16)
        slab = sb("slab", [128, NKD * TT], BF16)
        s_f = sb("s_f", [128, NH * DKV], F32)
        s_bf = sb("s_bf", [128, NH * DKV], BF16)
        g_roll = sb("g_roll", [128, 2 * TT], F32)
        lp_roll = sb("lp_roll", [128, 2 * TT], F32)
        ep_roll = sb("ep_roll", [128, 2 * TT], BF16)
        em_roll = sb("em_roll", [128, 2 * TT], BF16)
        kh_roll = sb("kh_roll", [128, 2 * TT], BF16)
        qh_roll = sb("qh_roll", [128, TT], BF16)
        dpb_roll = sb("dpb_roll", [128, 2 * NPAIR * DKV], BF16)
        vtok_roll = sb("vtok_roll", [128, 2 * NPAIR * DKV], BF16)
        ktok_roll = sb("ktok_roll", [128, 2 * DKV], BF16)
        a_roll = sb("a_roll", [128, 2 * 128], BF16)
        el16 = sb("el16", [128, 2 * 16], BF16)
        pt16 = sb("pt16", [16, 2 * 128], BF16)
        sq_roll = sb("sq_roll", [128, 2 * TT], BF16)
        rowf = sb("rowf", [1, 2 * TT], F32)
        rowb = sb("rowb", [1, 2 * TT], BF16)
        tb_f = lp_roll[:, 0:TT]
        tnb_f = sgn[:, 0:TT]
        cast_f = xt[:, 0:2 * D]
        cast_b = xf[:, 0:2 * D]
        qrollA = xl[:, 0:TT]
        qrollB = xl[:, TT:2 * TT]
        gwsb = sb("gwsb", [128, NKD], F32)
        gwrow = sb("gwrow", [1, 128], BF16)
        gwbc = sb("gwbc", [128, 128], BF16)
        gwdiag = sb("gwdiag", [128, 128], BF16)
        ones1 = sb("ones1", [128, 1], BF16)
        ones1f = sb("ones1f", [128, 1], F32)
        onesk1 = sb("onesk1", [1, 128], BF16)
        onesk1f = sb("onesk1f", [1, 128], F32)
        onesrowf = rowf[:, 0:TT]
        ones8 = sb("ones8", [128, 8], BF16)
        zer_b = sb("zer_b", [128, C], BF16)
        trimask = sb("trimask", [128, 128], BF16)
        ident = sb("ident", [128, 128], BF16)
        maskC = sb("maskC", [128, TT], F32)
        sel_fb = sb("sel_fb", [16, TT], BF16)
        sel_d0 = sb("sel_d0", [16, TT // 2], BF16)
        sel_dp = sb("sel_dp", [16, NPAIR * DKV], BF16)
        io_a = o_s[:, 0:256].bitcast(I32)
        io_fb = sgn[0:16, 0:TT].bitcast(I32)
        io_d0 = sgn[0:16, TT:TT + TT // 2].bitcast(I32)
        io_dp = sgn[0:16, 2 * TT:2 * TT + NPAIR * DKV].bitcast(I32)
        tsc = sb("tsc", [1, 4], F32)
        epst = sb("epst", [1, 1], F32)
        trow = rowf[:, TT:2 * TT]

        pb = [ctx.enter_context(nc.psum_tensor("pb%d" % i, [128, TT], F32)).ap()
              for i in range(8)]
        pbh = [p.bitcast(BF16) for p in pb]

        sems = {}
        for s in ("dma", "gdma", "odma", "S", "V", "T", "G"):
            sems[s] = ctx.enter_context(nc.semaphore(s))

        block = ctx.enter_context(nc.Block())
        P = Prog()

        def E(eng, fn, *a, incs=(), waits=(), **k):
            P.emit(eng, fn, *a, incs=incs, waits=waits, **k)

        def DBG(nm, src_ap, wait_sems):
            if dbg and nm in dbg_d:
                E("gpsimd", "dma_start", out=dbg_d[nm][:, :], in_=src_ap,
                  waits=wait_sems, incs=[("gdma", 16)])
                P.wait("vector", "gdma", P.val("gdma"))

        bank_guard = [("V", 0)] * 8

        def use_bank(i):
            return [bank_guard[i]]

        def bank_done(i, sem="V"):
            bank_guard[i] = (sem, P.val(sem))

        E("gpsimd", "iota", io_a[:, :], pattern=[[1, 128]], base=0,
          channel_multiplier=-1, incs=[("G", 1)])
        E("gpsimd", "iota", io_fb[:, :], pattern=[[1, 8], [0, C]], base=8,
          channel_multiplier=-1, incs=[("G", 1)])
        E("gpsimd", "iota", io_d0[:, :], pattern=[[2, 4], [0, C]], base=0,
          channel_multiplier=-1, incs=[("G", 1)])
        E("gpsimd", "iota", io_dp[:, :], pattern=[[2, 4], [0, DKV]], base=8,
          channel_multiplier=-1, incs=[("G", 1)])
        E("vector", "tensor_scalar", trimask[:, :], io_a[:, :], 0.0, None,
          ALU.is_ge, waits=[("G", 4)], incs=[("V", 1)])
        E("vector", "tensor_scalar", ident[:, :], io_a[:, :], 0.0, None,
          ALU.is_equal, incs=[("V", 1)])
        E("vector", "tensor_scalar", sel_fb[:, :], io_fb[:, :], 0.0, None,
          ALU.is_equal, incs=[("V", 1)])
        E("vector", "tensor_scalar", sel_d0[:, :], io_d0[:, :], 0.0, None,
          ALU.is_equal, incs=[("V", 1)])
        E("vector", "tensor_scalar", sel_dp[:, :], io_dp[:, :], 0.0, None,
          ALU.is_equal, incs=[("V", 1)])
        for cst, val in ((ones1, 1.0), (ones1f, 1.0), (onesk1, 1.0),
                         (onesk1f, 1.0),
                         (onesrowf, 1.0), (ones8, 1.0), (zer_b, 0.0)):
            E("vector", "memset", cst[:, :], val, incs=[("V", 1)])
        E("vector", "memset", epst[:, :], EPS_Q, incs=[("V", 1)])
        E("vector", "memset", maskC[:, :], 1.0)
        E("vector", "memset", maskC[:, 0:TT:C], 0.0, incs=[("V", 1)])
        E("vector", "memset", s_f[:, :], 0.0, incs=[("V", 1)])
        E("vector", "memset", s_bf[:, :], 0.0, incs=[("V", 1)])

        E("sync", "dma_start", out=gwsb[:, :],
          in_=gw_d.rearrange("(k p) -> p k", p=128),
          allow_slow_non_contiguous=True, incs=[("dma", 16)])

        def prepass_cast(src_ap, dst_sc, nk, outdim, fold_gw=False):
            ncg = outdim // D
            for k in range(nk):
                if fold_gw:
                    E("vector", "tensor_copy", gwbc[:, 0:1], gwsb[:, k:k + 1],
                      waits=[("dma", 16)], incs=[("V", 1)])
                    E("tensor", "transpose", pbh[6][0:1, 0:128], gwbc[:, 0:1],
                      ident[:, :],
                      waits=[("V", P.val("V"))] + use_bank(6),
                      incs=[("T", 1)])
                    E("vector", "tensor_copy", gwrow[:, :], pbh[6][0:1, 0:128],
                      waits=[("T", P.val("T"))], incs=[("V", 1)])
                    bank_done(6)
                    E("tensor", "matmul", pb[7][:, 0:128], gwrow[:, :],
                      onesk1[:, :], start=True, stop=True,
                      waits=[("V", P.val("V"))] + use_bank(7), incs=[("T", 1)])
                    E("vector", "tensor_copy", gwbc[:, :], pb[7][:, 0:128],
                      waits=[("T", P.val("T"))], incs=[("V", 1)])
                    bank_done(7)
                    E("vector", "tensor_tensor", gwdiag[:, :], ident[:, :],
                      gwbc[:, :], ALU.mult, incs=[("V", 1)])
                for cg in range(ncg):
                    buf = (k * ncg + cg) % 2
                    cf = cast_f[:, buf * D:(buf + 1) * D]
                    cb = cast_b[:, buf * D:(buf + 1) * D]
                    E("sync", "dma_start", out=cf,
                      in_=src_ap[k * 128:(k + 1) * 128, cg * D:(cg + 1) * D],
                      waits=[("gdma", max(0, P.val("gdma") - 16))],
                      incs=[("dma", 16)])
                    E("vector", "tensor_copy", cb, cf,
                      waits=[("dma", P.val("dma")),
                             ("gdma", max(0, P.val("gdma") - 16))],
                      incs=[("V", 1)])
                    if fold_gw:
                        for q4 in range(4):
                            bnk = q4 % 2
                            E("tensor", "matmul", pb[bnk][:, :], gwdiag[:, :],
                              cb[:, q4 * TT:(q4 + 1) * TT], start=True, stop=True,
                              waits=[("V", P.val("V"))] + use_bank(bnk),
                              incs=[("T", 1)])
                            E("vector", "tensor_copy", cb[:, q4 * TT:(q4 + 1) * TT],
                              pb[bnk][:, :], waits=[("T", P.val("T"))],
                              incs=[("V", 1)])
                            bank_done(bnk)
                    E("gpsimd", "dma_start",
                      out=dst_sc[k, :, cg * D:(cg + 1) * D], in_=cb,
                      waits=[("V", P.val("V"))], incs=[("gdma", 16)])

        prepass_cast(wq_d, wsc["q"], NKD, D)
        prepass_cast(wk_d, wsc["k"], NKD, D)
        prepass_cast(wi_d, wsc["i"], NKD, D)
        prepass_cast(wo_d, wsc["o"], NKD, D, fold_gw=True)

        def ffn_quant(src_ap, dst_sc, nk, outdim, dbg_t=None, dbg_row=None):
            CW = 1408 if outdim % D else D
            ncg = outdim // CW
            n_el = float(nk * 128 * outdim)
            nq = (CW + TT - 1) // TT
            first = True
            Sbase, Tbase = P.val("S"), P.val("T")
            for k in range(nk):
                for cg in range(ncg):
                    n_it = k * ncg + cg
                    buf = n_it % 2
                    cf = cast_f[:, buf * D:buf * D + CW]
                    cb = cast_b[:, buf * D:buf * D + CW]
                    E("sync", "dma_start", out=cf,
                      in_=src_ap[k * 128:(k + 1) * 128, cg * CW:(cg + 1) * CW],
                      waits=[("S", max(Sbase, P.val("S") - 1)),
                             ("T", P.val("T")),
                             ("gdma", P.val("gdma"))], incs=[("dma", 16)])
                    E("scalar", "activation", cb, cf, AF.Abs,
                      waits=[("dma", P.val("dma")),
                             ("T", max(Tbase, P.val("T") - 1)),
                             ("gdma", P.val("gdma"))], incs=[("S", 1)])
                    for q4 in range(nq):
                        w = min(TT, CW - q4 * TT)
                        E("tensor", "matmul", pb[5][:, 0:w], ident[:, :],
                          cb[:, q4 * TT:q4 * TT + w], start=first,
                          stop=(k == nk - 1 and cg == ncg - 1 and q4 == nq - 1),
                          waits=([("S", P.val("S"))] +
                                 (use_bank(5) if first else [])),
                          incs=([("T", 1)] if q4 == nq - 1 else []))
                        first = False
            pcol4 = sgn[:, 4 * TT:4 * TT + 4]
            for j in range(4):
                E("vector", "tensor_reduce", pcol4[:, j:j + 1],
                  pb[5][:, j * 128:(j + 1) * 128],
                  mybir.AxisListType.X, ALU.add,
                  waits=[("T", P.val("T"))] if j == 0 else [],
                  incs=[("V", 1)])
            bank_done(5)
            E("sync", "dma_start", out=gsc_d[:, :], in_=pcol4,
              waits=[("V", P.val("V"))], incs=[("dma", 16)])
            E("sync", "dma_start", out=onesrowf[0:1, 0:TT],
              in_=gsc_d.rearrange("(o p) j -> o (p j)", o=1),
              waits=[("dma", P.val("dma"))], incs=[("dma", 16)])
            E("vector", "tensor_reduce", tsc[:, 0:1], onesrowf[0:1, 0:TT],
              mybir.AxisListType.X, ALU.add, waits=[("dma", P.val("dma"))],
              incs=[("V", 1)])
            E("scalar", "activation", tsc[:, 1:2], tsc[:, 0:1], AF.Copy,
              scale=1.0 / n_el, waits=[("V", P.val("V"))], incs=[("S", 1)])
            E("vector", "tensor_tensor", tsc[:, 2:3], tsc[:, 1:2],
              epst[:, :], ALU.add, waits=[("S", P.val("S"))], incs=[("V", 1)])
            E("scalar", "activation", tsc[:, 2:3], tsc[:, 2:3], AF.Copy,
              scale=0.5, waits=[("V", P.val("V"))], incs=[("S", 1)])
            if dbg_t:
                DBG(dbg_t, tsc[:, :], [("V", P.val("V"))])
            E("vector", "memset", onesrowf[:, :], 1.0, incs=[("V", 1)])
            E("tensor", "matmul", pb[5][0:1, :], tsc[:, 2:3], onesrowf[:, :],
              start=True, stop=True,
              waits=[("V", P.val("V")), ("S", P.val("S"))] + use_bank(5),
              incs=[("T", 1)])
            E("vector", "tensor_copy", trow[:, :], pb[5][0:1, :],
              waits=[("T", P.val("T"))], incs=[("V", 1)])
            bank_done(5)
            E("tensor", "matmul", pb[5][:, :], onesk1f[:, :], trow[:, :],
              start=True, stop=True,
              waits=[("V", P.val("V"))] + use_bank(5), incs=[("T", 1)])
            E("vector", "tensor_copy", tb_f[:, :], pb[5][:, :],
              waits=[("T", P.val("T"))], incs=[("V", 1)])
            bank_done(5)
            E("vector", "tensor_scalar", tnb_f[:, :], tb_f[:, :], -1.0, None,
              ALU.mult, incs=[("V", 1)])
            for k in range(nk):
                for cg in range(ncg):
                    buf = (k * ncg + cg) % 2
                    cf = cast_f[:, buf * D:buf * D + CW]
                    cb = cast_b[:, buf * D:buf * D + CW]
                    E("sync", "dma_start", out=cf,
                      in_=src_ap[k * 128:(k + 1) * 128, cg * CW:(cg + 1) * CW],
                      waits=[("gdma", max(0, P.val("gdma") - 16)),
                             ("S", P.val("S"))],
                      incs=[("dma", 16)])
                    for q4 in range(nq):
                        w = min(TT, CW - q4 * TT)
                        sl = cf[:, q4 * TT:q4 * TT + w]
                        E("vector", "tensor_tensor", qrollA[:, 0:w], sl,
                          tb_f[:, 0:w], ALU.is_gt,
                          waits=[("dma", P.val("dma"))] if q4 == 0 else [])
                        E("vector", "tensor_tensor", qrollB[:, 0:w], sl,
                          tnb_f[:, 0:w], ALU.is_lt)
                        E("vector", "tensor_tensor", cb[:, q4 * TT:q4 * TT + w],
                          qrollA[:, 0:w], qrollB[:, 0:w], ALU.subtract,
                          waits=([("gdma", max(0, P.val("gdma") - 16))]
                                 if q4 == 0 else []),
                          incs=[("V", 1)])
                    E("gpsimd", "dma_start",
                      out=dst_sc[k, :, cg * CW:(cg + 1) * CW], in_=cb,
                      waits=[("V", P.val("V"))], incs=[("gdma", 16)])

        ffn_quant(w1_d, wsc["1"], NKD, HID, dbg_t="tsc1", dbg_row="grow1")
        DBG("w1q", wsc["1"][0, :, :], [("gdma", P.val("gdma"))])
        if dbg:
            E("gpsimd", "dma_start", out=dbg_d["w1full"].rearrange(
                "(k p) h -> k p h", p=128), in_=wsc["1"][:, :, :],
              waits=[("gdma", P.val("gdma"))], incs=[("gdma", 16)])
        ffn_quant(w2_d, wsc["2"], NKD, HID, dbg_t="tsc2")
        ffn_quant(w3_d, wsc["3"], NKH, D, dbg_t="tsc3")
        DBG("w3q", wsc["3"][0, :, :], [("gdma", P.val("gdma"))])

        def emit_ln():
            xtb = slab[:, 0:D]
            for g in range(NGRP):
                E("vector", "tensor_copy", xtb, xt[:, g * D:(g + 1) * D],
                  waits=[("dma", P.val("dma")), ("T", P.val("T")),
                         ("V", P.val("V"))], incs=[("V", 1)])
                castV = P.val("V")
                for kq in range(NKD // 4):
                    bnk = (g * (NKD // 4) + kq) % 2
                    ws = use_bank(bnk) + [("V", castV)]
                    for j in range(4):
                        k = kq * 4 + j
                        E("tensor", "transpose",
                          pbh[bnk][:, j * 128:(j + 1) * 128],
                          xtb[:, k * 128:(k + 1) * 128], ident[:, :],
                          waits=ws if j == 0 else [],
                          incs=[("T", 1)] if j == 3 else [])
                    dst = xf[:].rearrange("p (k t) -> p k t", k=NKD)[
                        :, kq * 4:kq * 4 + 4, g * 128:(g + 1) * 128]
                    E("vector", "tensor_copy", dst,
                      pbh[bnk][:, 0:512].rearrange("p (j c) -> p j c", j=4),
                      waits=[("T", P.val("T"))], incs=[("V", 1)])
                    bank_done(bnk)
            vS = P.val("S")
            for k in range(NKD):
                xfk = xf[:, k * TT:(k + 1) * TT]
                E("scalar", "activation",
                  sq_roll[:, (k % 2) * TT:(k % 2 + 1) * TT], xfk, AF.Square,
                  waits=[("V", P.val("V"))] if k == 0 else [], incs=[("S", 1)])
                E("tensor", "matmul", pb[2][0:1, :], ones1[:, :], xfk,
                  start=(k == 0), stop=(k == NKD - 1),
                  waits=(use_bank(2) + [("V", P.val("V"))]) if k == 0 else [],
                  incs=[("T", 1)] if k == NKD - 1 else [])
                E("tensor", "matmul", pb[3][0:1, :], ones1[:, :],
                  sq_roll[:, (k % 2) * TT:(k % 2 + 1) * TT],
                  start=(k == 0), stop=(k == NKD - 1),
                  waits=(use_bank(3) if k == 0 else []) + [("S", P.val("S"))],
                  incs=[("T", 1)] if k == NKD - 1 else [])
            tT = P.val("T")
            E("vector", "tensor_scalar", rowf[:, 0:TT], pb[2][0:1, :], 1.0 / D,
              None, ALU.mult, waits=[("T", tT)], incs=[("V", 1)])
            bank_done(2)
            E("vector", "tensor_tensor", rowf[:, TT:2 * TT], rowf[:, 0:TT],
              rowf[:, 0:TT], ALU.mult)
            E("vector", "scalar_tensor_tensor", rowf[:, TT:2 * TT],
              pb[3][0:1, :], 1.0 / D, rowf[:, TT:2 * TT], ALU.mult,
              ALU.subtract, incs=[("V", 1)])
            bank_done(3)
            E("vector", "tensor_scalar", rowf[:, TT:2 * TT],
              rowf[:, TT:2 * TT], EPS_LN, None, ALU.add, incs=[("V", 1)])
            E("scalar", "activation", rowf[:, TT:2 * TT],
              rowf[:, TT:2 * TT], AF.Ln, waits=[("V", P.val("V"))])
            E("scalar", "activation", rowb[:, TT:2 * TT],
              rowf[:, TT:2 * TT], AF.Exp, scale=-0.5, incs=[("S", 1)])
            E("vector", "tensor_copy", rowb[:, 0:TT], rowf[:, 0:TT],
              incs=[("V", 1)])
            E("tensor", "matmul", pb[4][:, :], onesk1[:, :], rowb[:, 0:TT],
              start=True, stop=True,
              waits=use_bank(4) + [("V", P.val("V"))], incs=[("T", 1)])
            E("tensor", "matmul", pb[5][:, :], onesk1[:, :], rowb[:, TT:2 * TT],
              start=True, stop=True,
              waits=use_bank(5) + [("S", P.val("S"))], incs=[("T", 1)])
            tT = P.val("T")
            for k in range(NKD):
                E("vector", "tensor_tensor", xl[:, k * TT:(k + 1) * TT],
                  xf[:, k * TT:(k + 1) * TT], pb[4][:, :], ALU.subtract,
                  waits=[("T", tT)] if k == 0 else [])
                E("vector", "tensor_tensor", xl[:, k * TT:(k + 1) * TT],
                  xl[:, k * TT:(k + 1) * TT], pb[5][:, :], ALU.mult,
                  incs=[("V", 1)])
            bank_done(4)
            bank_done(5)

        def emit_proj_feat(w_key, m_lo, m_hi, evac_cb):
            for mg in range(m_lo // 4, m_hi // 4):
                for half in range(2):
                    ks, ke = half * 8, half * 8 + 8
                    E("sync", "dma_start",
                      out=slab[:, ks * TT:ke * TT].rearrange(
                          "p (k c) -> p k c", c=TT),
                      in_=wsc[w_key][ks:ke, :, mg * TT:(mg + 1) * TT].transpose(
                          [1, 0, 2]),
                      waits=[("T", P.val("T"))], incs=[("dma", 16)])
                dmav = P.val("dma")
                for m in range(mg * 4, mg * 4 + 4):
                    bnk = m % 2
                    ws = use_bank(bnk) + [("dma", dmav), ("V", P.val("V"))]
                    mo = (m - mg * 4) * 128
                    for k in range(NKD):
                        E("tensor", "matmul", pb[bnk][:, :],
                          slab[:, k * TT + mo:k * TT + mo + 128],
                          xl[:, k * TT:(k + 1) * TT],
                          start=(k == 0), stop=(k == NKD - 1),
                          waits=ws if k == 0 else [],
                          incs=[("T", 1)] if k == NKD - 1 else [])
                    evac_cb(m, bnk)

        cons = {"g": [0, 0], "epem": [0, 0], "kh": [("V", 0), ("V", 0)],
                "qh": ("V", 0), "dpb": [("V", 0), ("V", 0)],
                "vtok": [("V", 0), ("V", 0)], "ktok": [("V", 0), ("V", 0)],
                "a": [("V", 0), ("V", 0)], "el": [0, 0], "pt": [0, 0],
                "lp": [0, 0]}
        obank_rot = [1, 7]

        def emit_scan_head(h, hslot, qreadyS, kreadyS, vreadyV):
            r = h % 2
            ht = h * TT
            E("scalar", "activation", g_roll[:, r * TT:(r + 1) * TT],
              sgn[:, hslot * TT:(hslot + 1) * TT], AF.Ln,
              waits=[("S", kreadyS), ("V", cons["g"][r])], incs=[("S", 1)])
            E("vector", "tensor_tensor_scan", lp_roll[:, r * TT:(r + 1) * TT],
              maskC[:, :], g_roll[:, r * TT:(r + 1) * TT], 0.0, ALU.mult,
              ALU.add, waits=[("S", P.val("S"))], incs=[("V", 1)])
            cons["g"][r] = P.val("V")
            scanV = P.val("V")
            lpr = lp_roll[:, r * TT:(r + 1) * TT]
            E("scalar", "activation", ep_roll[:, r * TT:(r + 1) * TT], lpr,
              AF.Exp, waits=[("V", scanV), ("V", cons["el"][r]),
                             ("V", cons["epem"][r])], incs=[("S", 1)])
            E("scalar", "activation", em_roll[:, r * TT:(r + 1) * TT], lpr,
              AF.Exp, scale=-1.0, incs=[("S", 1)])
            E("scalar", "activation", el16[:, r * 16:r * 16 + 8],
              lp_roll[:, r * TT + C - 1:(r + 1) * TT:C], AF.Exp,
              incs=[("S", 1)])
            sExps = P.val("S")
            er = el16[:, r * 16:(r + 1) * 16]
            E("vector", "tensor_tensor", er[:, 8:16], er[:, 0:8], ones8[:, :],
              ALU.mult, waits=[("S", sExps)])
            E("vector", "tensor_tensor", er[:, 8:16:2], er[:, 0:8:2],
              er[:, 1:8:2], ALU.mult, incs=[("V", 1)])
            elV = P.val("V")
            cons["el"][r] = elV
            E("vector", "scalar_tensor_tensor", qt[:, ht:ht + TT],
              qt[:, ht:ht + TT], SCALE, ep_roll[:, r * TT:(r + 1) * TT],
              ALU.mult, ALU.mult, waits=[("S", qreadyS)], incs=[("V", 1)])
            E("vector", "tensor_tensor", sig[:, ht:ht + TT], sig[:, ht:ht + TT],
              em_roll[:, r * TT:(r + 1) * TT], ALU.mult, incs=[("V", 1)])
            qktV = P.val("V")
            cons["epem"][r] = qktV
            E("tensor", "transpose", pbh[0][0:16, 0:128], er, ident[:, :],
              waits=[("V", elV)] + use_bank(0), incs=[("T", 1)])
            E("vector", "tensor_copy", pt16[:, r * 128:(r + 1) * 128],
              pbh[0][0:16, 0:128], waits=[("T", P.val("T"))], incs=[("V", 1)])
            ptr = pt16[:, r * 128:(r + 1) * 128]
            E("tensor", "matmul", pb[0][:, :], ptr, sel_fb[:, :], start=True,
              stop=True, waits=[("V", P.val("V"))], incs=[("T", 1)])
            E("vector", "tensor_tensor", kh_roll[:, r * TT:(r + 1) * TT],
              sig[:, ht:ht + TT], pb[0][:, :], ALU.mult,
              waits=[("T", P.val("T")), cons["kh"][r]], incs=[("V", 1)])
            khV = P.val("V")
            E("tensor", "matmul", pb[0][:, 0:TT // 2], ptr, sel_d0[:, :],
              start=True, stop=True, waits=[("V", khV)], incs=[("T", 1)])
            qtv = qt[:, ht:ht + TT].rearrange("p (j two c) -> p j two c",
                                              two=2, c=C)
            E("vector", "tensor_tensor",
              qh_roll[:, 0:TT // 2].rearrange("p (j c) -> p j c", c=C),
              qtv[:, :, 1, :],
              pb[0][:, 0:TT // 2].rearrange("p (j c) -> p j c", c=C), ALU.mult,
              waits=[("T", P.val("T")), cons["qh"]], incs=[("V", 1)])
            qhV = P.val("V")
            E("tensor", "matmul", pb[0][:, :], ptr, sel_dp[:, :], start=True,
              stop=True, waits=[("V", qhV)], incs=[("T", 1)])
            E("vector", "tensor_copy", dpb_roll[:, r * TT:(r + 1) * TT],
              pb[0][:, :], waits=[("T", P.val("T")), cons["dpb"][r]],
              incs=[("V", 1)])
            dpbV = P.val("V")
            for j in range(NPAIR):
                E("tensor", "transpose", pbh[0][:, j * 128:(j + 1) * 128],
                  vft[:, ht + j * 128:ht + (j + 1) * 128], ident[:, :],
                  waits=([("V", dpbV), ("V", vreadyV)]) if j == 0 else [],
                  incs=[("T", 1)] if j == NPAIR - 1 else [])
            E("vector", "tensor_copy", vtok_roll[:, r * TT:(r + 1) * TT],
              pbh[0][:, 0:TT], waits=[("T", P.val("T")), cons["vtok"][r]],
              incs=[("V", 1)])
            bank_done(0)
            vtokV = P.val("V")
            sfh = s_f[:, h * DKV:(h + 1) * DKV]
            sbh = s_bf[:, h * DKV:(h + 1) * DKV]
            chainV = P.val("V")
            for j in range(NPAIR):
                ab = 2 + j % 2
                kb = 4 + j % 2
                c0o = ht + 2 * j * C
                c1o = ht + (2 * j + 1) * C
                q0 = qt[:, c0o:c0o + C]
                q1h = qh_roll[:, j * C:(j + 1) * C]
                k0 = sig[:, c0o:c0o + C]
                k1 = sig[:, c1o:c1o + C]
                vp = vtok_roll[:, r * TT + j * DKV:r * TT + (j + 1) * DKV]
                ws = use_bank(ab) + [("V", qktV), ("V", qhV), ("V", vtokV)]
                E("tensor", "matmul", pb[ab][0:64, 0:C], k0, q0, start=True,
                  stop=False, tile_position=(0, 0), waits=ws)
                E("tensor", "matmul", pb[ab][0:64, C:2 * C], k0, q1h,
                  start=False, stop=True, tile_position=(0, 0))
                E("tensor", "matmul", pb[ab][64:128, C:2 * C], k1,
                  qt[:, c1o:c1o + C], start=True, stop=False,
                  tile_position=(0, 64))
                E("tensor", "matmul", pb[ab][64:128, 0:C], zer_b[:, 0:C], q0,
                  start=False, stop=True, tile_position=(0, 64),
                  incs=[("T", 1)])
                aT = P.val("T")
                E("tensor", "transpose", pbh[kb][:, 0:128],
                  kh_roll[:, r * TT + j * 128:r * TT + (j + 1) * 128],
                  ident[:, :], waits=use_bank(kb) + [("V", khV)],
                  incs=[("T", 1)])
                ktT = P.val("T")
                E("vector", "tensor_tensor", a_roll[:, (j % 2) * 128:
                  (j % 2 + 1) * 128], pb[ab][:, 0:128], trimask[:, :],
                  ALU.mult, waits=[("T", aT), cons["a"][j % 2]],
                  incs=[("V", 1)])
                bank_done(ab)
                aV = P.val("V")
                E("vector", "tensor_copy", ktok_roll[:, (j % 2) * 128:
                  (j % 2 + 1) * 128], pbh[kb][:, 0:128],
                  waits=[("T", ktT), cons["ktok"][j % 2]], incs=[("V", 1)])
                bank_done(kb)
                ktokV = P.val("V")
                ob = obank_rot[0]
                wso = (use_bank(ob) + [("V", chainV)]) if j == 0 else \
                    [("V", chainV)]
                E("tensor", "matmul", pb[ob][:, 2 * j * C:(2 * j + 1) * C],
                  sbh, q0, start=(j == 0), stop=False, waits=wso)
                E("tensor", "matmul", pb[ob][:, (2 * j + 1) * C:(2 * j + 2) * C],
                  sbh, q1h, start=False, stop=False)
                E("tensor", "matmul", pb[ob][:, 2 * j * C:(2 * j + 2) * C], vp,
                  a_roll[:, (j % 2) * 128:(j % 2 + 1) * 128], start=False,
                  stop=True, waits=[("V", aV)],
                  incs=[("T", 1)] if j == NPAIR - 1 else [])
                E("tensor", "matmul", pb[6][:, 0:128],
                  ktok_roll[:, (j % 2) * 128:(j % 2 + 1) * 128], vp,
                  start=True, stop=True,
                  waits=[("V", ktokV)] + use_bank(6), incs=[("T", 1)])
                uT = P.val("T")
                E("vector", "tensor_tensor", sfh, sfh,
                  dpb_roll[:, r * TT + j * DKV:r * TT + (j + 1) * DKV],
                  ALU.mult, waits=[("T", uT)])
                E("vector", "tensor_add", sfh, sfh, pb[6][:, 0:128],
                  incs=[("V", 1)])
                bank_done(6)
                E("vector", "tensor_copy", sbh, sfh, incs=[("V", 1)])
                chainV = P.val("V")
                cons["a"][j % 2] = ("T", P.val("T"))
                cons["ktok"][j % 2] = ("T", P.val("T"))
            oT = P.val("T")
            E("vector", "tensor_copy", o_s[:, ht:ht + TT],
              pb[obank_rot[0]][:, :], waits=[("T", oT)], incs=[("V", 1)])
            bank_done(obank_rot[0])
            obank_rot.reverse()
            cons["kh"][r] = ("T", P.val("T"))
            cons["qh"] = ("T", P.val("T"))
            cons["dpb"][r] = ("V", P.val("V"))
            cons["vtok"][r] = ("T", P.val("T"))

        def emit_tokout(w_key, nk, lhs_of_k, evac_cb, slab_ks=8):
            for mg in range(4):
                nhalf = nk // slab_ks
                for half in range(nhalf):
                    ks, ke = half * slab_ks, (half + 1) * slab_ks
                    E("sync", "dma_start",
                      out=slab[:, ks * TT:ke * TT].rearrange(
                          "p (k c) -> p k c", c=TT),
                      in_=wsc[w_key][ks:ke, :, mg * TT:(mg + 1) * TT].transpose(
                          [1, 0, 2]),
                      waits=[("T", P.val("T"))], incs=[("dma", 16)])
                dmav = P.val("dma")
                for g in range(NGRP):
                    bnk = 4 + g % 2
                    ws = use_bank(bnk) + [("dma", dmav), ("V", P.val("V"))]
                    for k in range(nk):
                        E("tensor", "matmul", pb[bnk][:, :], lhs_of_k(k, g),
                          slab[:, k * TT:(k + 1) * TT],
                          start=(k == 0), stop=(k == nk - 1),
                          waits=ws if k == 0 else [],
                          incs=[("T", 1)] if k == nk - 1 else [])
                    evac_cb(mg, g, bnk)

        for i in range(n_tiles):
            E("sync", "dma_start",
              out=xt[:].rearrange("p (g d) -> p g d", g=NGRP),
              in_=x_d[i * TT:(i + 1) * TT, :].rearrange("(g p) d -> p g d",
                                                        p=128),
              waits=[("odma", P.val("odma")), ("V", P.val("V"))],
              incs=[("dma", 16)])
            emit_ln()
            if i == 0:
                DBG("xf", xf[:, :], [("V", P.val("V"))])
                DBG("xl", xl[:, :], [("V", P.val("V"))])

            def evac_q(m, bnk):
                E("scalar", "activation", qt[:, m * TT:(m + 1) * TT],
                  pb[bnk][:, :], AF.Relu, waits=[("T", P.val("T"))],
                  incs=[("S", 1)])
                bank_guard[bnk] = ("S", P.val("S"))

            emit_proj_feat("q", 0, NH, evac_q)
            qreadyS = P.val("S")
            if i == 0:
                DBG("qt0", qt, [("S", qreadyS)])

            def evac_v(m, bnk):
                E("vector", "tensor_copy", vft[:, m * TT:(m + 1) * TT],
                  pb[bnk][:, :], waits=[("T", P.val("T"))], incs=[("V", 1)])
                bank_done(bnk)

            emit_proj_feat("i", 0, NH, evac_v)
            vreadyV = P.val("V")
            if i == 0:
                DBG("vft0", vft, [("V", vreadyV)])

            for hg in range(2):
                h_lo, h_hi = hg * 8, hg * 8 + 8

                def evac_k(m, bnk, _h_lo=h_lo):
                    E("scalar", "activation", sig[:, m * TT:(m + 1) * TT],
                      pb[bnk][:, :], AF.Sigmoid, waits=[("T", P.val("T"))],
                      incs=[("S", 1)])
                    E("scalar", "activation",
                      sgn[:, (m - _h_lo) * TT:(m - _h_lo + 1) * TT],
                      pb[bnk][:, :], AF.Sigmoid, scale=-1.0, incs=[("S", 1)])
                    bank_guard[bnk] = ("S", P.val("S"))

                emit_proj_feat("k", h_lo, h_hi, evac_k)
                kreadyS = P.val("S")
                if i == 0 and hg == 0:
                    DBG("sig0", sig, [("S", kreadyS)])
                    DBG("sgn0", sgn[:, :], [("S", kreadyS)])
                for h in range(h_lo, h_hi):
                    emit_scan_head(h, h - h_lo, qreadyS, kreadyS, vreadyV)

            if i == 0:
                DBG("qt1", qt, [("V", P.val("V"))])
                DBG("sig1", sig, [("V", P.val("V"))])
                DBG("os0", o_s[:, :], [("V", P.val("V"))])
                DBG("sf0", s_f[:, :], [("V", P.val("V"))])
            sS0 = P.val("S")
            for h in range(NH):
                E("scalar", "activation",
                  sq_roll[:, (h % 2) * TT:(h % 2 + 1) * TT],
                  o_s[:, h * TT:(h + 1) * TT], AF.Square,
                  waits=[("V", P.val("V"))] if h == 0 else [], incs=[("S", 1)])
                E("tensor", "matmul", pb[2][0:1, :], ones1[:, :],
                  sq_roll[:, (h % 2) * TT:(h % 2 + 1) * TT],
                  start=(h == 0), stop=(h == NH - 1),
                  waits=(use_bank(2) if h == 0 else []) + [("S", P.val("S"))],
                  incs=[("T", 1)] if h == NH - 1 else [])
            E("vector", "tensor_scalar", rowf[:, 0:TT], pb[2][0:1, :], 1.0 / D,
              EPS_RMS, ALU.mult, ALU.add, waits=[("T", P.val("T"))],
              incs=[("V", 1)])
            bank_done(2)
            E("scalar", "activation", rowf[:, TT:2 * TT], rowf[:, 0:TT],
              AF.Ln, waits=[("V", P.val("V"))])
            E("scalar", "activation", rowb[:, 0:TT], rowf[:, TT:2 * TT],
              AF.Exp, scale=-0.5, incs=[("S", 1)])
            E("tensor", "matmul", pb[3][:, :], onesk1[:, :], rowb[:, 0:TT],
              start=True, stop=True,
              waits=use_bank(3) + [("S", P.val("S"))], incs=[("T", 1)])
            tT = P.val("T")
            for h in range(NH):
                E("vector", "tensor_tensor", o_s[:, h * TT:(h + 1) * TT],
                  o_s[:, h * TT:(h + 1) * TT], pb[3][:, :], ALU.mult,
                  waits=[("T", tT)] if h == 0 else [], incs=[("V", 1)])
            bank_done(3)

            if i == 0:
                DBG("orms", o_s[:, :], [("V", P.val("V"))])
            def evac_wo(mg, g, bnk):
                E("vector", "tensor_tensor",
                  xt[:, g * D + mg * TT:g * D + (mg + 1) * TT],
                  xt[:, g * D + mg * TT:g * D + (mg + 1) * TT],
                  pb[bnk][:, :], ALU.add, waits=[("T", P.val("T"))],
                  incs=[("V", 1)])
                bank_done(bnk)

            emit_tokout("o", NH,
                        lambda k, g: o_s[:, k * TT + g * 128:k * TT +
                                         (g + 1) * 128],
                        evac_wo)

            if i == 0:
                DBG("y0", xt[:, :], [("V", P.val("V"))])
            emit_ln()

            for cg in range(NKH // 4):
                E("sync", "dma_start",
                  out=slab[:, 0:8 * TT].rearrange("p (k c) -> p k c", c=TT),
                  in_=wsc["1"][0:8, :, cg * TT:(cg + 1) * TT].transpose(
                      [1, 0, 2]),
                  waits=[("T", P.val("T"))], incs=[("dma", 16)])
                E("sync", "dma_start",
                  out=slab[:, 8 * TT:16 * TT].rearrange("p (k c) -> p k c",
                                                        c=TT),
                  in_=wsc["1"][8:16, :, cg * TT:(cg + 1) * TT].transpose(
                      [1, 0, 2]),
                  incs=[("dma", 16)])
                dmav1 = P.val("dma")
                for m in range(4):
                    bnk = m % 4
                    ws = use_bank(bnk) + [("dma", dmav1), ("V", P.val("V"))]
                    mo = m * 128
                    for k in range(NKD):
                        E("tensor", "matmul", pb[bnk][:, :],
                          slab[:, k * TT + mo:k * TT + mo + 128],
                          xl[:, k * TT:(k + 1) * TT],
                          start=(k == 0), stop=(k == NKD - 1),
                          waits=ws if k == 0 else [],
                          incs=[("T", 1)] if k == NKD - 1 else [])
                x1T = P.val("T")
                E("sync", "dma_start",
                  out=slab[:, 0:8 * TT].rearrange("p (k c) -> p k c", c=TT),
                  in_=wsc["2"][0:8, :, cg * TT:(cg + 1) * TT].transpose(
                      [1, 0, 2]),
                  waits=[("T", x1T)], incs=[("dma", 16)])
                E("sync", "dma_start",
                  out=slab[:, 8 * TT:16 * TT].rearrange("p (k c) -> p k c",
                                                        c=TT),
                  in_=wsc["2"][8:16, :, cg * TT:(cg + 1) * TT].transpose(
                      [1, 0, 2]),
                  incs=[("dma", 16)])
                dmav2 = P.val("dma")
                for m in range(4):
                    bnk = 4 + m % 4
                    ws = use_bank(bnk) + [("dma", dmav2), ("V", P.val("V"))]
                    mo = m * 128
                    for k in range(NKD):
                        E("tensor", "matmul", pb[bnk][:, :],
                          slab[:, k * TT + mo:k * TT + mo + 128],
                          xl[:, k * TT:(k + 1) * TT],
                          start=(k == 0), stop=(k == NKD - 1),
                          waits=ws if k == 0 else [],
                          incs=[("T", 1)] if k == NKD - 1 else [])
                x2T = P.val("T")
                for m in range(4):
                    hk = cg * 4 + m
                    E("scalar", "activation",
                      sq_roll[:, (m % 2) * TT:(m % 2 + 1) * TT],
                      pb[m][:, :], AF.Sigmoid,
                      waits=[("T", x2T), ("V", P.val("V"))] if m == 0 else [],
                      incs=[("S", 1)])
                    E("vector", "tensor_tensor", hid_b[:, hk * TT:(hk + 1) * TT],
                      sq_roll[:, (m % 2) * TT:(m % 2 + 1) * TT],
                      pb[4 + m][:, :], ALU.mult,
                      waits=[("S", P.val("S"))], incs=[("V", 1)])
                    bank_done(m, "S")
                    bank_done(4 + m)
            hidV = P.val("V")
            if i == 0:
                DBG("hid0", hid_b, [("V", hidV)])

            for mg in range(4):
                for kq in range(4):
                    ks, ke = kq * 11, kq * 11 + 11
                    E("sync", "dma_start",
                      out=slab[:, 0:11 * TT].rearrange("p (k c) -> p k c",
                                                       c=TT),
                      in_=wsc["3"][ks:ke, :, mg * TT:(mg + 1) * TT].transpose(
                          [1, 0, 2]),
                      waits=[("T", P.val("T"))], incs=[("dma", 16)])
                    dmav = P.val("dma")
                    for g in range(NGRP):
                        bnk = 4 + g
                        ws = ((use_bank(bnk) if kq == 0 else []) +
                              [("dma", dmav), ("V", hidV)])
                        for k in range(11):
                            E("tensor", "matmul", pb[bnk][:, :],
                              hid_b[:, (kq * 11 + k) * TT + g * 128:
                                    (kq * 11 + k) * TT + (g + 1) * 128],
                              slab[:, k * TT:(k + 1) * TT],
                              start=(kq == 0 and k == 0),
                              stop=(kq == 3 and k == 10),
                              waits=ws if k == 0 else [],
                              incs=([("T", 1)] if k == 10 else []))
                for g in range(NGRP):
                    bnk = 4 + g
                    E("vector", "tensor_tensor",
                      xt[:, g * D + mg * TT:g * D + (mg + 1) * TT],
                      xt[:, g * D + mg * TT:g * D + (mg + 1) * TT],
                      pb[bnk][:, :], ALU.add,
                      waits=[("T", P.val("T"))] if g == 0 else [],
                      incs=[("V", 1)])
                    bank_done(bnk)
            E("sync", "dma_start",
              out=out_d[i * TT:(i + 1) * TT, :].rearrange("(g p) d -> p g d",
                                                          p=128),
              in_=xt[:].rearrange("p (g d) -> p g d", g=NGRP),
              waits=[("V", P.val("V"))], incs=[("odma", 16)])

        P.wait("sync", "odma", P.val("odma"))
        P.wait("gpsimd", "gdma", P.val("gdma"))

        @block.sync
        def _(sync):
            P.replay("sync", sync, sems)

        @block.gpsimd
        def _(gpsimd):
            P.replay("gpsimd", gpsimd, sems)

        @block.scalar
        def _(scalar):
            P.replay("scalar", scalar, sems)

        @block.vector
        def _(vector):
            P.replay("vector", vector, sems)

        @block.tensor
        def _(tensor):
            P.replay("tensor", tensor, sems)

    return nc, P


def host_inputs(inputs, core, T_core):
    x = inputs["L0_attn_hidden_states"][core]
    return {
        "x": np.ascontiguousarray(x[:T_core]),
        "wq": np.ascontiguousarray(inputs["L0_attn_Wq"]),
        "wk": np.ascontiguousarray(inputs["L0_attn_Wk"]),
        "wi": np.ascontiguousarray(inputs["L0_attn_Wi"]),
        "wo": np.ascontiguousarray(inputs["L0_attn_Wo"]),
        "gwt": np.ascontiguousarray(inputs["L0_attn_g_weight"]),
        "w1t": np.ascontiguousarray(inputs["L0_ffn_w1"].T),
        "w2t": np.ascontiguousarray(inputs["L0_ffn_w2"].T),
        "w3t": np.ascontiguousarray(inputs["L0_ffn_w3"].T),
    }



def kernel(**inputs):
    T_core = 8192
    nc, _ = build(T_core)
    in_maps = [host_inputs(inputs, c, T_core) for c in range(4)]
    res = run_bass_kernel_spmd(nc, in_maps, core_ids=[0, 1, 2, 3])
    out = np.stack([res.results[c]["out"] for c in range(4)], axis=0)
    return out.astype(np.float32)


# version 03
# speedup vs baseline: 1.2101x; 1.2101x over previous
import numpy as np
import ml_dtypes
import concourse.bass as bass
import concourse.mybir as mybir
from concourse.bass_utils import run_bass_kernel_spmd
from contextlib import ExitStack

F32 = mybir.dt.float32
BF16 = mybir.dt.bfloat16
I32 = mybir.dt.int32
AF = mybir.ActivationFunctionType
ALU = mybir.AluOpType

D = 2048
HID = 5632
NH = 16
DKV = 128
SCALE = DKV ** -0.5
EPS_LN = 1e-5
EPS_RMS = 1e-5
EPS_Q = 1e-8
C = 64
NKD = D // 128
NKH = HID // 128
TT = 512
NGRP = TT // 128
NPAIR = TT // 128


class Prog:
    ENGINES = ("sync", "gpsimd", "scalar", "vector", "tensor")

    def __init__(self):
        self.ops = {e: [] for e in self.ENGINES}
        self.cnt = {}

    def val(self, sem):
        return self.cnt.get(sem, 0)

    def emit(self, engine, fn, *args, incs=(), waits=(), **kwargs):
        for s, v in waits:
            if v > 0:
                assert v <= self.val(s), \
                    f"wait {s}>={v} but only {self.val(s)} emitted ({engine}.{fn})"
                self.ops[engine].append(("wait", s, v))
        self.ops[engine].append(("op", fn, args, kwargs, list(incs)))
        for s, n in incs:
            self.cnt[s] = self.cnt.get(s, 0) + n

    def wait(self, engine, sem, value):
        self.ops[engine].append(("wait", sem, value))

    def replay(self, engine_name, eng, sems):
        last_wait = {}
        for item in self.ops[engine_name]:
            if item[0] == "wait":
                _, s, v = item
                if last_wait.get(s, -1) < v:
                    eng.wait_ge(sems[s], v)
                    last_wait[s] = v
            else:
                _, fn, args, kwargs, incs = item
                try:
                    r = getattr(eng, fn)(*args, **kwargs)
                except Exception:
                    shapes = [getattr(a, 'shape', a) for a in args]
                    print(f"REPLAY FAIL {engine_name}.{fn} args={shapes} kw={list(kwargs)}")
                    raise
                for s, n in incs:
                    r.then_inc(sems[s], n)

    def n_ops(self):
        return {e: len(v) for e, v in self.ops.items()}


def build(T_core, dbg=False):
    n_tiles = T_core // TT
    assert T_core % TT == 0
    nc = bass.Bass("TRN2", target_bir_lowering=False, debug=False,
                   detect_race_conditions=False)

    x_d = nc.dram_tensor("x", [T_core, D], F32, kind="ExternalInput").ap()
    wq_d = nc.dram_tensor("wq", [D, D], F32, kind="ExternalInput").ap()
    wk_d = nc.dram_tensor("wk", [D, D], F32, kind="ExternalInput").ap()
    wi_d = nc.dram_tensor("wi", [D, D], F32, kind="ExternalInput").ap()
    wo_d = nc.dram_tensor("wo", [D, D], F32, kind="ExternalInput").ap()
    gw_d = nc.dram_tensor("gwt", [D], F32, kind="ExternalInput").ap()
    w1_d = nc.dram_tensor("w1t", [D, HID], F32, kind="ExternalInput").ap()
    w2_d = nc.dram_tensor("w2t", [D, HID], F32, kind="ExternalInput").ap()
    w3_d = nc.dram_tensor("w3t", [HID, D], F32, kind="ExternalInput").ap()
    out_d = nc.dram_tensor("out", [T_core, D], F32, kind="ExternalOutput").ap()
    dbg_d = {}
    if dbg:
        for nm, shp, dt_ in (("xf", [128, NKD * TT], BF16), ("xl", [128, NKD * TT], BF16),
                             ("qt0", [128, NH * TT], BF16), ("sig0", [128, NH * TT], BF16),
                             ("vft0", [128, NH * TT], BF16), ("sgn0", [128, 8 * TT], F32),
                             ("qt1", [128, NH * TT], BF16), ("sig1", [128, NH * TT], BF16),
                             ("os0", [128, NH * TT], BF16), ("sf0", [128, NH * DKV], F32),
                             ("orms", [128, NH * TT], BF16), ("y0", [128, NGRP * D], F32),
                             ("hid0", [128, NKH * TT], BF16),
                             ("w1q", [128, HID], BF16), ("w3q", [128, D], BF16),
                             ("w1full", [NKD * 128, HID], BF16),
                             ("tsc1", [1, 4], F32), ("tsc2", [1, 4], F32),
                             ("tsc3", [1, 4], F32), ("grow1", [1, TT], F32)):
            dbg_d[nm] = nc.dram_tensor("dbg_" + nm, shp, dt_, kind="ExternalOutput").ap()
    wsc = {
        "q": nc.dram_tensor("wq_s", [NKD, 128, D], BF16, kind="Internal").ap(),
        "k": nc.dram_tensor("wk_s", [NKD, 128, D], BF16, kind="Internal").ap(),
        "i": nc.dram_tensor("wi_s", [NKD, 128, D], BF16, kind="Internal").ap(),
        "o": nc.dram_tensor("wo_s", [NKD, 128, D], BF16, kind="Internal").ap(),
        "1": nc.dram_tensor("w1_s", [NKD, 128, HID], BF16, kind="Internal").ap(),
        "2": nc.dram_tensor("w2_s", [NKD, 128, HID], BF16, kind="Internal").ap(),
        "3": nc.dram_tensor("w3_s", [NKH, 128, D], BF16, kind="Internal").ap(),
    }
    gsc_d = nc.dram_tensor("gsc", [128, 4], F32, kind="Internal").ap()

    with ExitStack() as ctx:
        def sb(name, shape, dtype):
            return ctx.enter_context(nc.sbuf_tensor(name, shape, dtype)).ap()

        xt = sb("xt", [128, NGRP * D], F32)
        bigbuf = sb("bigbuf", [128, 3 * NH * TT], BF16)
        qt = bigbuf[:, 0:NH * TT]
        sig = bigbuf[:, NH * TT:2 * NH * TT]
        vft = bigbuf[:, 2 * NH * TT:3 * NH * TT]
        hid_b = bigbuf[:, 0:NKH * TT]
        xf = sb("xf", [128, NKD * TT], BF16)
        xl = sb("xl", [128, NKD * TT], BF16)
        sgn = sb("sgn", [128, 8 * TT], F32)
        o_s = sb("o_s", [128, NH * TT], BF16)
        slab = sb("slab", [128, NKD * TT], BF16)
        s_f = sb("s_f", [128, NH * DKV], F32)
        s_bf = sb("s_bf", [128, NH * DKV], BF16)
        g_roll = sb("g_roll", [128, 2 * TT], F32)
        lp_roll = sb("lp_roll", [128, 2 * TT], F32)
        ep_roll = sb("ep_roll", [128, 2 * TT], BF16)
        em_roll = sb("em_roll", [128, 2 * TT], BF16)
        kh_roll = sb("kh_roll", [128, 2 * TT], BF16)
        qh_roll = sb("qh_roll", [128, TT], BF16)
        dpb_roll = sb("dpb_roll", [128, 2 * NPAIR * DKV], BF16)
        vtok_roll = sb("vtok_roll", [128, 2 * NPAIR * DKV], BF16)
        ktok_roll = sb("ktok_roll", [128, 2 * DKV], BF16)
        a_roll = sb("a_roll", [128, 2 * 128], BF16)
        el16 = sb("el16", [128, 2 * 16], BF16)
        pt16 = sb("pt16", [16, 2 * 128], BF16)
        sq_roll = sb("sq_roll", [128, 2 * TT], BF16)
        rowf = sb("rowf", [1, 2 * TT], F32)
        rowb = sb("rowb", [1, 2 * TT], BF16)
        tb_f = lp_roll[:, 0:TT]
        tnb_f = sgn[:, 0:TT]
        cast_f = xt[:, 0:2 * D]
        cast_b = xf[:, 0:2 * D]
        qrollA = xl[:, 0:TT]
        qrollB = xl[:, TT:2 * TT]
        gwsb = sb("gwsb", [128, NKD], F32)
        gwrow = sb("gwrow", [1, 128], BF16)
        gwbc = sb("gwbc", [128, 128], BF16)
        gwdiag = sb("gwdiag", [128, 128], BF16)
        ones1 = sb("ones1", [128, 1], BF16)
        ones1f = sb("ones1f", [128, 1], F32)
        onesk1 = sb("onesk1", [1, 128], BF16)
        onesk1f = sb("onesk1f", [1, 128], F32)
        onesrowf = rowf[:, 0:TT]
        ones8 = sb("ones8", [128, 8], BF16)
        zer_b = sb("zer_b", [128, C], BF16)
        trimask = sb("trimask", [128, 128], BF16)
        ident = sb("ident", [128, 128], BF16)
        maskC = sb("maskC", [128, TT], F32)
        sel_fb = sb("sel_fb", [16, TT], BF16)
        sel_d0 = sb("sel_d0", [16, TT // 2], BF16)
        sel_dp = sb("sel_dp", [16, NPAIR * DKV], BF16)
        io_a = o_s[:, 0:256].bitcast(I32)
        io_fb = sgn[0:16, 0:TT].bitcast(I32)
        io_d0 = sgn[0:16, TT:TT + TT // 2].bitcast(I32)
        io_dp = sgn[0:16, 2 * TT:2 * TT + NPAIR * DKV].bitcast(I32)
        tsc = sb("tsc", [1, 4], F32)
        epst = sb("epst", [1, 1], F32)
        trow = rowf[:, TT:2 * TT]

        pb = [ctx.enter_context(nc.psum_tensor("pb%d" % i, [128, TT], F32)).ap()
              for i in range(8)]
        pbh = [p.bitcast(BF16) for p in pb]

        sems = {}
        for s in ("dma", "gdma", "odma", "S", "V", "T", "G"):
            sems[s] = ctx.enter_context(nc.semaphore(s))

        block = ctx.enter_context(nc.Block())
        P = Prog()

        def E(eng, fn, *a, incs=(), waits=(), **k):
            P.emit(eng, fn, *a, incs=incs, waits=waits, **k)

        def DBG(nm, src_ap, wait_sems):
            if dbg and nm in dbg_d:
                E("gpsimd", "dma_start", out=dbg_d[nm][:, :], in_=src_ap,
                  waits=wait_sems, incs=[("gdma", 16)])
                P.wait("vector", "gdma", P.val("gdma"))

        bank_guard = [("V", 0)] * 8

        def use_bank(i):
            return [bank_guard[i]]

        def bank_done(i, sem="V"):
            bank_guard[i] = (sem, P.val(sem))

        E("gpsimd", "iota", io_a[:, :], pattern=[[1, 128]], base=0,
          channel_multiplier=-1, incs=[("G", 1)])
        E("gpsimd", "iota", io_fb[:, :], pattern=[[1, 8], [0, C]], base=8,
          channel_multiplier=-1, incs=[("G", 1)])
        E("gpsimd", "iota", io_d0[:, :], pattern=[[2, 4], [0, C]], base=0,
          channel_multiplier=-1, incs=[("G", 1)])
        E("gpsimd", "iota", io_dp[:, :], pattern=[[2, 4], [0, DKV]], base=8,
          channel_multiplier=-1, incs=[("G", 1)])
        E("vector", "tensor_scalar", trimask[:, :], io_a[:, :], 0.0, None,
          ALU.is_ge, waits=[("G", 4)], incs=[("V", 1)])
        E("vector", "tensor_scalar", ident[:, :], io_a[:, :], 0.0, None,
          ALU.is_equal, incs=[("V", 1)])
        E("vector", "tensor_scalar", sel_fb[:, :], io_fb[:, :], 0.0, None,
          ALU.is_equal, incs=[("V", 1)])
        E("vector", "tensor_scalar", sel_d0[:, :], io_d0[:, :], 0.0, None,
          ALU.is_equal, incs=[("V", 1)])
        E("vector", "tensor_scalar", sel_dp[:, :], io_dp[:, :], 0.0, None,
          ALU.is_equal, incs=[("V", 1)])
        for cst, val in ((ones1, 1.0), (ones1f, 1.0), (onesk1, 1.0),
                         (onesk1f, 1.0),
                         (onesrowf, 1.0), (ones8, 1.0), (zer_b, 0.0)):
            E("vector", "memset", cst[:, :], val, incs=[("V", 1)])
        E("vector", "memset", epst[:, :], EPS_Q, incs=[("V", 1)])
        E("vector", "memset", maskC[:, :], 1.0)
        E("vector", "memset", maskC[:, 0:TT:C], 0.0, incs=[("V", 1)])
        E("vector", "memset", s_f[:, :], 0.0, incs=[("V", 1)])
        E("vector", "memset", s_bf[:, :], 0.0, incs=[("V", 1)])

        E("sync", "dma_start", out=gwsb[:, :],
          in_=gw_d.rearrange("(k p) -> p k", p=128),
          allow_slow_non_contiguous=True, incs=[("dma", 16)])

        def prepass_cast(src_ap, dst_sc, nk, outdim, fold_gw=False):
            ncg = outdim // D
            for k in range(nk):
                if fold_gw:
                    E("vector", "tensor_copy", gwbc[:, 0:1], gwsb[:, k:k + 1],
                      waits=[("dma", 16)], incs=[("V", 1)])
                    E("tensor", "transpose", pbh[6][0:1, 0:128], gwbc[:, 0:1],
                      ident[:, :],
                      waits=[("V", P.val("V"))] + use_bank(6),
                      incs=[("T", 1)])
                    E("vector", "tensor_copy", gwrow[:, :], pbh[6][0:1, 0:128],
                      waits=[("T", P.val("T"))], incs=[("V", 1)])
                    bank_done(6)
                    E("tensor", "matmul", pb[7][:, 0:128], gwrow[:, :],
                      onesk1[:, :], start=True, stop=True,
                      waits=[("V", P.val("V"))] + use_bank(7), incs=[("T", 1)])
                    E("vector", "tensor_copy", gwbc[:, :], pb[7][:, 0:128],
                      waits=[("T", P.val("T"))], incs=[("V", 1)])
                    bank_done(7)
                    E("vector", "tensor_tensor", gwdiag[:, :], ident[:, :],
                      gwbc[:, :], ALU.mult, incs=[("V", 1)])
                for cg in range(ncg):
                    buf = (k * ncg + cg) % 2
                    cf = cast_f[:, buf * D:(buf + 1) * D]
                    cb = cast_b[:, buf * D:(buf + 1) * D]
                    E("sync", "dma_start", out=cf,
                      in_=src_ap[k * 128:(k + 1) * 128, cg * D:(cg + 1) * D],
                      waits=[("gdma", max(0, P.val("gdma") - 16))],
                      incs=[("dma", 16)])
                    E("vector", "tensor_copy", cb, cf,
                      waits=[("dma", P.val("dma")),
                             ("gdma", max(0, P.val("gdma") - 16))],
                      incs=[("V", 1)])
                    if fold_gw:
                        for q4 in range(4):
                            bnk = q4 % 2
                            E("tensor", "matmul", pb[bnk][:, :], gwdiag[:, :],
                              cb[:, q4 * TT:(q4 + 1) * TT], start=True, stop=True,
                              waits=[("V", P.val("V"))] + use_bank(bnk),
                              incs=[("T", 1)])
                            E("vector", "tensor_copy", cb[:, q4 * TT:(q4 + 1) * TT],
                              pb[bnk][:, :], waits=[("T", P.val("T"))],
                              incs=[("V", 1)])
                            bank_done(bnk)
                    E("gpsimd", "dma_start",
                      out=dst_sc[k, :, cg * D:(cg + 1) * D], in_=cb,
                      waits=[("V", P.val("V"))], incs=[("gdma", 16)])

        prepass_cast(wq_d, wsc["q"], NKD, D)
        prepass_cast(wk_d, wsc["k"], NKD, D)
        prepass_cast(wi_d, wsc["i"], NKD, D)
        prepass_cast(wo_d, wsc["o"], NKD, D, fold_gw=True)

        def ffn_quant(src_ap, dst_sc, nk, outdim, dbg_t=None, dbg_row=None):
            CW = 1408 if outdim % D else D
            ncg = outdim // CW
            n_el = float(nk * 128 * outdim)
            nq = (CW + TT - 1) // TT
            first = True
            Sbase, Tbase = P.val("S"), P.val("T")
            for k in range(nk):
                for cg in range(ncg):
                    n_it = k * ncg + cg
                    buf = n_it % 2
                    cf = cast_f[:, buf * D:buf * D + CW]
                    cb = cast_b[:, buf * D:buf * D + CW]
                    E("sync", "dma_start", out=cf,
                      in_=src_ap[k * 128:(k + 1) * 128, cg * CW:(cg + 1) * CW],
                      waits=[("S", max(Sbase, P.val("S") - 1)),
                             ("T", P.val("T")),
                             ("gdma", P.val("gdma"))], incs=[("dma", 16)])
                    E("scalar", "activation", cb, cf, AF.Abs,
                      waits=[("dma", P.val("dma")),
                             ("T", max(Tbase, P.val("T") - 1)),
                             ("gdma", P.val("gdma"))], incs=[("S", 1)])
                    for q4 in range(nq):
                        w = min(TT, CW - q4 * TT)
                        E("tensor", "matmul", pb[5][:, 0:w], ident[:, :],
                          cb[:, q4 * TT:q4 * TT + w], start=first,
                          stop=(k == nk - 1 and cg == ncg - 1 and q4 == nq - 1),
                          waits=([("S", P.val("S"))] +
                                 (use_bank(5) if first else [])),
                          incs=([("T", 1)] if q4 == nq - 1 else []))
                        first = False
            pcol4 = sgn[:, 4 * TT:4 * TT + 4]
            for j in range(4):
                E("vector", "tensor_reduce", pcol4[:, j:j + 1],
                  pb[5][:, j * 128:(j + 1) * 128],
                  mybir.AxisListType.X, ALU.add,
                  waits=[("T", P.val("T"))] if j == 0 else [],
                  incs=[("V", 1)])
            bank_done(5)
            E("sync", "dma_start", out=gsc_d[:, :], in_=pcol4,
              waits=[("V", P.val("V"))], incs=[("dma", 16)])
            E("sync", "dma_start", out=onesrowf[0:1, 0:TT],
              in_=gsc_d.rearrange("(o p) j -> o (p j)", o=1),
              waits=[("dma", P.val("dma"))], incs=[("dma", 16)])
            E("vector", "tensor_reduce", tsc[:, 0:1], onesrowf[0:1, 0:TT],
              mybir.AxisListType.X, ALU.add, waits=[("dma", P.val("dma"))],
              incs=[("V", 1)])
            E("scalar", "activation", tsc[:, 1:2], tsc[:, 0:1], AF.Copy,
              scale=1.0 / n_el, waits=[("V", P.val("V"))], incs=[("S", 1)])
            E("vector", "tensor_tensor", tsc[:, 2:3], tsc[:, 1:2],
              epst[:, :], ALU.add, waits=[("S", P.val("S"))], incs=[("V", 1)])
            E("scalar", "activation", tsc[:, 2:3], tsc[:, 2:3], AF.Copy,
              scale=0.5, waits=[("V", P.val("V"))], incs=[("S", 1)])
            if dbg_t:
                DBG(dbg_t, tsc[:, :], [("V", P.val("V"))])
            E("vector", "memset", onesrowf[:, :], 1.0, incs=[("V", 1)])
            E("tensor", "matmul", pb[5][0:1, :], tsc[:, 2:3], onesrowf[:, :],
              start=True, stop=True,
              waits=[("V", P.val("V")), ("S", P.val("S"))] + use_bank(5),
              incs=[("T", 1)])
            E("vector", "tensor_copy", trow[:, :], pb[5][0:1, :],
              waits=[("T", P.val("T"))], incs=[("V", 1)])
            bank_done(5)
            E("tensor", "matmul", pb[5][:, :], onesk1f[:, :], trow[:, :],
              start=True, stop=True,
              waits=[("V", P.val("V"))] + use_bank(5), incs=[("T", 1)])
            E("vector", "tensor_copy", tb_f[:, :], pb[5][:, :],
              waits=[("T", P.val("T"))], incs=[("V", 1)])
            bank_done(5)
            E("vector", "tensor_scalar", tnb_f[:, :], tb_f[:, :], -1.0, None,
              ALU.mult, incs=[("V", 1)])
            for k in range(nk):
                for cg in range(ncg):
                    buf = (k * ncg + cg) % 2
                    cf = cast_f[:, buf * D:buf * D + CW]
                    cb = cast_b[:, buf * D:buf * D + CW]
                    E("sync", "dma_start", out=cf,
                      in_=src_ap[k * 128:(k + 1) * 128, cg * CW:(cg + 1) * CW],
                      waits=[("gdma", max(0, P.val("gdma") - 16)),
                             ("S", P.val("S"))],
                      incs=[("dma", 16)])
                    for q4 in range(nq):
                        w = min(TT, CW - q4 * TT)
                        sl = cf[:, q4 * TT:q4 * TT + w]
                        E("vector", "tensor_tensor", qrollA[:, 0:w], sl,
                          tb_f[:, 0:w], ALU.is_gt,
                          waits=[("dma", P.val("dma"))] if q4 == 0 else [])
                        E("vector", "tensor_tensor", qrollB[:, 0:w], sl,
                          tnb_f[:, 0:w], ALU.is_lt)
                        E("vector", "tensor_tensor", cb[:, q4 * TT:q4 * TT + w],
                          qrollA[:, 0:w], qrollB[:, 0:w], ALU.subtract,
                          waits=([("gdma", max(0, P.val("gdma") - 16))]
                                 if q4 == 0 else []),
                          incs=[("V", 1)])
                    E("gpsimd", "dma_start",
                      out=dst_sc[k, :, cg * CW:(cg + 1) * CW], in_=cb,
                      waits=[("V", P.val("V"))], incs=[("gdma", 16)])

        ffn_quant(w1_d, wsc["1"], NKD, HID, dbg_t="tsc1", dbg_row="grow1")
        DBG("w1q", wsc["1"][0, :, :], [("gdma", P.val("gdma"))])
        if dbg:
            E("gpsimd", "dma_start", out=dbg_d["w1full"].rearrange(
                "(k p) h -> k p h", p=128), in_=wsc["1"][:, :, :],
              waits=[("gdma", P.val("gdma"))], incs=[("gdma", 16)])
        ffn_quant(w2_d, wsc["2"], NKD, HID, dbg_t="tsc2")
        ffn_quant(w3_d, wsc["3"], NKH, D, dbg_t="tsc3")
        DBG("w3q", wsc["3"][0, :, :], [("gdma", P.val("gdma"))])

        def emit_ln():
            xtb = slab[:, 0:D]
            for g in range(NGRP):
                E("vector", "tensor_copy", xtb, xt[:, g * D:(g + 1) * D],
                  waits=[("dma", P.val("dma")), ("T", P.val("T")),
                         ("V", P.val("V"))], incs=[("V", 1)])
                castV = P.val("V")
                for kq in range(NKD // 4):
                    bnk = (g * (NKD // 4) + kq) % 2
                    ws = use_bank(bnk) + [("V", castV)]
                    for j in range(4):
                        k = kq * 4 + j
                        E("tensor", "transpose",
                          pbh[bnk][:, j * 128:(j + 1) * 128],
                          xtb[:, k * 128:(k + 1) * 128], ident[:, :],
                          waits=ws if j == 0 else [],
                          incs=[("T", 1)] if j == 3 else [])
                    dst = xf[:].rearrange("p (k t) -> p k t", k=NKD)[
                        :, kq * 4:kq * 4 + 4, g * 128:(g + 1) * 128]
                    E("vector", "tensor_copy", dst,
                      pbh[bnk][:, 0:512].rearrange("p (j c) -> p j c", j=4),
                      waits=[("T", P.val("T"))], incs=[("V", 1)])
                    bank_done(bnk)
            vS = P.val("S")
            for k in range(NKD):
                xfk = xf[:, k * TT:(k + 1) * TT]
                E("scalar", "activation",
                  sq_roll[:, (k % 2) * TT:(k % 2 + 1) * TT], xfk, AF.Square,
                  waits=[("V", P.val("V"))] if k == 0 else [], incs=[("S", 1)])
                E("tensor", "matmul", pb[2][0:1, :], ones1[:, :], xfk,
                  start=(k == 0), stop=(k == NKD - 1),
                  waits=(use_bank(2) + [("V", P.val("V"))]) if k == 0 else [],
                  incs=[("T", 1)] if k == NKD - 1 else [])
                E("tensor", "matmul", pb[3][0:1, :], ones1[:, :],
                  sq_roll[:, (k % 2) * TT:(k % 2 + 1) * TT],
                  start=(k == 0), stop=(k == NKD - 1),
                  waits=(use_bank(3) if k == 0 else []) + [("S", P.val("S"))],
                  incs=[("T", 1)] if k == NKD - 1 else [])
            tT = P.val("T")
            E("vector", "tensor_scalar", rowf[:, 0:TT], pb[2][0:1, :], 1.0 / D,
              None, ALU.mult, waits=[("T", tT)], incs=[("V", 1)])
            bank_done(2)
            E("vector", "tensor_tensor", rowf[:, TT:2 * TT], rowf[:, 0:TT],
              rowf[:, 0:TT], ALU.mult)
            E("vector", "scalar_tensor_tensor", rowf[:, TT:2 * TT],
              pb[3][0:1, :], 1.0 / D, rowf[:, TT:2 * TT], ALU.mult,
              ALU.subtract, incs=[("V", 1)])
            bank_done(3)
            E("vector", "tensor_scalar", rowf[:, TT:2 * TT],
              rowf[:, TT:2 * TT], EPS_LN, None, ALU.add, incs=[("V", 1)])
            E("scalar", "activation", rowf[:, TT:2 * TT],
              rowf[:, TT:2 * TT], AF.Ln, waits=[("V", P.val("V"))])
            E("scalar", "activation", rowb[:, TT:2 * TT],
              rowf[:, TT:2 * TT], AF.Exp, scale=-0.5, incs=[("S", 1)])
            E("vector", "tensor_copy", rowb[:, 0:TT], rowf[:, 0:TT],
              incs=[("V", 1)])
            E("tensor", "matmul", pb[4][:, :], onesk1[:, :], rowb[:, 0:TT],
              start=True, stop=True,
              waits=use_bank(4) + [("V", P.val("V"))], incs=[("T", 1)])
            E("tensor", "matmul", pb[5][:, :], onesk1[:, :], rowb[:, TT:2 * TT],
              start=True, stop=True,
              waits=use_bank(5) + [("S", P.val("S"))], incs=[("T", 1)])
            tT = P.val("T")
            for k in range(NKD):
                E("vector", "tensor_tensor", xl[:, k * TT:(k + 1) * TT],
                  xf[:, k * TT:(k + 1) * TT], pb[4][:, :], ALU.subtract,
                  waits=[("T", tT)] if k == 0 else [])
                E("vector", "tensor_tensor", xl[:, k * TT:(k + 1) * TT],
                  xl[:, k * TT:(k + 1) * TT], pb[5][:, :], ALU.mult,
                  incs=[("V", 1)])
            bank_done(4)
            bank_done(5)

        def emit_proj_feat(w_key, m_lo, m_hi, evac_cb):
            first_mg = True
            for mg in range(m_lo // 4, m_hi // 4):
                dmavs = []
                for half in range(2):
                    ks, ke = half * 8, half * 8 + 8
                    tw = P.val("T") if (first_mg or half == 1) \
                        else P.val("T") - 4
                    E("sync", "dma_start",
                      out=slab[:, ks * TT:ke * TT].rearrange(
                          "p (k c) -> p k c", c=TT),
                      in_=wsc[w_key][ks:ke, :, mg * TT:(mg + 1) * TT].transpose(
                          [1, 0, 2]),
                      waits=[("T", tw)], incs=[("dma", 16)])
                    dmavs.append(P.val("dma"))
                first_mg = False
                for half in range(2):
                    for m in range(mg * 4, mg * 4 + 4):
                        bnk = m % 4
                        mo = (m - mg * 4) * 128
                        for kk in range(8):
                            k = half * 8 + kk
                            ws = []
                            if kk == 0:
                                ws = [("dma", dmavs[half])]
                                if half == 0:
                                    ws += use_bank(bnk) + [("V", P.val("V"))]
                            E("tensor", "matmul", pb[bnk][:, :],
                              slab[:, k * TT + mo:k * TT + mo + 128],
                              xl[:, k * TT:(k + 1) * TT],
                              start=(k == 0), stop=(k == NKD - 1),
                              waits=ws, incs=[("T", 1)] if kk == 7 else [])
                        if half == 1:
                            evac_cb(m, bnk)

        cons = {"g": [0, 0], "epem": [0, 0], "kh": [("V", 0), ("V", 0)],
                "qh": ("V", 0), "dpb": [("V", 0), ("V", 0)],
                "vtok": [("V", 0), ("V", 0)], "ktok": [("V", 0), ("V", 0)],
                "a": [("V", 0), ("V", 0)], "el": [0, 0], "pt": [0, 0],
                "lp": [0, 0]}
        obank_rot = [1, 7]

        def emit_scan_head(h, hslot, qreadyS, kreadyS, vreadyV):
            r = h % 2
            ht = h * TT
            E("scalar", "activation", g_roll[:, r * TT:(r + 1) * TT],
              sgn[:, hslot * TT:(hslot + 1) * TT], AF.Ln,
              waits=[("S", kreadyS), ("V", cons["g"][r])], incs=[("S", 1)])
            E("vector", "tensor_tensor_scan", lp_roll[:, r * TT:(r + 1) * TT],
              maskC[:, :], g_roll[:, r * TT:(r + 1) * TT], 0.0, ALU.mult,
              ALU.add, waits=[("S", P.val("S"))], incs=[("V", 1)])
            cons["g"][r] = P.val("V")
            scanV = P.val("V")
            lpr = lp_roll[:, r * TT:(r + 1) * TT]
            E("scalar", "activation", ep_roll[:, r * TT:(r + 1) * TT], lpr,
              AF.Exp, waits=[("V", scanV), ("V", cons["el"][r]),
                             ("V", cons["epem"][r])], incs=[("S", 1)])
            E("scalar", "activation", em_roll[:, r * TT:(r + 1) * TT], lpr,
              AF.Exp, scale=-1.0, incs=[("S", 1)])
            E("scalar", "activation", el16[:, r * 16:r * 16 + 8],
              lp_roll[:, r * TT + C - 1:(r + 1) * TT:C], AF.Exp,
              incs=[("S", 1)])
            sExps = P.val("S")
            er = el16[:, r * 16:(r + 1) * 16]
            E("vector", "tensor_tensor", er[:, 8:16], er[:, 0:8], ones8[:, :],
              ALU.mult, waits=[("S", sExps)])
            E("vector", "tensor_tensor", er[:, 8:16:2], er[:, 0:8:2],
              er[:, 1:8:2], ALU.mult, incs=[("V", 1)])
            elV = P.val("V")
            cons["el"][r] = elV
            E("vector", "scalar_tensor_tensor", qt[:, ht:ht + TT],
              qt[:, ht:ht + TT], SCALE, ep_roll[:, r * TT:(r + 1) * TT],
              ALU.mult, ALU.mult, waits=[("S", qreadyS)], incs=[("V", 1)])
            E("vector", "tensor_tensor", sig[:, ht:ht + TT], sig[:, ht:ht + TT],
              em_roll[:, r * TT:(r + 1) * TT], ALU.mult, incs=[("V", 1)])
            qktV = P.val("V")
            cons["epem"][r] = qktV
            E("tensor", "transpose", pbh[0][0:16, 0:128], er, ident[:, :],
              waits=[("V", elV)] + use_bank(0), incs=[("T", 1)])
            E("vector", "tensor_copy", pt16[:, r * 128:(r + 1) * 128],
              pbh[0][0:16, 0:128], waits=[("T", P.val("T"))], incs=[("V", 1)])
            ptr = pt16[:, r * 128:(r + 1) * 128]
            E("tensor", "matmul", pb[0][:, :], ptr, sel_fb[:, :], start=True,
              stop=True, waits=[("V", P.val("V"))], incs=[("T", 1)])
            E("vector", "tensor_tensor", kh_roll[:, r * TT:(r + 1) * TT],
              sig[:, ht:ht + TT], pb[0][:, :], ALU.mult,
              waits=[("T", P.val("T")), cons["kh"][r]], incs=[("V", 1)])
            khV = P.val("V")
            E("tensor", "matmul", pb[0][:, 0:TT // 2], ptr, sel_d0[:, :],
              start=True, stop=True, waits=[("V", khV)], incs=[("T", 1)])
            qtv = qt[:, ht:ht + TT].rearrange("p (j two c) -> p j two c",
                                              two=2, c=C)
            E("vector", "tensor_tensor",
              qh_roll[:, 0:TT // 2].rearrange("p (j c) -> p j c", c=C),
              qtv[:, :, 1, :],
              pb[0][:, 0:TT // 2].rearrange("p (j c) -> p j c", c=C), ALU.mult,
              waits=[("T", P.val("T")), cons["qh"]], incs=[("V", 1)])
            qhV = P.val("V")
            E("tensor", "matmul", pb[0][:, :], ptr, sel_dp[:, :], start=True,
              stop=True, waits=[("V", qhV)], incs=[("T", 1)])
            E("vector", "tensor_copy", dpb_roll[:, r * TT:(r + 1) * TT],
              pb[0][:, :], waits=[("T", P.val("T")), cons["dpb"][r]],
              incs=[("V", 1)])
            dpbV = P.val("V")
            for j in range(NPAIR):
                E("tensor", "transpose", pbh[0][:, j * 128:(j + 1) * 128],
                  vft[:, ht + j * 128:ht + (j + 1) * 128], ident[:, :],
                  waits=([("V", dpbV), ("V", vreadyV)]) if j == 0 else [],
                  incs=[("T", 1)] if j == NPAIR - 1 else [])
            E("vector", "tensor_copy", vtok_roll[:, r * TT:(r + 1) * TT],
              pbh[0][:, 0:TT], waits=[("T", P.val("T")), cons["vtok"][r]],
              incs=[("V", 1)])
            bank_done(0)
            vtokV = P.val("V")
            sfh = s_f[:, h * DKV:(h + 1) * DKV]
            sbh = s_bf[:, h * DKV:(h + 1) * DKV]
            chainV = P.val("V")
            for j in range(NPAIR):
                ab = 2 + j % 2
                kb = 4 + j % 2
                c0o = ht + 2 * j * C
                c1o = ht + (2 * j + 1) * C
                q0 = qt[:, c0o:c0o + C]
                q1h = qh_roll[:, j * C:(j + 1) * C]
                k0 = sig[:, c0o:c0o + C]
                k1 = sig[:, c1o:c1o + C]
                vp = vtok_roll[:, r * TT + j * DKV:r * TT + (j + 1) * DKV]
                ws = use_bank(ab) + [("V", qktV), ("V", qhV), ("V", vtokV)]
                E("tensor", "matmul", pb[ab][0:64, 0:C], k0, q0, start=True,
                  stop=False, tile_position=(0, 0), waits=ws)
                E("tensor", "matmul", pb[ab][0:64, C:2 * C], k0, q1h,
                  start=False, stop=True, tile_position=(0, 0))
                E("tensor", "matmul", pb[ab][64:128, C:2 * C], k1,
                  qt[:, c1o:c1o + C], start=True, stop=False,
                  tile_position=(0, 64))
                E("tensor", "matmul", pb[ab][64:128, 0:C], zer_b[:, 0:C], q0,
                  start=False, stop=True, tile_position=(0, 64),
                  incs=[("T", 1)])
                aT = P.val("T")
                E("tensor", "transpose", pbh[kb][:, 0:128],
                  kh_roll[:, r * TT + j * 128:r * TT + (j + 1) * 128],
                  ident[:, :], waits=use_bank(kb) + [("V", khV)],
                  incs=[("T", 1)])
                ktT = P.val("T")
                E("vector", "tensor_tensor", a_roll[:, (j % 2) * 128:
                  (j % 2 + 1) * 128], pb[ab][:, 0:128], trimask[:, :],
                  ALU.mult, waits=[("T", aT), cons["a"][j % 2]],
                  incs=[("V", 1)])
                bank_done(ab)
                aV = P.val("V")
                E("vector", "tensor_copy", ktok_roll[:, (j % 2) * 128:
                  (j % 2 + 1) * 128], pbh[kb][:, 0:128],
                  waits=[("T", ktT), cons["ktok"][j % 2]], incs=[("V", 1)])
                bank_done(kb)
                ktokV = P.val("V")
                ob = obank_rot[0]
                wso = (use_bank(ob) + [("V", chainV)]) if j == 0 else \
                    [("V", chainV)]
                E("tensor", "matmul", pb[ob][:, 2 * j * C:(2 * j + 1) * C],
                  sbh, q0, start=(j == 0), stop=False, waits=wso)
                E("tensor", "matmul", pb[ob][:, (2 * j + 1) * C:(2 * j + 2) * C],
                  sbh, q1h, start=False, stop=False)
                E("tensor", "matmul", pb[ob][:, 2 * j * C:(2 * j + 2) * C], vp,
                  a_roll[:, (j % 2) * 128:(j % 2 + 1) * 128], start=False,
                  stop=True, waits=[("V", aV)],
                  incs=[("T", 1)] if j == NPAIR - 1 else [])
                E("tensor", "matmul", pb[6][:, 0:128],
                  ktok_roll[:, (j % 2) * 128:(j % 2 + 1) * 128], vp,
                  start=True, stop=True,
                  waits=[("V", ktokV)] + use_bank(6), incs=[("T", 1)])
                uT = P.val("T")
                E("vector", "tensor_tensor", sfh, sfh,
                  dpb_roll[:, r * TT + j * DKV:r * TT + (j + 1) * DKV],
                  ALU.mult, waits=[("T", uT)])
                E("vector", "tensor_add", sfh, sfh, pb[6][:, 0:128],
                  incs=[("V", 1)])
                bank_done(6)
                E("vector", "tensor_copy", sbh, sfh, incs=[("V", 1)])
                chainV = P.val("V")
                cons["a"][j % 2] = ("T", P.val("T"))
                cons["ktok"][j % 2] = ("T", P.val("T"))
            oT = P.val("T")
            E("vector", "tensor_copy", o_s[:, ht:ht + TT],
              pb[obank_rot[0]][:, :], waits=[("T", oT)], incs=[("V", 1)])
            bank_done(obank_rot[0])
            obank_rot.reverse()
            cons["kh"][r] = ("T", P.val("T"))
            cons["qh"] = ("T", P.val("T"))
            cons["dpb"][r] = ("V", P.val("V"))
            cons["vtok"][r] = ("T", P.val("T"))

        def emit_tokout(w_key, nk, lhs_of_k, evac_cb, slab_ks=8):
            nhalf = nk // slab_ks
            first_mg = True
            for mg in range(4):
                dmavs = []
                for half in range(nhalf):
                    ks, ke = half * slab_ks, (half + 1) * slab_ks
                    tw = P.val("T") if first_mg \
                        else P.val("T") - (nhalf - 1 - half) * 4
                    E("sync", "dma_start",
                      out=slab[:, ks * TT:ke * TT].rearrange(
                          "p (k c) -> p k c", c=TT),
                      in_=wsc[w_key][ks:ke, :, mg * TT:(mg + 1) * TT].transpose(
                          [1, 0, 2]),
                      waits=[("T", tw)], incs=[("dma", 16)])
                    dmavs.append(P.val("dma"))
                first_mg = False
                for half in range(nhalf):
                    for g in range(NGRP):
                        bnk = 4 + g
                        for kk in range(slab_ks):
                            k = half * slab_ks + kk
                            ws = []
                            if kk == 0:
                                ws = [("dma", dmavs[half])]
                                if half == 0:
                                    ws += use_bank(bnk) + [("V", P.val("V"))]
                            E("tensor", "matmul", pb[bnk][:, :],
                              lhs_of_k(k, g), slab[:, k * TT:(k + 1) * TT],
                              start=(k == 0), stop=(k == nk - 1),
                              waits=ws,
                              incs=[("T", 1)] if kk == slab_ks - 1 else [])
                        if half == nhalf - 1:
                            evac_cb(mg, g, bnk)

        for i in range(n_tiles):
            E("sync", "dma_start",
              out=xt[:].rearrange("p (g d) -> p g d", g=NGRP),
              in_=x_d[i * TT:(i + 1) * TT, :].rearrange("(g p) d -> p g d",
                                                        p=128),
              waits=[("odma", P.val("odma")), ("V", P.val("V"))],
              incs=[("dma", 16)])
            emit_ln()
            if i == 0:
                DBG("xf", xf[:, :], [("V", P.val("V"))])
                DBG("xl", xl[:, :], [("V", P.val("V"))])

            def evac_q(m, bnk):
                E("scalar", "activation", qt[:, m * TT:(m + 1) * TT],
                  pb[bnk][:, :], AF.Relu, waits=[("T", P.val("T"))],
                  incs=[("S", 1)])
                bank_guard[bnk] = ("S", P.val("S"))

            emit_proj_feat("q", 0, NH, evac_q)
            qreadyS = P.val("S")
            if i == 0:
                DBG("qt0", qt, [("S", qreadyS)])

            def evac_v(m, bnk):
                E("vector", "tensor_copy", vft[:, m * TT:(m + 1) * TT],
                  pb[bnk][:, :], waits=[("T", P.val("T"))], incs=[("V", 1)])
                bank_done(bnk)

            emit_proj_feat("i", 0, NH, evac_v)
            vreadyV = P.val("V")
            if i == 0:
                DBG("vft0", vft, [("V", vreadyV)])

            for hg in range(2):
                h_lo, h_hi = hg * 8, hg * 8 + 8

                def evac_k(m, bnk, _h_lo=h_lo):
                    E("scalar", "activation", sig[:, m * TT:(m + 1) * TT],
                      pb[bnk][:, :], AF.Sigmoid, waits=[("T", P.val("T"))],
                      incs=[("S", 1)])
                    E("scalar", "activation",
                      sgn[:, (m - _h_lo) * TT:(m - _h_lo + 1) * TT],
                      pb[bnk][:, :], AF.Sigmoid, scale=-1.0, incs=[("S", 1)])
                    bank_guard[bnk] = ("S", P.val("S"))

                emit_proj_feat("k", h_lo, h_hi, evac_k)
                kreadyS = P.val("S")
                if i == 0 and hg == 0:
                    DBG("sig0", sig, [("S", kreadyS)])
                    DBG("sgn0", sgn[:, :], [("S", kreadyS)])
                for h in range(h_lo, h_hi):
                    emit_scan_head(h, h - h_lo, qreadyS, kreadyS, vreadyV)

            if i == 0:
                DBG("qt1", qt, [("V", P.val("V"))])
                DBG("sig1", sig, [("V", P.val("V"))])
                DBG("os0", o_s[:, :], [("V", P.val("V"))])
                DBG("sf0", s_f[:, :], [("V", P.val("V"))])
            sS0 = P.val("S")
            for h in range(NH):
                E("scalar", "activation",
                  sq_roll[:, (h % 2) * TT:(h % 2 + 1) * TT],
                  o_s[:, h * TT:(h + 1) * TT], AF.Square,
                  waits=[("V", P.val("V"))] if h == 0 else [], incs=[("S", 1)])
                E("tensor", "matmul", pb[2][0:1, :], ones1[:, :],
                  sq_roll[:, (h % 2) * TT:(h % 2 + 1) * TT],
                  start=(h == 0), stop=(h == NH - 1),
                  waits=(use_bank(2) if h == 0 else []) + [("S", P.val("S"))],
                  incs=[("T", 1)] if h == NH - 1 else [])
            E("vector", "tensor_scalar", rowf[:, 0:TT], pb[2][0:1, :], 1.0 / D,
              EPS_RMS, ALU.mult, ALU.add, waits=[("T", P.val("T"))],
              incs=[("V", 1)])
            bank_done(2)
            E("scalar", "activation", rowf[:, TT:2 * TT], rowf[:, 0:TT],
              AF.Ln, waits=[("V", P.val("V"))])
            E("scalar", "activation", rowb[:, 0:TT], rowf[:, TT:2 * TT],
              AF.Exp, scale=-0.5, incs=[("S", 1)])
            E("tensor", "matmul", pb[3][:, :], onesk1[:, :], rowb[:, 0:TT],
              start=True, stop=True,
              waits=use_bank(3) + [("S", P.val("S"))], incs=[("T", 1)])
            tT = P.val("T")
            for h in range(NH):
                E("vector", "tensor_tensor", o_s[:, h * TT:(h + 1) * TT],
                  o_s[:, h * TT:(h + 1) * TT], pb[3][:, :], ALU.mult,
                  waits=[("T", tT)] if h == 0 else [], incs=[("V", 1)])
            bank_done(3)

            if i == 0:
                DBG("orms", o_s[:, :], [("V", P.val("V"))])
            def evac_wo(mg, g, bnk):
                E("vector", "tensor_tensor",
                  xt[:, g * D + mg * TT:g * D + (mg + 1) * TT],
                  xt[:, g * D + mg * TT:g * D + (mg + 1) * TT],
                  pb[bnk][:, :], ALU.add, waits=[("T", P.val("T"))],
                  incs=[("V", 1)])
                bank_done(bnk)

            emit_tokout("o", NH,
                        lambda k, g: o_s[:, k * TT + g * 128:k * TT +
                                         (g + 1) * 128],
                        evac_wo)

            if i == 0:
                DBG("y0", xt[:, :], [("V", P.val("V"))])
            emit_ln()

            def ffn_waves(w_key, bnk0, first_dma_full):
                dmavs = []
                for half in range(2):
                    ks, ke = half * 8, half * 8 + 8
                    tw = P.val("T") if (first_dma_full or half == 1) \
                        else P.val("T") - 4
                    E("sync", "dma_start",
                      out=slab[:, ks * TT:ke * TT].rearrange(
                          "p (k c) -> p k c", c=TT),
                      in_=wsc[w_key][ks:ke, :, cg * TT:(cg + 1) * TT].transpose(
                          [1, 0, 2]),
                      waits=[("T", tw)], incs=[("dma", 16)])
                    dmavs.append(P.val("dma"))
                for half in range(2):
                    for m in range(4):
                        bnk = bnk0 + m
                        mo = m * 128
                        for kk in range(8):
                            k = half * 8 + kk
                            ws = []
                            if kk == 0:
                                ws = [("dma", dmavs[half])]
                                if half == 0:
                                    ws += use_bank(bnk) + [("V", P.val("V"))]
                            E("tensor", "matmul", pb[bnk][:, :],
                              slab[:, k * TT + mo:k * TT + mo + 128],
                              xl[:, k * TT:(k + 1) * TT],
                              start=(k == 0), stop=(k == NKD - 1),
                              waits=ws, incs=[("T", 1)] if kk == 7 else [])

            for cg in range(NKH // 4):
                ffn_waves("1", 0, cg == 0)
                ffn_waves("2", 4, False)
                x2T = P.val("T")
                for m in range(4):
                    hk = cg * 4 + m
                    E("scalar", "activation",
                      sq_roll[:, (m % 2) * TT:(m % 2 + 1) * TT],
                      pb[m][:, :], AF.Sigmoid,
                      waits=[("T", x2T), ("V", P.val("V"))] if m == 0 else [],
                      incs=[("S", 1)])
                    E("vector", "tensor_tensor", hid_b[:, hk * TT:(hk + 1) * TT],
                      sq_roll[:, (m % 2) * TT:(m % 2 + 1) * TT],
                      pb[4 + m][:, :], ALU.mult,
                      waits=[("S", P.val("S"))], incs=[("V", 1)])
                    bank_done(m, "S")
                    bank_done(4 + m)
            hidV = P.val("V")
            if i == 0:
                DBG("hid0", hid_b, [("V", hidV)])

            for mg in range(4):
                for kq in range(4):
                    ks, ke = kq * 11, kq * 11 + 11
                    E("sync", "dma_start",
                      out=slab[:, 0:11 * TT].rearrange("p (k c) -> p k c",
                                                       c=TT),
                      in_=wsc["3"][ks:ke, :, mg * TT:(mg + 1) * TT].transpose(
                          [1, 0, 2]),
                      waits=[("T", P.val("T"))], incs=[("dma", 16)])
                    dmav = P.val("dma")
                    for g in range(NGRP):
                        bnk = 4 + g
                        ws = ((use_bank(bnk) if kq == 0 else []) +
                              [("dma", dmav), ("V", hidV)])
                        for k in range(11):
                            E("tensor", "matmul", pb[bnk][:, :],
                              hid_b[:, (kq * 11 + k) * TT + g * 128:
                                    (kq * 11 + k) * TT + (g + 1) * 128],
                              slab[:, k * TT:(k + 1) * TT],
                              start=(kq == 0 and k == 0),
                              stop=(kq == 3 and k == 10),
                              waits=ws if k == 0 else [],
                              incs=([("T", 1)] if k == 10 else []))
                for g in range(NGRP):
                    bnk = 4 + g
                    E("vector", "tensor_tensor",
                      xt[:, g * D + mg * TT:g * D + (mg + 1) * TT],
                      xt[:, g * D + mg * TT:g * D + (mg + 1) * TT],
                      pb[bnk][:, :], ALU.add,
                      waits=[("T", P.val("T"))] if g == 0 else [],
                      incs=[("V", 1)])
                    bank_done(bnk)
            E("sync", "dma_start",
              out=out_d[i * TT:(i + 1) * TT, :].rearrange("(g p) d -> p g d",
                                                          p=128),
              in_=xt[:].rearrange("p (g d) -> p g d", g=NGRP),
              waits=[("V", P.val("V"))], incs=[("odma", 16)])

        P.wait("sync", "odma", P.val("odma"))
        P.wait("gpsimd", "gdma", P.val("gdma"))

        @block.sync
        def _(sync):
            P.replay("sync", sync, sems)

        @block.gpsimd
        def _(gpsimd):
            P.replay("gpsimd", gpsimd, sems)

        @block.scalar
        def _(scalar):
            P.replay("scalar", scalar, sems)

        @block.vector
        def _(vector):
            P.replay("vector", vector, sems)

        @block.tensor
        def _(tensor):
            P.replay("tensor", tensor, sems)

    return nc, P


def host_inputs(inputs, core, T_core):
    x = inputs["L0_attn_hidden_states"][core]
    return {
        "x": np.ascontiguousarray(x[:T_core]),
        "wq": np.ascontiguousarray(inputs["L0_attn_Wq"]),
        "wk": np.ascontiguousarray(inputs["L0_attn_Wk"]),
        "wi": np.ascontiguousarray(inputs["L0_attn_Wi"]),
        "wo": np.ascontiguousarray(inputs["L0_attn_Wo"]),
        "gwt": np.ascontiguousarray(inputs["L0_attn_g_weight"]),
        "w1t": np.ascontiguousarray(inputs["L0_ffn_w1"].T),
        "w2t": np.ascontiguousarray(inputs["L0_ffn_w2"].T),
        "w3t": np.ascontiguousarray(inputs["L0_ffn_w3"].T),
    }



def kernel(**inputs):
    T_core = 8192
    nc, _ = build(T_core)
    in_maps = [host_inputs(inputs, c, T_core) for c in range(4)]
    res = run_bass_kernel_spmd(nc, in_maps, core_ids=[0, 1, 2, 3])
    out = np.stack([res.results[c]["out"] for c in range(4)], axis=0)
    return out.astype(np.float32)


# version 04
# speedup vs baseline: 1.3542x; 1.1191x over previous
import numpy as np
import ml_dtypes
import concourse.bass as bass
import concourse.mybir as mybir
from concourse.bass_utils import run_bass_kernel_spmd
from contextlib import ExitStack

F32 = mybir.dt.float32
BF16 = mybir.dt.bfloat16
I32 = mybir.dt.int32
AF = mybir.ActivationFunctionType
ALU = mybir.AluOpType

D = 2048
HID = 5632
NH = 16
DKV = 128
SCALE = DKV ** -0.5
EPS_LN = 1e-5
EPS_RMS = 1e-5
EPS_Q = 1e-8
C = 64
NKD = D // 128
NKH = HID // 128
TT = 512
NGRP = TT // 128
NPAIR = TT // 128


class Prog:
    ENGINES = ("sync", "gpsimd", "scalar", "vector", "tensor")

    def __init__(self):
        self.ops = {e: [] for e in self.ENGINES}
        self.cnt = {}

    def val(self, sem):
        return self.cnt.get(sem, 0)

    def emit(self, engine, fn, *args, incs=(), waits=(), **kwargs):
        for s, v in waits:
            if v > 0:
                assert v <= self.val(s), \
                    f"wait {s}>={v} but only {self.val(s)} emitted ({engine}.{fn})"
                self.ops[engine].append(("wait", s, v))
        self.ops[engine].append(("op", fn, args, kwargs, list(incs)))
        for s, n in incs:
            self.cnt[s] = self.cnt.get(s, 0) + n

    def wait(self, engine, sem, value):
        self.ops[engine].append(("wait", sem, value))

    def replay(self, engine_name, eng, sems):
        last_wait = {}
        for item in self.ops[engine_name]:
            if item[0] == "wait":
                _, s, v = item
                if last_wait.get(s, -1) < v:
                    eng.wait_ge(sems[s], v)
                    last_wait[s] = v
            else:
                _, fn, args, kwargs, incs = item
                try:
                    r = getattr(eng, fn)(*args, **kwargs)
                except Exception:
                    shapes = [getattr(a, 'shape', a) for a in args]
                    print(f"REPLAY FAIL {engine_name}.{fn} args={shapes} kw={list(kwargs)}")
                    raise
                for s, n in incs:
                    r.then_inc(sems[s], n)

    def n_ops(self):
        return {e: len(v) for e, v in self.ops.items()}


def build(T_core, dbg=False):
    n_tiles = T_core // TT
    assert T_core % TT == 0
    nc = bass.Bass("TRN2", target_bir_lowering=False, debug=False,
                   detect_race_conditions=False)

    x_d = nc.dram_tensor("x", [T_core, D], F32, kind="ExternalInput").ap()
    wq_d = nc.dram_tensor("wq", [D, D], F32, kind="ExternalInput").ap()
    wk_d = nc.dram_tensor("wk", [D, D], F32, kind="ExternalInput").ap()
    wi_d = nc.dram_tensor("wi", [D, D], F32, kind="ExternalInput").ap()
    wo_d = nc.dram_tensor("wo", [D, D], F32, kind="ExternalInput").ap()
    gw_d = nc.dram_tensor("gwt", [D], F32, kind="ExternalInput").ap()
    w1_d = nc.dram_tensor("w1t", [D, HID], F32, kind="ExternalInput").ap()
    w2_d = nc.dram_tensor("w2t", [D, HID], F32, kind="ExternalInput").ap()
    w3_d = nc.dram_tensor("w3t", [HID, D], F32, kind="ExternalInput").ap()
    out_d = nc.dram_tensor("out", [T_core, D], F32, kind="ExternalOutput").ap()
    dbg_d = {}
    if dbg:
        for nm, shp, dt_ in (("xf", [128, NKD * TT], BF16), ("xl", [128, NKD * TT], BF16),
                             ("qt0", [128, NH * TT], BF16), ("sig0", [128, NH * TT], BF16),
                             ("vft0", [128, NH * TT], BF16), ("sgn0", [128, 8 * TT], F32),
                             ("qt1", [128, NH * TT], BF16), ("sig1", [128, NH * TT], BF16),
                             ("os0", [128, NH * TT], BF16), ("sf0", [128, NH * DKV], F32),
                             ("orms", [128, NH * TT], BF16), ("y0", [128, NGRP * D], F32),
                             ("hid0", [128, NKH * TT], BF16),
                             ("w1q", [128, HID], BF16), ("w3q", [128, D], BF16),
                             ("w1full", [NKD * 128, HID], BF16),
                             ("tsc1", [1, 4], F32), ("tsc2", [1, 4], F32),
                             ("tsc3", [1, 4], F32), ("grow1", [1, TT], F32)):
            dbg_d[nm] = nc.dram_tensor("dbg_" + nm, shp, dt_, kind="ExternalOutput").ap()
    wsc = {
        "q": nc.dram_tensor("wq_s", [NKD, 128, D], BF16, kind="Internal").ap(),
        "k": nc.dram_tensor("wk_s", [NKD, 128, D], BF16, kind="Internal").ap(),
        "i": nc.dram_tensor("wi_s", [NKD, 128, D], BF16, kind="Internal").ap(),
        "o": nc.dram_tensor("wo_s", [NKD, 128, D], BF16, kind="Internal").ap(),
        "1": nc.dram_tensor("w1_s", [NKD, 128, HID], BF16, kind="Internal").ap(),
        "2": nc.dram_tensor("w2_s", [NKD, 128, HID], BF16, kind="Internal").ap(),
        "3": nc.dram_tensor("w3_s", [NKH, 128, D], BF16, kind="Internal").ap(),
    }
    gsc_d = nc.dram_tensor("gsc", [128, 4], F32, kind="Internal").ap()

    with ExitStack() as ctx:
        def sb(name, shape, dtype):
            return ctx.enter_context(nc.sbuf_tensor(name, shape, dtype)).ap()

        xt = sb("xt", [128, NGRP * D], F32)
        bigbuf = sb("bigbuf", [128, 3 * NH * TT], BF16)
        qt = bigbuf[:, 0:NH * TT]
        sig = bigbuf[:, NH * TT:2 * NH * TT]
        vft = bigbuf[:, 2 * NH * TT:3 * NH * TT]
        hid_b = bigbuf[:, 0:NKH * TT]
        xf = sb("xf", [128, NKD * TT], BF16)
        xl = sb("xl", [128, NKD * TT], BF16)
        sgn = sb("sgn", [128, 8 * TT], F32)
        o_s = sb("o_s", [128, NH * TT], BF16)
        slab = sb("slab", [128, NKD * TT], BF16)
        s_f = sb("s_f", [128, NH * DKV], F32)
        s_bf = sb("s_bf", [128, NH * DKV], BF16)
        g_roll = sb("g_roll", [128, 2 * TT], F32)
        lp_roll = sb("lp_roll", [128, 2 * TT], F32)
        ep_roll = sb("ep_roll", [128, 2 * TT], BF16)
        em_roll = sb("em_roll", [128, 2 * TT], BF16)
        kh_roll = sb("kh_roll", [128, 2 * TT], BF16)
        qh_roll = sb("qh_roll", [128, TT], BF16)
        dpb_roll = sb("dpb_roll", [128, 2 * NPAIR * DKV], BF16)
        vtok_roll = sb("vtok_roll", [128, 2 * NPAIR * DKV], BF16)
        ktok_roll = sb("ktok_roll", [128, 2 * DKV], BF16)
        a_roll = sb("a_roll", [128, 2 * 128], BF16)
        el16 = sb("el16", [128, 2 * 16], BF16)
        pt16 = sb("pt16", [16, 2 * 128], BF16)
        sq_roll = sb("sq_roll", [128, 2 * TT], BF16)
        rowf = sb("rowf", [1, 2 * TT], F32)
        rowb = sb("rowb", [1, 2 * TT], BF16)
        tb_f = lp_roll[:, 0:TT]
        tnb_f = sgn[:, 0:TT]
        cast_f = xt[:, 0:2 * D]
        cast_b = xf[:, 0:2 * D]
        qrollA = xl[:, 0:TT]
        qrollB = xl[:, TT:2 * TT]
        gwsb = sb("gwsb", [128, NKD], F32)
        gwrow = sb("gwrow", [1, 128], BF16)
        gwbc = sb("gwbc", [128, 128], BF16)
        gwdiag = sb("gwdiag", [128, 128], BF16)
        ones1 = sb("ones1", [128, 1], BF16)
        ones1f = sb("ones1f", [128, 1], F32)
        onesk1 = sb("onesk1", [1, 128], BF16)
        onesk1f = sb("onesk1f", [1, 128], F32)
        onesrowf = rowf[:, 0:TT]
        ones8 = sb("ones8", [128, 8], BF16)
        zer_b = sb("zer_b", [128, C], BF16)
        trimask = sb("trimask", [128, 128], BF16)
        ident = sb("ident", [128, 128], BF16)
        maskC = sb("maskC", [128, TT], F32)
        sel_fb = sb("sel_fb", [16, TT], BF16)
        sel_d0 = sb("sel_d0", [16, TT // 2], BF16)
        sel_dp = sb("sel_dp", [16, NPAIR * DKV], BF16)
        io_a = o_s[:, 0:256].bitcast(I32)
        io_fb = sgn[0:16, 0:TT].bitcast(I32)
        io_d0 = sgn[0:16, TT:TT + TT // 2].bitcast(I32)
        io_dp = sgn[0:16, 2 * TT:2 * TT + NPAIR * DKV].bitcast(I32)
        tsc = sb("tsc", [1, 4], F32)
        epst = sb("epst", [1, 1], F32)
        trow = rowf[:, TT:2 * TT]

        pb = [ctx.enter_context(nc.psum_tensor("pb%d" % i, [128, TT], F32)).ap()
              for i in range(8)]
        pbh = [p.bitcast(BF16) for p in pb]

        sems = {}
        for s in ("dma", "gdma", "odma", "S", "V", "T", "G"):
            sems[s] = ctx.enter_context(nc.semaphore(s))

        block = ctx.enter_context(nc.Block())
        P = Prog()

        def E(eng, fn, *a, incs=(), waits=(), **k):
            P.emit(eng, fn, *a, incs=incs, waits=waits, **k)

        def DBG(nm, src_ap, wait_sems):
            if dbg and nm in dbg_d:
                E("gpsimd", "dma_start", out=dbg_d[nm][:, :], in_=src_ap,
                  waits=wait_sems, incs=[("gdma", 16)])
                P.wait("vector", "gdma", P.val("gdma"))

        bank_guard = [("V", 0)] * 8

        def use_bank(i):
            return [bank_guard[i]]

        def bank_done(i, sem="V"):
            bank_guard[i] = (sem, P.val(sem))

        E("gpsimd", "iota", io_a[:, :], pattern=[[1, 128]], base=0,
          channel_multiplier=-1, incs=[("G", 1)])
        E("gpsimd", "iota", io_fb[:, :], pattern=[[1, 8], [0, C]], base=8,
          channel_multiplier=-1, incs=[("G", 1)])
        E("gpsimd", "iota", io_d0[:, :], pattern=[[2, 4], [0, C]], base=0,
          channel_multiplier=-1, incs=[("G", 1)])
        E("gpsimd", "iota", io_dp[:, :], pattern=[[2, 4], [0, DKV]], base=8,
          channel_multiplier=-1, incs=[("G", 1)])
        E("vector", "tensor_scalar", trimask[:, :], io_a[:, :], 0.0, None,
          ALU.is_ge, waits=[("G", 4)], incs=[("V", 1)])
        E("vector", "tensor_scalar", ident[:, :], io_a[:, :], 0.0, None,
          ALU.is_equal, incs=[("V", 1)])
        E("vector", "tensor_scalar", sel_fb[:, :], io_fb[:, :], 0.0, None,
          ALU.is_equal, incs=[("V", 1)])
        E("vector", "tensor_scalar", sel_d0[:, :], io_d0[:, :], 0.0, None,
          ALU.is_equal, incs=[("V", 1)])
        E("vector", "tensor_scalar", sel_dp[:, :], io_dp[:, :], 0.0, None,
          ALU.is_equal, incs=[("V", 1)])
        for cst, val in ((ones1, 1.0), (ones1f, 1.0), (onesk1, 1.0),
                         (onesk1f, 1.0),
                         (onesrowf, 1.0), (ones8, 1.0), (zer_b, 0.0)):
            E("vector", "memset", cst[:, :], val, incs=[("V", 1)])
        E("vector", "memset", epst[:, :], EPS_Q, incs=[("V", 1)])
        E("vector", "memset", maskC[:, :], 1.0)
        E("vector", "memset", maskC[:, 0:TT:C], 0.0, incs=[("V", 1)])
        E("vector", "memset", s_f[:, :], 0.0, incs=[("V", 1)])
        E("vector", "memset", s_bf[:, :], 0.0, incs=[("V", 1)])

        E("sync", "dma_start", out=gwsb[:, :],
          in_=gw_d.rearrange("(k p) -> p k", p=128),
          allow_slow_non_contiguous=True, incs=[("dma", 16)])

        def prepass_cast(src_ap, dst_sc, nk, outdim, fold_gw=False):
            ncg = outdim // D
            for k in range(nk):
                if fold_gw:
                    E("vector", "tensor_copy", gwbc[:, 0:1], gwsb[:, k:k + 1],
                      waits=[("dma", 16)], incs=[("V", 1)])
                    E("tensor", "transpose", pbh[6][0:1, 0:128], gwbc[:, 0:1],
                      ident[:, :],
                      waits=[("V", P.val("V"))] + use_bank(6),
                      incs=[("T", 1)])
                    E("vector", "tensor_copy", gwrow[:, :], pbh[6][0:1, 0:128],
                      waits=[("T", P.val("T"))], incs=[("V", 1)])
                    bank_done(6)
                    E("tensor", "matmul", pb[7][:, 0:128], gwrow[:, :],
                      onesk1[:, :], start=True, stop=True,
                      waits=[("V", P.val("V"))] + use_bank(7), incs=[("T", 1)])
                    E("vector", "tensor_copy", gwbc[:, :], pb[7][:, 0:128],
                      waits=[("T", P.val("T"))], incs=[("V", 1)])
                    bank_done(7)
                    E("vector", "tensor_tensor", gwdiag[:, :], ident[:, :],
                      gwbc[:, :], ALU.mult, incs=[("V", 1)])
                for cg in range(ncg):
                    buf = (k * ncg + cg) % 2
                    cf = cast_f[:, buf * D:(buf + 1) * D]
                    cb = cast_b[:, buf * D:(buf + 1) * D]
                    E("sync", "dma_start", out=cf,
                      in_=src_ap[k * 128:(k + 1) * 128, cg * D:(cg + 1) * D],
                      waits=[("gdma", max(0, P.val("gdma") - 16))],
                      incs=[("dma", 16)])
                    E("vector", "tensor_copy", cb, cf,
                      waits=[("dma", P.val("dma")),
                             ("gdma", max(0, P.val("gdma") - 16))],
                      incs=[("V", 1)])
                    if fold_gw:
                        for q4 in range(4):
                            bnk = q4 % 2
                            E("tensor", "matmul", pb[bnk][:, :], gwdiag[:, :],
                              cb[:, q4 * TT:(q4 + 1) * TT], start=True, stop=True,
                              waits=[("V", P.val("V"))] + use_bank(bnk),
                              incs=[("T", 1)])
                            E("vector", "tensor_copy", cb[:, q4 * TT:(q4 + 1) * TT],
                              pb[bnk][:, :], waits=[("T", P.val("T"))],
                              incs=[("V", 1)])
                            bank_done(bnk)
                    E("gpsimd", "dma_start",
                      out=dst_sc[k, :, cg * D:(cg + 1) * D], in_=cb,
                      waits=[("V", P.val("V"))], incs=[("gdma", 16)])

        prepass_cast(wq_d, wsc["q"], NKD, D)
        prepass_cast(wk_d, wsc["k"], NKD, D)
        prepass_cast(wi_d, wsc["i"], NKD, D)
        prepass_cast(wo_d, wsc["o"], NKD, D, fold_gw=True)

        def ffn_quant(src_ap, dst_sc, nk, outdim, dbg_t=None, dbg_row=None):
            CW = 1408 if outdim % D else D
            ncg = outdim // CW
            n_el = float(nk * 128 * outdim)
            nq = (CW + TT - 1) // TT
            first = True
            Sbase, Tbase = P.val("S"), P.val("T")
            for k in range(nk):
                for cg in range(ncg):
                    n_it = k * ncg + cg
                    buf = n_it % 2
                    cf = cast_f[:, buf * D:buf * D + CW]
                    cb = cast_b[:, buf * D:buf * D + CW]
                    E("sync", "dma_start", out=cf,
                      in_=src_ap[k * 128:(k + 1) * 128, cg * CW:(cg + 1) * CW],
                      waits=[("S", max(Sbase, P.val("S") - 1))] +
                            ([("T", P.val("T")), ("gdma", P.val("gdma"))]
                             if n_it == 0 else []),
                      incs=[("dma", 16)])
                    E("scalar", "activation", cb, cf, AF.Abs,
                      waits=[("dma", P.val("dma")),
                             ("T", max(Tbase, P.val("T") - 1)),
                             ("gdma", P.val("gdma"))], incs=[("S", 1)])
                    for q4 in range(nq):
                        w = min(TT, CW - q4 * TT)
                        E("tensor", "matmul", pb[5][:, 0:w], ident[:, :],
                          cb[:, q4 * TT:q4 * TT + w], start=first,
                          stop=(k == nk - 1 and cg == ncg - 1 and q4 == nq - 1),
                          waits=([("S", P.val("S"))] +
                                 (use_bank(5) if first else [])),
                          incs=([("T", 1)] if q4 == nq - 1 else []))
                        first = False
            pcol4 = sgn[:, 4 * TT:4 * TT + 4]
            for j in range(4):
                E("vector", "tensor_reduce", pcol4[:, j:j + 1],
                  pb[5][:, j * 128:(j + 1) * 128],
                  mybir.AxisListType.X, ALU.add,
                  waits=[("T", P.val("T"))] if j == 0 else [],
                  incs=[("V", 1)])
            bank_done(5)
            E("sync", "dma_start", out=gsc_d[:, :], in_=pcol4,
              waits=[("V", P.val("V"))], incs=[("dma", 16)])
            E("sync", "dma_start", out=onesrowf[0:1, 0:TT],
              in_=gsc_d.rearrange("(o p) j -> o (p j)", o=1),
              waits=[("dma", P.val("dma"))], incs=[("dma", 16)])
            E("vector", "tensor_reduce", tsc[:, 0:1], onesrowf[0:1, 0:TT],
              mybir.AxisListType.X, ALU.add, waits=[("dma", P.val("dma"))],
              incs=[("V", 1)])
            E("scalar", "activation", tsc[:, 1:2], tsc[:, 0:1], AF.Copy,
              scale=1.0 / n_el, waits=[("V", P.val("V"))], incs=[("S", 1)])
            E("vector", "tensor_tensor", tsc[:, 2:3], tsc[:, 1:2],
              epst[:, :], ALU.add, waits=[("S", P.val("S"))], incs=[("V", 1)])
            E("scalar", "activation", tsc[:, 2:3], tsc[:, 2:3], AF.Copy,
              scale=0.5, waits=[("V", P.val("V"))], incs=[("S", 1)])
            if dbg_t:
                DBG(dbg_t, tsc[:, :], [("V", P.val("V"))])
            E("vector", "memset", onesrowf[:, :], 1.0, incs=[("V", 1)])
            E("tensor", "matmul", pb[5][0:1, :], tsc[:, 2:3], onesrowf[:, :],
              start=True, stop=True,
              waits=[("V", P.val("V")), ("S", P.val("S"))] + use_bank(5),
              incs=[("T", 1)])
            E("vector", "tensor_copy", trow[:, :], pb[5][0:1, :],
              waits=[("T", P.val("T"))], incs=[("V", 1)])
            bank_done(5)
            E("tensor", "matmul", pb[5][:, :], onesk1f[:, :], trow[:, :],
              start=True, stop=True,
              waits=[("V", P.val("V"))] + use_bank(5), incs=[("T", 1)])
            E("vector", "tensor_copy", tb_f[:, :], pb[5][:, :],
              waits=[("T", P.val("T"))], incs=[("V", 1)])
            bank_done(5)
            E("vector", "tensor_scalar", tnb_f[:, :], tb_f[:, :], -1.0, None,
              ALU.mult, incs=[("V", 1)])
            for k in range(nk):
                for cg in range(ncg):
                    buf = (k * ncg + cg) % 2
                    cf = cast_f[:, buf * D:buf * D + CW]
                    cb = cast_b[:, buf * D:buf * D + CW]
                    E("sync", "dma_start", out=cf,
                      in_=src_ap[k * 128:(k + 1) * 128, cg * CW:(cg + 1) * CW],
                      waits=[("gdma", max(0, P.val("gdma") - 16)),
                             ("S", P.val("S"))],
                      incs=[("dma", 16)])
                    for q4 in range(nq):
                        w = min(TT, CW - q4 * TT)
                        sl = cf[:, q4 * TT:q4 * TT + w]
                        E("vector", "tensor_tensor", qrollA[:, 0:w], sl,
                          tb_f[:, 0:w], ALU.is_gt,
                          waits=[("dma", P.val("dma"))] if q4 == 0 else [])
                        E("vector", "tensor_tensor", qrollB[:, 0:w], sl,
                          tnb_f[:, 0:w], ALU.is_lt)
                        E("vector", "tensor_tensor", cb[:, q4 * TT:q4 * TT + w],
                          qrollA[:, 0:w], qrollB[:, 0:w], ALU.subtract,
                          waits=([("gdma", max(0, P.val("gdma") - 16))]
                                 if q4 == 0 else []),
                          incs=[("V", 1)])
                    E("gpsimd", "dma_start",
                      out=dst_sc[k, :, cg * CW:(cg + 1) * CW], in_=cb,
                      waits=[("V", P.val("V"))], incs=[("gdma", 16)])

        ffn_quant(w1_d, wsc["1"], NKD, HID, dbg_t="tsc1", dbg_row="grow1")
        DBG("w1q", wsc["1"][0, :, :], [("gdma", P.val("gdma"))])
        if dbg:
            E("gpsimd", "dma_start", out=dbg_d["w1full"].rearrange(
                "(k p) h -> k p h", p=128), in_=wsc["1"][:, :, :],
              waits=[("gdma", P.val("gdma"))], incs=[("gdma", 16)])
        ffn_quant(w2_d, wsc["2"], NKD, HID, dbg_t="tsc2")
        ffn_quant(w3_d, wsc["3"], NKH, D, dbg_t="tsc3")
        DBG("w3q", wsc["3"][0, :, :], [("gdma", P.val("gdma"))])

        def emit_ln():
            xtb = slab[:, 0:D]
            for g in range(NGRP):
                E("vector", "tensor_copy", xtb, xt[:, g * D:(g + 1) * D],
                  waits=[("dma", P.val("dma")), ("T", P.val("T")),
                         ("V", P.val("V"))], incs=[("V", 1)])
                castV = P.val("V")
                for kq in range(NKD // 4):
                    bnk = (g * (NKD // 4) + kq) % 2
                    ws = use_bank(bnk) + [("V", castV)]
                    for j in range(4):
                        k = kq * 4 + j
                        E("tensor", "transpose",
                          pbh[bnk][:, j * 128:(j + 1) * 128],
                          xtb[:, k * 128:(k + 1) * 128], ident[:, :],
                          waits=ws if j == 0 else [],
                          incs=[("T", 1)] if j == 3 else [])
                    dst = xf[:].rearrange("p (k t) -> p k t", k=NKD)[
                        :, kq * 4:kq * 4 + 4, g * 128:(g + 1) * 128]
                    E("vector", "tensor_copy", dst,
                      pbh[bnk][:, 0:512].rearrange("p (j c) -> p j c", j=4),
                      waits=[("T", P.val("T"))], incs=[("V", 1)])
                    bank_done(bnk)
            vS = P.val("S")
            for k in range(NKD):
                xfk = xf[:, k * TT:(k + 1) * TT]
                E("scalar", "activation",
                  sq_roll[:, (k % 2) * TT:(k % 2 + 1) * TT], xfk, AF.Square,
                  waits=[("V", P.val("V"))] if k == 0 else [], incs=[("S", 1)])
                E("tensor", "matmul", pb[2][0:1, :], ones1[:, :], xfk,
                  start=(k == 0), stop=(k == NKD - 1),
                  waits=(use_bank(2) + [("V", P.val("V"))]) if k == 0 else [],
                  incs=[("T", 1)] if k == NKD - 1 else [])
                E("tensor", "matmul", pb[3][0:1, :], ones1[:, :],
                  sq_roll[:, (k % 2) * TT:(k % 2 + 1) * TT],
                  start=(k == 0), stop=(k == NKD - 1),
                  waits=(use_bank(3) if k == 0 else []) + [("S", P.val("S"))],
                  incs=[("T", 1)] if k == NKD - 1 else [])
            tT = P.val("T")
            E("vector", "tensor_scalar", rowf[:, 0:TT], pb[2][0:1, :], 1.0 / D,
              None, ALU.mult, waits=[("T", tT)], incs=[("V", 1)])
            bank_done(2)
            E("vector", "tensor_tensor", rowf[:, TT:2 * TT], rowf[:, 0:TT],
              rowf[:, 0:TT], ALU.mult)
            E("vector", "scalar_tensor_tensor", rowf[:, TT:2 * TT],
              pb[3][0:1, :], 1.0 / D, rowf[:, TT:2 * TT], ALU.mult,
              ALU.subtract, incs=[("V", 1)])
            bank_done(3)
            E("vector", "tensor_scalar", rowf[:, TT:2 * TT],
              rowf[:, TT:2 * TT], EPS_LN, None, ALU.add, incs=[("V", 1)])
            E("scalar", "activation", rowf[:, TT:2 * TT],
              rowf[:, TT:2 * TT], AF.Ln, waits=[("V", P.val("V"))])
            E("scalar", "activation", rowb[:, TT:2 * TT],
              rowf[:, TT:2 * TT], AF.Exp, scale=-0.5, incs=[("S", 1)])
            E("vector", "tensor_copy", rowb[:, 0:TT], rowf[:, 0:TT],
              incs=[("V", 1)])
            E("tensor", "matmul", pb[4][:, :], onesk1[:, :], rowb[:, 0:TT],
              start=True, stop=True,
              waits=use_bank(4) + [("V", P.val("V"))], incs=[("T", 1)])
            E("tensor", "matmul", pb[5][:, :], onesk1[:, :], rowb[:, TT:2 * TT],
              start=True, stop=True,
              waits=use_bank(5) + [("S", P.val("S"))], incs=[("T", 1)])
            tT = P.val("T")
            for k in range(NKD):
                E("vector", "tensor_tensor", xl[:, k * TT:(k + 1) * TT],
                  xf[:, k * TT:(k + 1) * TT], pb[4][:, :], ALU.subtract,
                  waits=[("T", tT)] if k == 0 else [])
                E("vector", "tensor_tensor", xl[:, k * TT:(k + 1) * TT],
                  xl[:, k * TT:(k + 1) * TT], pb[5][:, :], ALU.mult,
                  incs=[("V", 1)])
            bank_done(4)
            bank_done(5)

        def emit_proj_feat(w_key, m_lo, m_hi, evac_cb):
            first_mg = True
            for mg in range(m_lo // 4, m_hi // 4):
                dmavs = []
                for half in range(2):
                    ks, ke = half * 8, half * 8 + 8
                    tw = P.val("T") if (first_mg or half == 1) \
                        else P.val("T") - 4
                    E("sync", "dma_start",
                      out=slab[:, ks * TT:ke * TT].rearrange(
                          "p (k c) -> p k c", c=TT),
                      in_=wsc[w_key][ks:ke, :, mg * TT:(mg + 1) * TT].transpose(
                          [1, 0, 2]),
                      waits=[("T", tw)], incs=[("dma", 16)])
                    dmavs.append(P.val("dma"))
                first_mg = False
                for half in range(2):
                    for m in range(mg * 4, mg * 4 + 4):
                        bnk = m % 4
                        mo = (m - mg * 4) * 128
                        for kk in range(8):
                            k = half * 8 + kk
                            ws = []
                            if kk == 0:
                                ws = [("dma", dmavs[half])]
                                if half == 0:
                                    ws += use_bank(bnk) + [("V", P.val("V"))]
                            E("tensor", "matmul", pb[bnk][:, :],
                              slab[:, k * TT + mo:k * TT + mo + 128],
                              xl[:, k * TT:(k + 1) * TT],
                              start=(k == 0), stop=(k == NKD - 1),
                              waits=ws, incs=[("T", 1)] if kk == 7 else [])
                        if half == 1:
                            evac_cb(m, bnk)

        cons = {"g": [0, 0], "epem": [0, 0], "kh": [("V", 0), ("V", 0)],
                "qh": ("V", 0), "dpb": [("V", 0), ("V", 0)],
                "vtok": [("V", 0), ("V", 0)], "ktok": [("V", 0), ("V", 0)],
                "a": [("V", 0), ("V", 0)], "el": [0, 0], "pt": [0, 0],
                "lp": [0, 0]}
        obank_rot = [1, 7]

        def emit_scan_head(h, hslot, qreadyS, kreadyS, vreadyV):
            r = h % 2
            ht = h * TT
            E("scalar", "activation", g_roll[:, r * TT:(r + 1) * TT],
              sgn[:, hslot * TT:(hslot + 1) * TT], AF.Ln,
              waits=[("S", kreadyS), ("V", cons["g"][r])], incs=[("S", 1)])
            E("vector", "tensor_tensor_scan", lp_roll[:, r * TT:(r + 1) * TT],
              maskC[:, :], g_roll[:, r * TT:(r + 1) * TT], 0.0, ALU.mult,
              ALU.add, waits=[("S", P.val("S"))], incs=[("V", 1)])
            cons["g"][r] = P.val("V")
            scanV = P.val("V")
            lpr = lp_roll[:, r * TT:(r + 1) * TT]
            E("scalar", "activation", ep_roll[:, r * TT:(r + 1) * TT], lpr,
              AF.Exp, waits=[("V", scanV), ("V", cons["el"][r]),
                             ("V", cons["epem"][r])], incs=[("S", 1)])
            E("scalar", "activation", em_roll[:, r * TT:(r + 1) * TT], lpr,
              AF.Exp, scale=-1.0, incs=[("S", 1)])
            E("scalar", "activation", el16[:, r * 16:r * 16 + 8],
              lp_roll[:, r * TT + C - 1:(r + 1) * TT:C], AF.Exp,
              incs=[("S", 1)])
            sExps = P.val("S")
            er = el16[:, r * 16:(r + 1) * 16]
            E("vector", "tensor_tensor", er[:, 8:16], er[:, 0:8], ones8[:, :],
              ALU.mult, waits=[("S", sExps)])
            E("vector", "tensor_tensor", er[:, 8:16:2], er[:, 0:8:2],
              er[:, 1:8:2], ALU.mult, incs=[("V", 1)])
            elV = P.val("V")
            cons["el"][r] = elV
            E("vector", "scalar_tensor_tensor", qt[:, ht:ht + TT],
              qt[:, ht:ht + TT], SCALE, ep_roll[:, r * TT:(r + 1) * TT],
              ALU.mult, ALU.mult, waits=[("S", qreadyS)], incs=[("V", 1)])
            E("vector", "tensor_tensor", sig[:, ht:ht + TT], sig[:, ht:ht + TT],
              em_roll[:, r * TT:(r + 1) * TT], ALU.mult, incs=[("V", 1)])
            qktV = P.val("V")
            cons["epem"][r] = qktV
            E("tensor", "transpose", pbh[0][0:16, 0:128], er, ident[:, :],
              waits=[("V", elV)] + use_bank(0), incs=[("T", 1)])
            E("vector", "tensor_copy", pt16[:, r * 128:(r + 1) * 128],
              pbh[0][0:16, 0:128], waits=[("T", P.val("T"))], incs=[("V", 1)])
            ptr = pt16[:, r * 128:(r + 1) * 128]
            E("tensor", "matmul", pb[0][:, :], ptr, sel_fb[:, :], start=True,
              stop=True, waits=[("V", P.val("V"))], incs=[("T", 1)])
            E("vector", "tensor_tensor", kh_roll[:, r * TT:(r + 1) * TT],
              sig[:, ht:ht + TT], pb[0][:, :], ALU.mult,
              waits=[("T", P.val("T")), cons["kh"][r]], incs=[("V", 1)])
            khV = P.val("V")
            E("tensor", "matmul", pb[0][:, 0:TT // 2], ptr, sel_d0[:, :],
              start=True, stop=True, waits=[("V", khV)], incs=[("T", 1)])
            qtv = qt[:, ht:ht + TT].rearrange("p (j two c) -> p j two c",
                                              two=2, c=C)
            E("vector", "tensor_tensor",
              qh_roll[:, 0:TT // 2].rearrange("p (j c) -> p j c", c=C),
              qtv[:, :, 1, :],
              pb[0][:, 0:TT // 2].rearrange("p (j c) -> p j c", c=C), ALU.mult,
              waits=[("T", P.val("T")), cons["qh"]], incs=[("V", 1)])
            qhV = P.val("V")
            E("tensor", "matmul", pb[0][:, :], ptr, sel_dp[:, :], start=True,
              stop=True, waits=[("V", qhV)], incs=[("T", 1)])
            E("vector", "tensor_copy", dpb_roll[:, r * TT:(r + 1) * TT],
              pb[0][:, :], waits=[("T", P.val("T")), cons["dpb"][r]],
              incs=[("V", 1)])
            dpbV = P.val("V")
            for j in range(NPAIR):
                E("tensor", "transpose", pbh[0][:, j * 128:(j + 1) * 128],
                  vft[:, ht + j * 128:ht + (j + 1) * 128], ident[:, :],
                  waits=([("V", dpbV), ("V", vreadyV)]) if j == 0 else [],
                  incs=[("T", 1)] if j == NPAIR - 1 else [])
            E("vector", "tensor_copy", vtok_roll[:, r * TT:(r + 1) * TT],
              pbh[0][:, 0:TT], waits=[("T", P.val("T")), cons["vtok"][r]],
              incs=[("V", 1)])
            bank_done(0)
            vtokV = P.val("V")
            sfh = s_f[:, h * DKV:(h + 1) * DKV]
            sbh = s_bf[:, h * DKV:(h + 1) * DKV]
            chainV = P.val("V")
            for j in range(NPAIR):
                ab = 2 + j % 2
                kb = 4 + j % 2
                c0o = ht + 2 * j * C
                c1o = ht + (2 * j + 1) * C
                q0 = qt[:, c0o:c0o + C]
                q1h = qh_roll[:, j * C:(j + 1) * C]
                k0 = sig[:, c0o:c0o + C]
                k1 = sig[:, c1o:c1o + C]
                vp = vtok_roll[:, r * TT + j * DKV:r * TT + (j + 1) * DKV]
                ws = use_bank(ab) + [("V", qktV), ("V", qhV), ("V", vtokV)]
                E("tensor", "matmul", pb[ab][0:64, 0:C], k0, q0, start=True,
                  stop=False, tile_position=(0, 0), waits=ws)
                E("tensor", "matmul", pb[ab][0:64, C:2 * C], k0, q1h,
                  start=False, stop=True, tile_position=(0, 0))
                E("tensor", "matmul", pb[ab][64:128, C:2 * C], k1,
                  qt[:, c1o:c1o + C], start=True, stop=False,
                  tile_position=(0, 64))
                E("tensor", "matmul", pb[ab][64:128, 0:C], zer_b[:, 0:C], q0,
                  start=False, stop=True, tile_position=(0, 64),
                  incs=[("T", 1)])
                aT = P.val("T")
                E("tensor", "transpose", pbh[kb][:, 0:128],
                  kh_roll[:, r * TT + j * 128:r * TT + (j + 1) * 128],
                  ident[:, :], waits=use_bank(kb) + [("V", khV)],
                  incs=[("T", 1)])
                ktT = P.val("T")
                E("vector", "tensor_tensor", a_roll[:, (j % 2) * 128:
                  (j % 2 + 1) * 128], pb[ab][:, 0:128], trimask[:, :],
                  ALU.mult, waits=[("T", aT), cons["a"][j % 2]],
                  incs=[("V", 1)])
                bank_done(ab)
                aV = P.val("V")
                E("vector", "tensor_copy", ktok_roll[:, (j % 2) * 128:
                  (j % 2 + 1) * 128], pbh[kb][:, 0:128],
                  waits=[("T", ktT), cons["ktok"][j % 2]], incs=[("V", 1)])
                bank_done(kb)
                ktokV = P.val("V")
                ob = obank_rot[0]
                wso = (use_bank(ob) + [("V", chainV)]) if j == 0 else \
                    [("V", chainV)]
                E("tensor", "matmul", pb[ob][:, 2 * j * C:(2 * j + 1) * C],
                  sbh, q0, start=(j == 0), stop=False, waits=wso)
                E("tensor", "matmul", pb[ob][:, (2 * j + 1) * C:(2 * j + 2) * C],
                  sbh, q1h, start=False, stop=False)
                E("tensor", "matmul", pb[ob][:, 2 * j * C:(2 * j + 2) * C], vp,
                  a_roll[:, (j % 2) * 128:(j % 2 + 1) * 128], start=False,
                  stop=True, waits=[("V", aV)],
                  incs=[("T", 1)] if j == NPAIR - 1 else [])
                E("tensor", "matmul", pb[6][:, 0:128],
                  ktok_roll[:, (j % 2) * 128:(j % 2 + 1) * 128], vp,
                  start=True, stop=True,
                  waits=[("V", ktokV)] + use_bank(6), incs=[("T", 1)])
                uT = P.val("T")
                E("vector", "tensor_tensor", sfh, sfh,
                  dpb_roll[:, r * TT + j * DKV:r * TT + (j + 1) * DKV],
                  ALU.mult, waits=[("T", uT)])
                E("vector", "tensor_add", sfh, sfh, pb[6][:, 0:128],
                  incs=[("V", 1)])
                bank_done(6)
                E("vector", "tensor_copy", sbh, sfh, incs=[("V", 1)])
                chainV = P.val("V")
                cons["a"][j % 2] = ("T", P.val("T"))
                cons["ktok"][j % 2] = ("T", P.val("T"))
            oT = P.val("T")
            E("vector", "tensor_copy", o_s[:, ht:ht + TT],
              pb[obank_rot[0]][:, :], waits=[("T", oT)], incs=[("V", 1)])
            bank_done(obank_rot[0])
            obank_rot.reverse()
            cons["kh"][r] = ("T", P.val("T"))
            cons["qh"] = ("T", P.val("T"))
            cons["dpb"][r] = ("V", P.val("V"))
            cons["vtok"][r] = ("T", P.val("T"))

        def emit_tokout(w_key, nk, lhs_of_k, evac_cb, slab_ks=8):
            nhalf = nk // slab_ks
            first_mg = True
            for mg in range(4):
                dmavs = []
                for half in range(nhalf):
                    ks, ke = half * slab_ks, (half + 1) * slab_ks
                    tw = P.val("T") if first_mg \
                        else P.val("T") - (nhalf - 1 - half) * 4
                    E("sync", "dma_start",
                      out=slab[:, ks * TT:ke * TT].rearrange(
                          "p (k c) -> p k c", c=TT),
                      in_=wsc[w_key][ks:ke, :, mg * TT:(mg + 1) * TT].transpose(
                          [1, 0, 2]),
                      waits=[("T", tw)], incs=[("dma", 16)])
                    dmavs.append(P.val("dma"))
                first_mg = False
                for half in range(nhalf):
                    for g in range(NGRP):
                        bnk = 4 + g
                        for kk in range(slab_ks):
                            k = half * slab_ks + kk
                            ws = []
                            if kk == 0:
                                ws = [("dma", dmavs[half])]
                                if half == 0:
                                    ws += use_bank(bnk) + [("V", P.val("V"))]
                            E("tensor", "matmul", pb[bnk][:, :],
                              lhs_of_k(k, g), slab[:, k * TT:(k + 1) * TT],
                              start=(k == 0), stop=(k == nk - 1),
                              waits=ws,
                              incs=[("T", 1)] if kk == slab_ks - 1 else [])
                        if half == nhalf - 1:
                            evac_cb(mg, g, bnk)

        for i in range(n_tiles):
            E("sync", "dma_start",
              out=xt[:].rearrange("p (g d) -> p g d", g=NGRP),
              in_=x_d[i * TT:(i + 1) * TT, :].rearrange("(g p) d -> p g d",
                                                        p=128),
              waits=[("odma", P.val("odma")), ("V", P.val("V"))],
              incs=[("dma", 16)])
            emit_ln()
            if i == 0:
                DBG("xf", xf[:, :], [("V", P.val("V"))])
                DBG("xl", xl[:, :], [("V", P.val("V"))])

            def evac_q(m, bnk):
                E("scalar", "activation", qt[:, m * TT:(m + 1) * TT],
                  pb[bnk][:, :], AF.Relu, waits=[("T", P.val("T"))],
                  incs=[("S", 1)])
                bank_guard[bnk] = ("S", P.val("S"))

            emit_proj_feat("q", 0, NH, evac_q)
            qreadyS = P.val("S")
            if i == 0:
                DBG("qt0", qt, [("S", qreadyS)])

            def evac_v(m, bnk):
                E("vector", "tensor_copy", vft[:, m * TT:(m + 1) * TT],
                  pb[bnk][:, :], waits=[("T", P.val("T"))], incs=[("V", 1)])
                bank_done(bnk)

            emit_proj_feat("i", 0, NH, evac_v)
            vreadyV = P.val("V")
            if i == 0:
                DBG("vft0", vft, [("V", vreadyV)])

            for hg in range(2):
                h_lo, h_hi = hg * 8, hg * 8 + 8

                def evac_k(m, bnk, _h_lo=h_lo):
                    E("scalar", "activation", sig[:, m * TT:(m + 1) * TT],
                      pb[bnk][:, :], AF.Sigmoid, waits=[("T", P.val("T"))],
                      incs=[("S", 1)])
                    E("scalar", "activation",
                      sgn[:, (m - _h_lo) * TT:(m - _h_lo + 1) * TT],
                      pb[bnk][:, :], AF.Sigmoid, scale=-1.0, incs=[("S", 1)])
                    bank_guard[bnk] = ("S", P.val("S"))

                emit_proj_feat("k", h_lo, h_hi, evac_k)
                kreadyS = P.val("S")
                if i == 0 and hg == 0:
                    DBG("sig0", sig, [("S", kreadyS)])
                    DBG("sgn0", sgn[:, :], [("S", kreadyS)])
                for h in range(h_lo, h_hi):
                    emit_scan_head(h, h - h_lo, qreadyS, kreadyS, vreadyV)

            if i == 0:
                DBG("qt1", qt, [("V", P.val("V"))])
                DBG("sig1", sig, [("V", P.val("V"))])
                DBG("os0", o_s[:, :], [("V", P.val("V"))])
                DBG("sf0", s_f[:, :], [("V", P.val("V"))])
            sS0 = P.val("S")
            for h in range(NH):
                E("scalar", "activation",
                  sq_roll[:, (h % 2) * TT:(h % 2 + 1) * TT],
                  o_s[:, h * TT:(h + 1) * TT], AF.Square,
                  waits=[("V", P.val("V"))] if h == 0 else [], incs=[("S", 1)])
                E("tensor", "matmul", pb[2][0:1, :], ones1[:, :],
                  sq_roll[:, (h % 2) * TT:(h % 2 + 1) * TT],
                  start=(h == 0), stop=(h == NH - 1),
                  waits=(use_bank(2) if h == 0 else []) + [("S", P.val("S"))],
                  incs=[("T", 1)] if h == NH - 1 else [])
            E("vector", "tensor_scalar", rowf[:, 0:TT], pb[2][0:1, :], 1.0 / D,
              EPS_RMS, ALU.mult, ALU.add, waits=[("T", P.val("T"))],
              incs=[("V", 1)])
            bank_done(2)
            E("scalar", "activation", rowf[:, TT:2 * TT], rowf[:, 0:TT],
              AF.Ln, waits=[("V", P.val("V"))])
            E("scalar", "activation", rowb[:, 0:TT], rowf[:, TT:2 * TT],
              AF.Exp, scale=-0.5, incs=[("S", 1)])
            E("tensor", "matmul", pb[3][:, :], onesk1[:, :], rowb[:, 0:TT],
              start=True, stop=True,
              waits=use_bank(3) + [("S", P.val("S"))], incs=[("T", 1)])
            tT = P.val("T")
            for h in range(NH):
                E("vector", "tensor_tensor", o_s[:, h * TT:(h + 1) * TT],
                  o_s[:, h * TT:(h + 1) * TT], pb[3][:, :], ALU.mult,
                  waits=[("T", tT)] if h == 0 else [], incs=[("V", 1)])
            bank_done(3)

            if i == 0:
                DBG("orms", o_s[:, :], [("V", P.val("V"))])
            def evac_wo(mg, g, bnk):
                E("vector", "tensor_tensor",
                  xt[:, g * D + mg * TT:g * D + (mg + 1) * TT],
                  xt[:, g * D + mg * TT:g * D + (mg + 1) * TT],
                  pb[bnk][:, :], ALU.add, waits=[("T", P.val("T"))],
                  incs=[("V", 1)])
                bank_done(bnk)

            emit_tokout("o", NH,
                        lambda k, g: o_s[:, k * TT + g * 128:k * TT +
                                         (g + 1) * 128],
                        evac_wo)

            if i == 0:
                DBG("y0", xt[:, :], [("V", P.val("V"))])
            emit_ln()

            def ffn_waves(w_key, bnk0, first_dma_full):
                dmavs = []
                for half in range(2):
                    ks, ke = half * 8, half * 8 + 8
                    tw = P.val("T") if (first_dma_full or half == 1) \
                        else P.val("T") - 4
                    E("sync", "dma_start",
                      out=slab[:, ks * TT:ke * TT].rearrange(
                          "p (k c) -> p k c", c=TT),
                      in_=wsc[w_key][ks:ke, :, cg * TT:(cg + 1) * TT].transpose(
                          [1, 0, 2]),
                      waits=[("T", tw)], incs=[("dma", 16)])
                    dmavs.append(P.val("dma"))
                for half in range(2):
                    for m in range(4):
                        bnk = bnk0 + m
                        mo = m * 128
                        for kk in range(8):
                            k = half * 8 + kk
                            ws = []
                            if kk == 0:
                                ws = [("dma", dmavs[half])]
                                if half == 0:
                                    ws += use_bank(bnk) + [("V", P.val("V"))]
                            E("tensor", "matmul", pb[bnk][:, :],
                              slab[:, k * TT + mo:k * TT + mo + 128],
                              xl[:, k * TT:(k + 1) * TT],
                              start=(k == 0), stop=(k == NKD - 1),
                              waits=ws, incs=[("T", 1)] if kk == 7 else [])

            for cg in range(NKH // 4):
                ffn_waves("1", 0, cg == 0)
                ffn_waves("2", 4, False)
                x2T = P.val("T")
                for m in range(4):
                    hk = cg * 4 + m
                    E("scalar", "activation",
                      sq_roll[:, (m % 2) * TT:(m % 2 + 1) * TT],
                      pb[m][:, :], AF.Sigmoid,
                      waits=[("T", x2T), ("V", P.val("V"))] if m == 0 else [],
                      incs=[("S", 1)])
                    E("vector", "tensor_tensor", hid_b[:, hk * TT:(hk + 1) * TT],
                      sq_roll[:, (m % 2) * TT:(m % 2 + 1) * TT],
                      pb[4 + m][:, :], ALU.mult,
                      waits=[("S", P.val("S"))], incs=[("V", 1)])
                    bank_done(m, "S")
                    bank_done(4 + m)
            hidV = P.val("V")
            if i == 0:
                DBG("hid0", hid_b, [("V", hidV)])

            w3_grps = [(0, 8), (8, 16), (16, 24), (24, 32), (32, 40),
                       (40, 44)]
            phaseT = P.val("T")
            tafter = []
            for mg in range(4):
                for gi, (ks, ke) in enumerate(w3_grps):
                    buf = gi % 2
                    gflat = mg * len(w3_grps) + gi
                    tw = tafter[gflat - 2] if gflat >= 2 else phaseT
                    E("sync", "dma_start",
                      out=slab[:, buf * 8 * TT:buf * 8 * TT +
                               (ke - ks) * TT].rearrange(
                          "p (k c) -> p k c", c=TT),
                      in_=wsc["3"][ks:ke, :, mg * TT:(mg + 1) * TT].transpose(
                          [1, 0, 2]),
                      waits=[("T", tw)], incs=[("dma", 16)])
                    dmav = P.val("dma")
                    for g in range(NGRP):
                        bnk = 4 + g
                        ws = ((use_bank(bnk) if gi == 0 else []) +
                              [("dma", dmav), ("V", hidV)])
                        for kk in range(ke - ks):
                            k = ks + kk
                            E("tensor", "matmul", pb[bnk][:, :],
                              hid_b[:, k * TT + g * 128:
                                    k * TT + (g + 1) * 128],
                              slab[:, buf * 8 * TT + kk * TT:
                                   buf * 8 * TT + (kk + 1) * TT],
                              start=(k == 0), stop=(k == NKH - 1),
                              waits=ws if kk == 0 else [],
                              incs=([("T", 1)] if kk == ke - ks - 1 else []))
                    tafter.append(P.val("T"))
                for g in range(NGRP):
                    bnk = 4 + g
                    E("vector", "tensor_tensor",
                      xt[:, g * D + mg * TT:g * D + (mg + 1) * TT],
                      xt[:, g * D + mg * TT:g * D + (mg + 1) * TT],
                      pb[bnk][:, :], ALU.add,
                      waits=[("T", P.val("T"))] if g == 0 else [],
                      incs=[("V", 1)])
                    bank_done(bnk)
            E("sync", "dma_start",
              out=out_d[i * TT:(i + 1) * TT, :].rearrange("(g p) d -> p g d",
                                                          p=128),
              in_=xt[:].rearrange("p (g d) -> p g d", g=NGRP),
              waits=[("V", P.val("V"))], incs=[("odma", 16)])

        P.wait("sync", "odma", P.val("odma"))
        P.wait("gpsimd", "gdma", P.val("gdma"))

        @block.sync
        def _(sync):
            P.replay("sync", sync, sems)

        @block.gpsimd
        def _(gpsimd):
            P.replay("gpsimd", gpsimd, sems)

        @block.scalar
        def _(scalar):
            P.replay("scalar", scalar, sems)

        @block.vector
        def _(vector):
            P.replay("vector", vector, sems)

        @block.tensor
        def _(tensor):
            P.replay("tensor", tensor, sems)

    return nc, P


def host_inputs(inputs, core, T_core):
    x = inputs["L0_attn_hidden_states"][core]
    return {
        "x": np.ascontiguousarray(x[:T_core]),
        "wq": np.ascontiguousarray(inputs["L0_attn_Wq"]),
        "wk": np.ascontiguousarray(inputs["L0_attn_Wk"]),
        "wi": np.ascontiguousarray(inputs["L0_attn_Wi"]),
        "wo": np.ascontiguousarray(inputs["L0_attn_Wo"]),
        "gwt": np.ascontiguousarray(inputs["L0_attn_g_weight"]),
        "w1t": np.ascontiguousarray(inputs["L0_ffn_w1"].T),
        "w2t": np.ascontiguousarray(inputs["L0_ffn_w2"].T),
        "w3t": np.ascontiguousarray(inputs["L0_ffn_w3"].T),
    }



def kernel(**inputs):
    T_core = 8192
    nc, _ = build(T_core)
    in_maps = [host_inputs(inputs, c, T_core) for c in range(4)]
    res = run_bass_kernel_spmd(nc, in_maps, core_ids=[0, 1, 2, 3])
    out = np.stack([res.results[c]["out"] for c in range(4)], axis=0)
    return out.astype(np.float32)


# version 05 (final)
# speedup vs baseline: 1.4013x; 1.0348x over previous
import numpy as np
import ml_dtypes
import concourse.bass as bass
import concourse.mybir as mybir
from concourse.bass_utils import run_bass_kernel_spmd
from contextlib import ExitStack

F32 = mybir.dt.float32
BF16 = mybir.dt.bfloat16
I32 = mybir.dt.int32
AF = mybir.ActivationFunctionType
ALU = mybir.AluOpType

D = 2048
HID = 5632
NH = 16
DKV = 128
SCALE = DKV ** -0.5
EPS_LN = 1e-5
EPS_RMS = 1e-5
EPS_Q = 1e-8
C = 64
NKD = D // 128
NKH = HID // 128
TT = 512
NGRP = TT // 128
NPAIR = TT // 128


class Prog:
    ENGINES = ("sync", "gpsimd", "scalar", "vector", "tensor")

    def __init__(self):
        self.ops = {e: [] for e in self.ENGINES}
        self.cnt = {}

    def val(self, sem):
        return self.cnt.get(sem, 0)

    def emit(self, engine, fn, *args, incs=(), waits=(), **kwargs):
        for s, v in waits:
            if v > 0:
                assert v <= self.val(s), \
                    f"wait {s}>={v} but only {self.val(s)} emitted ({engine}.{fn})"
                self.ops[engine].append(("wait", s, v))
        self.ops[engine].append(("op", fn, args, kwargs, list(incs)))
        for s, n in incs:
            self.cnt[s] = self.cnt.get(s, 0) + n

    def wait(self, engine, sem, value):
        self.ops[engine].append(("wait", sem, value))

    def replay(self, engine_name, eng, sems):
        last_wait = {}
        for item in self.ops[engine_name]:
            if item[0] == "wait":
                _, s, v = item
                if last_wait.get(s, -1) < v:
                    eng.wait_ge(sems[s], v)
                    last_wait[s] = v
            else:
                _, fn, args, kwargs, incs = item
                try:
                    r = getattr(eng, fn)(*args, **kwargs)
                except Exception:
                    shapes = [getattr(a, 'shape', a) for a in args]
                    print(f"REPLAY FAIL {engine_name}.{fn} args={shapes} kw={list(kwargs)}")
                    raise
                for s, n in incs:
                    r.then_inc(sems[s], n)

    def n_ops(self):
        return {e: len(v) for e, v in self.ops.items()}


def build(T_core, dbg=False):
    n_tiles = T_core // TT
    assert T_core % TT == 0
    nc = bass.Bass("TRN2", target_bir_lowering=False, debug=False,
                   detect_race_conditions=False)

    x_d = nc.dram_tensor("x", [T_core, D], F32, kind="ExternalInput").ap()
    wq_d = nc.dram_tensor("wq", [D, D], F32, kind="ExternalInput").ap()
    wk_d = nc.dram_tensor("wk", [D, D], F32, kind="ExternalInput").ap()
    wi_d = nc.dram_tensor("wi", [D, D], F32, kind="ExternalInput").ap()
    wo_d = nc.dram_tensor("wo", [D, D], F32, kind="ExternalInput").ap()
    gw_d = nc.dram_tensor("gwt", [D], F32, kind="ExternalInput").ap()
    w1_d = nc.dram_tensor("w1t", [D, HID], F32, kind="ExternalInput").ap()
    w2_d = nc.dram_tensor("w2t", [D, HID], F32, kind="ExternalInput").ap()
    w3_d = nc.dram_tensor("w3t", [HID, D], F32, kind="ExternalInput").ap()
    out_d = nc.dram_tensor("out", [T_core, D], F32, kind="ExternalOutput").ap()
    dbg_d = {}
    if dbg:
        for nm, shp, dt_ in (("xf", [128, NKD * TT], BF16), ("xl", [128, NKD * TT], BF16),
                             ("qt0", [128, NH * TT], BF16), ("sig0", [128, NH * TT], BF16),
                             ("vft0", [128, NH * TT], BF16), ("sgn0", [128, 8 * TT], F32),
                             ("qt1", [128, NH * TT], BF16), ("sig1", [128, NH * TT], BF16),
                             ("os0", [128, NH * TT], BF16), ("sf0", [128, NH * DKV], F32),
                             ("orms", [128, NH * TT], BF16), ("y0", [128, NGRP * D], F32),
                             ("hid0", [128, NKH * TT], BF16),
                             ("w1q", [128, HID], BF16), ("w3q", [128, D], BF16),
                             ("w1full", [NKD * 128, HID], BF16),
                             ("tsc1", [1, 4], F32), ("tsc2", [1, 4], F32),
                             ("tsc3", [1, 4], F32), ("grow1", [1, TT], F32)):
            dbg_d[nm] = nc.dram_tensor("dbg_" + nm, shp, dt_, kind="ExternalOutput").ap()
    wsc = {
        "q": nc.dram_tensor("wq_s", [NKD, 128, D], BF16, kind="Internal").ap(),
        "k": nc.dram_tensor("wk_s", [NKD, 128, D], BF16, kind="Internal").ap(),
        "i": nc.dram_tensor("wi_s", [NKD, 128, D], BF16, kind="Internal").ap(),
        "o": nc.dram_tensor("wo_s", [NKD, 128, D], BF16, kind="Internal").ap(),
        "1": nc.dram_tensor("w1_s", [NKD, 128, HID], BF16, kind="Internal").ap(),
        "2": nc.dram_tensor("w2_s", [NKD, 128, HID], BF16, kind="Internal").ap(),
        "3": nc.dram_tensor("w3_s", [NKH, 128, D], BF16, kind="Internal").ap(),
    }
    gsc_d = nc.dram_tensor("gsc", [128, 4], F32, kind="Internal").ap()

    with ExitStack() as ctx:
        def sb(name, shape, dtype):
            return ctx.enter_context(nc.sbuf_tensor(name, shape, dtype)).ap()

        xt = sb("xt", [128, NGRP * D], F32)
        bigbuf = sb("bigbuf", [128, 3 * NH * TT], BF16)
        qt = bigbuf[:, 0:NH * TT]
        sig = bigbuf[:, NH * TT:2 * NH * TT]
        vft = bigbuf[:, 2 * NH * TT:3 * NH * TT]
        hid_b = bigbuf[:, 0:NKH * TT]
        xf = sb("xf", [128, NKD * TT], BF16)
        xl = sb("xl", [128, NKD * TT], BF16)
        sgn = sb("sgn", [128, 8 * TT], F32)
        o_s = sb("o_s", [128, NH * TT], BF16)
        slab = sb("slab", [128, NKD * TT], BF16)
        s_f = sb("s_f", [128, NH * DKV], F32)
        s_bf = sb("s_bf", [128, NH * DKV], BF16)
        g_roll = sb("g_roll", [128, 2 * TT], F32)
        lp_roll = sb("lp_roll", [128, 2 * TT], F32)
        ep_roll = sb("ep_roll", [128, 2 * TT], BF16)
        em_roll = sb("em_roll", [128, 2 * TT], BF16)
        kh_roll = sb("kh_roll", [128, 2 * TT], BF16)
        qh_roll = sb("qh_roll", [128, TT], BF16)
        dpb_roll = sb("dpb_roll", [128, 2 * NPAIR * DKV], BF16)
        vtok_roll = sb("vtok_roll", [128, 2 * NPAIR * DKV], BF16)
        ktok_roll = sb("ktok_roll", [128, 2 * DKV], BF16)
        a_roll = sb("a_roll", [128, 2 * 128], BF16)
        el16 = sb("el16", [128, 2 * 16], BF16)
        pt16 = sb("pt16", [16, 2 * 128], BF16)
        sq_roll = sb("sq_roll", [128, 2 * TT], BF16)
        rowf = sb("rowf", [1, 2 * TT], F32)
        rowb = sb("rowb", [1, 2 * TT], BF16)
        tb_f = lp_roll[:, 0:TT]
        tnb_f = sgn[:, 0:TT]
        cast_f = xt[:, 0:2 * D]
        cast_b = xf[:, 0:2 * D]
        qrollA = xl[:, 0:TT]
        qrollB = xl[:, TT:2 * TT]
        gwsb = sb("gwsb", [128, NKD], F32)
        gwrow = sb("gwrow", [1, 128], BF16)
        gwbc = sb("gwbc", [128, 128], BF16)
        gwdiag = sb("gwdiag", [128, 128], BF16)
        ones1 = sb("ones1", [128, 1], BF16)
        ones1f = sb("ones1f", [128, 1], F32)
        onesk1 = sb("onesk1", [1, 128], BF16)
        onesk1f = sb("onesk1f", [1, 128], F32)
        onesrowf = rowf[:, 0:TT]
        ones8 = sb("ones8", [128, 8], BF16)
        zer_b = sb("zer_b", [128, C], BF16)
        trimask = sb("trimask", [128, 128], BF16)
        ident = sb("ident", [128, 128], BF16)
        maskC = sb("maskC", [128, TT], F32)
        sel_fb = sb("sel_fb", [16, TT], BF16)
        sel_d0 = sb("sel_d0", [16, TT // 2], BF16)
        sel_dp = sb("sel_dp", [16, NPAIR * DKV], BF16)
        io_a = o_s[:, 0:256].bitcast(I32)
        io_fb = sgn[0:16, 0:TT].bitcast(I32)
        io_d0 = sgn[0:16, TT:TT + TT // 2].bitcast(I32)
        io_dp = sgn[0:16, 2 * TT:2 * TT + NPAIR * DKV].bitcast(I32)
        tsc = sb("tsc", [1, 4], F32)
        epst = sb("epst", [1, 1], F32)
        trow = rowf[:, TT:2 * TT]

        pb = [ctx.enter_context(nc.psum_tensor("pb%d" % i, [128, TT], F32)).ap()
              for i in range(8)]
        pbh = [p.bitcast(BF16) for p in pb]

        sems = {}
        for s in ("dma", "gdma", "odma", "S", "V", "T", "G"):
            sems[s] = ctx.enter_context(nc.semaphore(s))

        block = ctx.enter_context(nc.Block())
        P = Prog()

        def E(eng, fn, *a, incs=(), waits=(), **k):
            P.emit(eng, fn, *a, incs=incs, waits=waits, **k)

        def DBG(nm, src_ap, wait_sems):
            if dbg and nm in dbg_d:
                E("gpsimd", "dma_start", out=dbg_d[nm][:, :], in_=src_ap,
                  waits=wait_sems, incs=[("gdma", 16)])
                P.wait("vector", "gdma", P.val("gdma"))

        bank_guard = [("V", 0)] * 8

        def use_bank(i):
            return [bank_guard[i]]

        def bank_done(i, sem="V"):
            bank_guard[i] = (sem, P.val(sem))

        E("gpsimd", "iota", io_a[:, :], pattern=[[1, 128]], base=0,
          channel_multiplier=-1, incs=[("G", 1)])
        E("gpsimd", "iota", io_fb[:, :], pattern=[[1, 8], [0, C]], base=8,
          channel_multiplier=-1, incs=[("G", 1)])
        E("gpsimd", "iota", io_d0[:, :], pattern=[[2, 4], [0, C]], base=0,
          channel_multiplier=-1, incs=[("G", 1)])
        E("gpsimd", "iota", io_dp[:, :], pattern=[[2, 4], [0, DKV]], base=8,
          channel_multiplier=-1, incs=[("G", 1)])
        E("vector", "tensor_scalar", trimask[:, :], io_a[:, :], 0.0, None,
          ALU.is_ge, waits=[("G", 4)], incs=[("V", 1)])
        E("vector", "tensor_scalar", ident[:, :], io_a[:, :], 0.0, None,
          ALU.is_equal, incs=[("V", 1)])
        E("vector", "tensor_scalar", sel_fb[:, :], io_fb[:, :], 0.0, None,
          ALU.is_equal, incs=[("V", 1)])
        E("vector", "tensor_scalar", sel_d0[:, :], io_d0[:, :], 0.0, None,
          ALU.is_equal, incs=[("V", 1)])
        E("vector", "tensor_scalar", sel_dp[:, :], io_dp[:, :], 0.0, None,
          ALU.is_equal, incs=[("V", 1)])
        for cst, val in ((ones1, 1.0), (ones1f, 1.0), (onesk1, 1.0),
                         (onesk1f, 1.0),
                         (onesrowf, 1.0), (ones8, 1.0), (zer_b, 0.0)):
            E("vector", "memset", cst[:, :], val, incs=[("V", 1)])
        E("vector", "memset", epst[:, :], EPS_Q, incs=[("V", 1)])
        E("vector", "memset", maskC[:, :], 1.0)
        E("vector", "memset", maskC[:, 0:TT:C], 0.0, incs=[("V", 1)])
        E("vector", "memset", s_f[:, :], 0.0, incs=[("V", 1)])
        E("vector", "memset", s_bf[:, :], 0.0, incs=[("V", 1)])

        E("sync", "dma_start", out=gwsb[:, :],
          in_=gw_d.rearrange("(k p) -> p k", p=128),
          allow_slow_non_contiguous=True, incs=[("dma", 16)])

        def prepass_cast(src_ap, dst_sc, nk, outdim, fold_gw=False):
            ncg = outdim // D
            for k in range(nk):
                if fold_gw:
                    E("vector", "tensor_copy", gwbc[:, 0:1], gwsb[:, k:k + 1],
                      waits=[("dma", 16)], incs=[("V", 1)])
                    E("tensor", "transpose", pbh[6][0:1, 0:128], gwbc[:, 0:1],
                      ident[:, :],
                      waits=[("V", P.val("V"))] + use_bank(6),
                      incs=[("T", 1)])
                    E("vector", "tensor_copy", gwrow[:, :], pbh[6][0:1, 0:128],
                      waits=[("T", P.val("T"))], incs=[("V", 1)])
                    bank_done(6)
                    E("tensor", "matmul", pb[7][:, 0:128], gwrow[:, :],
                      onesk1[:, :], start=True, stop=True,
                      waits=[("V", P.val("V"))] + use_bank(7), incs=[("T", 1)])
                    E("vector", "tensor_copy", gwbc[:, :], pb[7][:, 0:128],
                      waits=[("T", P.val("T"))], incs=[("V", 1)])
                    bank_done(7)
                    E("vector", "tensor_tensor", gwdiag[:, :], ident[:, :],
                      gwbc[:, :], ALU.mult, incs=[("V", 1)])
                for cg in range(ncg):
                    buf = (k * ncg + cg) % 2
                    cf = cast_f[:, buf * D:(buf + 1) * D]
                    cb = cast_b[:, buf * D:(buf + 1) * D]
                    E("sync", "dma_start", out=cf,
                      in_=src_ap[k * 128:(k + 1) * 128, cg * D:(cg + 1) * D],
                      waits=[("gdma", max(0, P.val("gdma") - 16))],
                      incs=[("dma", 16)])
                    E("vector", "tensor_copy", cb, cf,
                      waits=[("dma", P.val("dma")),
                             ("gdma", max(0, P.val("gdma") - 16))],
                      incs=[("V", 1)])
                    if fold_gw:
                        for q4 in range(4):
                            bnk = q4 % 2
                            E("tensor", "matmul", pb[bnk][:, :], gwdiag[:, :],
                              cb[:, q4 * TT:(q4 + 1) * TT], start=True, stop=True,
                              waits=[("V", P.val("V"))] + use_bank(bnk),
                              incs=[("T", 1)])
                            E("vector", "tensor_copy", cb[:, q4 * TT:(q4 + 1) * TT],
                              pb[bnk][:, :], waits=[("T", P.val("T"))],
                              incs=[("V", 1)])
                            bank_done(bnk)
                    E("gpsimd", "dma_start",
                      out=dst_sc[k, :, cg * D:(cg + 1) * D], in_=cb,
                      waits=[("V", P.val("V"))], incs=[("gdma", 16)])

        prepass_cast(wq_d, wsc["q"], NKD, D)
        prepass_cast(wk_d, wsc["k"], NKD, D)
        prepass_cast(wi_d, wsc["i"], NKD, D)
        prepass_cast(wo_d, wsc["o"], NKD, D, fold_gw=True)

        def ffn_quant(src_ap, dst_sc, nk, outdim, dbg_t=None, dbg_row=None):
            CW = 1408 if outdim % D else D
            ncg = outdim // CW
            n_el = float(nk * 128 * outdim)
            nq = (CW + TT - 1) // TT
            first = True
            Sbase, Tbase = P.val("S"), P.val("T")
            for k in range(nk):
                for cg in range(ncg):
                    n_it = k * ncg + cg
                    buf = n_it % 2
                    cf = cast_f[:, buf * D:buf * D + CW]
                    cb = cast_b[:, buf * D:buf * D + CW]
                    E("sync", "dma_start", out=cf,
                      in_=src_ap[k * 128:(k + 1) * 128, cg * CW:(cg + 1) * CW],
                      waits=[("S", max(Sbase, P.val("S") - 1))] +
                            ([("T", P.val("T")), ("gdma", P.val("gdma"))]
                             if n_it == 0 else []),
                      incs=[("dma", 16)])
                    E("scalar", "activation", cb, cf, AF.Abs,
                      waits=[("dma", P.val("dma")),
                             ("T", max(Tbase, P.val("T") - 1)),
                             ("gdma", P.val("gdma"))], incs=[("S", 1)])
                    for q4 in range(nq):
                        w = min(TT, CW - q4 * TT)
                        E("tensor", "matmul", pb[5][:, 0:w], ident[:, :],
                          cb[:, q4 * TT:q4 * TT + w], start=first,
                          stop=(k == nk - 1 and cg == ncg - 1 and q4 == nq - 1),
                          waits=([("S", P.val("S"))] +
                                 (use_bank(5) if first else [])),
                          incs=([("T", 1)] if q4 == nq - 1 else []))
                        first = False
            pcol4 = sgn[:, 4 * TT:4 * TT + 4]
            for j in range(4):
                E("vector", "tensor_reduce", pcol4[:, j:j + 1],
                  pb[5][:, j * 128:(j + 1) * 128],
                  mybir.AxisListType.X, ALU.add,
                  waits=[("T", P.val("T"))] if j == 0 else [],
                  incs=[("V", 1)])
            bank_done(5)
            E("sync", "dma_start", out=gsc_d[:, :], in_=pcol4,
              waits=[("V", P.val("V"))], incs=[("dma", 16)])
            E("sync", "dma_start", out=onesrowf[0:1, 0:TT],
              in_=gsc_d.rearrange("(o p) j -> o (p j)", o=1),
              waits=[("dma", P.val("dma"))], incs=[("dma", 16)])
            E("vector", "tensor_reduce", tsc[:, 0:1], onesrowf[0:1, 0:TT],
              mybir.AxisListType.X, ALU.add, waits=[("dma", P.val("dma"))],
              incs=[("V", 1)])
            E("scalar", "activation", tsc[:, 1:2], tsc[:, 0:1], AF.Copy,
              scale=1.0 / n_el, waits=[("V", P.val("V"))], incs=[("S", 1)])
            E("vector", "tensor_tensor", tsc[:, 2:3], tsc[:, 1:2],
              epst[:, :], ALU.add, waits=[("S", P.val("S"))], incs=[("V", 1)])
            E("scalar", "activation", tsc[:, 2:3], tsc[:, 2:3], AF.Copy,
              scale=0.5, waits=[("V", P.val("V"))], incs=[("S", 1)])
            if dbg_t:
                DBG(dbg_t, tsc[:, :], [("V", P.val("V"))])
            E("vector", "memset", onesrowf[:, :], 1.0, incs=[("V", 1)])
            E("tensor", "matmul", pb[5][0:1, :], tsc[:, 2:3], onesrowf[:, :],
              start=True, stop=True,
              waits=[("V", P.val("V")), ("S", P.val("S"))] + use_bank(5),
              incs=[("T", 1)])
            E("vector", "tensor_copy", trow[:, :], pb[5][0:1, :],
              waits=[("T", P.val("T"))], incs=[("V", 1)])
            bank_done(5)
            E("tensor", "matmul", pb[5][:, :], onesk1f[:, :], trow[:, :],
              start=True, stop=True,
              waits=[("V", P.val("V"))] + use_bank(5), incs=[("T", 1)])
            E("vector", "tensor_copy", tb_f[:, :], pb[5][:, :],
              waits=[("T", P.val("T"))], incs=[("V", 1)])
            bank_done(5)
            E("vector", "tensor_scalar", tnb_f[:, :], tb_f[:, :], -1.0, None,
              ALU.mult, incs=[("V", 1)])
            for k in range(nk):
                for cg in range(ncg):
                    buf = (k * ncg + cg) % 2
                    cf = cast_f[:, buf * D:buf * D + CW]
                    cb = cast_b[:, buf * D:buf * D + CW]
                    E("sync", "dma_start", out=cf,
                      in_=src_ap[k * 128:(k + 1) * 128, cg * CW:(cg + 1) * CW],
                      waits=[("gdma", max(0, P.val("gdma") - 16)),
                             ("S", P.val("S"))],
                      incs=[("dma", 16)])
                    for q4 in range(nq):
                        w = min(TT, CW - q4 * TT)
                        sl = cf[:, q4 * TT:q4 * TT + w]
                        E("vector", "tensor_tensor", qrollA[:, 0:w], sl,
                          tb_f[:, 0:w], ALU.is_gt,
                          waits=[("dma", P.val("dma"))] if q4 == 0 else [])
                        E("vector", "tensor_tensor", qrollB[:, 0:w], sl,
                          tnb_f[:, 0:w], ALU.is_lt)
                        E("vector", "tensor_tensor", cb[:, q4 * TT:q4 * TT + w],
                          qrollA[:, 0:w], qrollB[:, 0:w], ALU.subtract,
                          waits=([("gdma", max(0, P.val("gdma") - 16))]
                                 if q4 == 0 else []),
                          incs=[("V", 1)])
                    E("gpsimd", "dma_start",
                      out=dst_sc[k, :, cg * CW:(cg + 1) * CW], in_=cb,
                      waits=[("V", P.val("V"))], incs=[("gdma", 16)])

        ffn_quant(w1_d, wsc["1"], NKD, HID, dbg_t="tsc1", dbg_row="grow1")
        DBG("w1q", wsc["1"][0, :, :], [("gdma", P.val("gdma"))])
        if dbg:
            E("gpsimd", "dma_start", out=dbg_d["w1full"].rearrange(
                "(k p) h -> k p h", p=128), in_=wsc["1"][:, :, :],
              waits=[("gdma", P.val("gdma"))], incs=[("gdma", 16)])
        ffn_quant(w2_d, wsc["2"], NKD, HID, dbg_t="tsc2")
        ffn_quant(w3_d, wsc["3"], NKH, D, dbg_t="tsc3")
        DBG("w3q", wsc["3"][0, :, :], [("gdma", P.val("gdma"))])

        def emit_ln():
            xtb = slab[:, 0:D]
            for g in range(NGRP):
                E("vector", "tensor_copy", xtb, xt[:, g * D:(g + 1) * D],
                  waits=[("dma", P.val("dma")), ("T", P.val("T")),
                         ("V", P.val("V"))], incs=[("V", 1)])
                castV = P.val("V")
                for kq in range(NKD // 4):
                    bnk = (g * (NKD // 4) + kq) % 2
                    ws = use_bank(bnk) + [("V", castV)]
                    for j in range(4):
                        k = kq * 4 + j
                        E("tensor", "transpose",
                          pbh[bnk][:, j * 128:(j + 1) * 128],
                          xtb[:, k * 128:(k + 1) * 128], ident[:, :],
                          waits=ws if j == 0 else [],
                          incs=[("T", 1)] if j == 3 else [])
                    dst = xf[:].rearrange("p (k t) -> p k t", k=NKD)[
                        :, kq * 4:kq * 4 + 4, g * 128:(g + 1) * 128]
                    E("vector", "tensor_copy", dst,
                      pbh[bnk][:, 0:512].rearrange("p (j c) -> p j c", j=4),
                      waits=[("T", P.val("T"))], incs=[("V", 1)])
                    bank_done(bnk)
            vS = P.val("S")
            for k in range(NKD):
                xfk = xf[:, k * TT:(k + 1) * TT]
                E("scalar", "activation",
                  sq_roll[:, (k % 2) * TT:(k % 2 + 1) * TT], xfk, AF.Square,
                  waits=[("V", P.val("V"))] if k == 0 else [], incs=[("S", 1)])
                E("tensor", "matmul", pb[2][0:1, :], ones1[:, :], xfk,
                  start=(k == 0), stop=(k == NKD - 1),
                  waits=(use_bank(2) + [("V", P.val("V"))]) if k == 0 else [],
                  incs=[("T", 1)] if k == NKD - 1 else [])
                E("tensor", "matmul", pb[3][0:1, :], ones1[:, :],
                  sq_roll[:, (k % 2) * TT:(k % 2 + 1) * TT],
                  start=(k == 0), stop=(k == NKD - 1),
                  waits=(use_bank(3) if k == 0 else []) + [("S", P.val("S"))],
                  incs=[("T", 1)] if k == NKD - 1 else [])
            tT = P.val("T")
            E("vector", "tensor_scalar", rowf[:, 0:TT], pb[2][0:1, :], 1.0 / D,
              None, ALU.mult, waits=[("T", tT)], incs=[("V", 1)])
            bank_done(2)
            E("vector", "tensor_tensor", rowf[:, TT:2 * TT], rowf[:, 0:TT],
              rowf[:, 0:TT], ALU.mult)
            E("vector", "scalar_tensor_tensor", rowf[:, TT:2 * TT],
              pb[3][0:1, :], 1.0 / D, rowf[:, TT:2 * TT], ALU.mult,
              ALU.subtract, incs=[("V", 1)])
            bank_done(3)
            E("vector", "tensor_scalar", rowf[:, TT:2 * TT],
              rowf[:, TT:2 * TT], EPS_LN, None, ALU.add, incs=[("V", 1)])
            E("scalar", "activation", rowf[:, TT:2 * TT],
              rowf[:, TT:2 * TT], AF.Ln, waits=[("V", P.val("V"))])
            E("scalar", "activation", rowb[:, TT:2 * TT],
              rowf[:, TT:2 * TT], AF.Exp, scale=-0.5, incs=[("S", 1)])
            E("vector", "tensor_copy", rowb[:, 0:TT], rowf[:, 0:TT],
              incs=[("V", 1)])
            E("tensor", "matmul", pb[4][:, :], onesk1[:, :], rowb[:, 0:TT],
              start=True, stop=True,
              waits=use_bank(4) + [("V", P.val("V"))], incs=[("T", 1)])
            E("tensor", "matmul", pb[5][:, :], onesk1[:, :], rowb[:, TT:2 * TT],
              start=True, stop=True,
              waits=use_bank(5) + [("S", P.val("S"))], incs=[("T", 1)])
            tT = P.val("T")
            for k in range(NKD):
                E("vector", "tensor_tensor", xl[:, k * TT:(k + 1) * TT],
                  xf[:, k * TT:(k + 1) * TT], pb[4][:, :], ALU.subtract,
                  waits=[("T", tT)] if k == 0 else [])
                E("vector", "tensor_tensor", xl[:, k * TT:(k + 1) * TT],
                  xl[:, k * TT:(k + 1) * TT], pb[5][:, :], ALU.mult,
                  incs=[("V", 1)])
            bank_done(4)
            bank_done(5)

        def emit_proj_feat(w_key, m_lo, m_hi, evac_cb, slack_first=False):
            first_mg = not slack_first
            for mg in range(m_lo // 4, m_hi // 4):
                dmavs = []
                for half in range(2):
                    ks, ke = half * 8, half * 8 + 8
                    tw = P.val("T") if (first_mg or half == 1) \
                        else P.val("T") - 4
                    E("sync", "dma_start",
                      out=slab[:, ks * TT:ke * TT].rearrange(
                          "p (k c) -> p k c", c=TT),
                      in_=wsc[w_key][ks:ke, :, mg * TT:(mg + 1) * TT].transpose(
                          [1, 0, 2]),
                      waits=[("T", tw)], incs=[("dma", 16)])
                    dmavs.append(P.val("dma"))
                first_mg = False
                for half in range(2):
                    for m in range(mg * 4, mg * 4 + 4):
                        bnk = m % 4
                        mo = (m - mg * 4) * 128
                        for kk in range(8):
                            k = half * 8 + kk
                            ws = []
                            if kk == 0:
                                ws = [("dma", dmavs[half])]
                                if half == 0:
                                    ws += use_bank(bnk) + [("V", P.val("V"))]
                            E("tensor", "matmul", pb[bnk][:, :],
                              slab[:, k * TT + mo:k * TT + mo + 128],
                              xl[:, k * TT:(k + 1) * TT],
                              start=(k == 0), stop=(k == NKD - 1),
                              waits=ws, incs=[("T", 1)] if kk == 7 else [])
                        if half == 1:
                            evac_cb(m, bnk)

        cons = {"g": [0, 0], "epem": [0, 0], "kh": [("V", 0), ("V", 0)],
                "qh": ("V", 0), "dpb": [("V", 0), ("V", 0)],
                "vtok": [("V", 0), ("V", 0)], "ktok": [("V", 0), ("V", 0)],
                "a": [("V", 0), ("V", 0)], "el": [0, 0], "pt": [0, 0],
                "lp": [0, 0]}
        obank_rot = [1, 7]

        def emit_scan_head(h, hslot, qreadyS, kreadyS, vreadyV):
            r = h % 2
            ht = h * TT
            E("scalar", "activation", g_roll[:, r * TT:(r + 1) * TT],
              sgn[:, hslot * TT:(hslot + 1) * TT], AF.Ln,
              waits=[("S", kreadyS), ("V", cons["g"][r])], incs=[("S", 1)])
            E("vector", "tensor_tensor_scan", lp_roll[:, r * TT:(r + 1) * TT],
              maskC[:, :], g_roll[:, r * TT:(r + 1) * TT], 0.0, ALU.mult,
              ALU.add, waits=[("S", P.val("S"))], incs=[("V", 1)])
            cons["g"][r] = P.val("V")
            scanV = P.val("V")
            lpr = lp_roll[:, r * TT:(r + 1) * TT]
            E("scalar", "activation", ep_roll[:, r * TT:(r + 1) * TT], lpr,
              AF.Exp, waits=[("V", scanV), ("V", cons["el"][r]),
                             ("V", cons["epem"][r])], incs=[("S", 1)])
            E("scalar", "activation", em_roll[:, r * TT:(r + 1) * TT], lpr,
              AF.Exp, scale=-1.0, incs=[("S", 1)])
            E("scalar", "activation", el16[:, r * 16:r * 16 + 8],
              lp_roll[:, r * TT + C - 1:(r + 1) * TT:C], AF.Exp,
              incs=[("S", 1)])
            sExps = P.val("S")
            er = el16[:, r * 16:(r + 1) * 16]
            E("vector", "tensor_tensor", er[:, 8:16], er[:, 0:8], ones8[:, :],
              ALU.mult, waits=[("S", sExps)])
            E("vector", "tensor_tensor", er[:, 8:16:2], er[:, 0:8:2],
              er[:, 1:8:2], ALU.mult, incs=[("V", 1)])
            elV = P.val("V")
            cons["el"][r] = elV
            E("vector", "scalar_tensor_tensor", qt[:, ht:ht + TT],
              qt[:, ht:ht + TT], SCALE, ep_roll[:, r * TT:(r + 1) * TT],
              ALU.mult, ALU.mult, waits=[("S", qreadyS)], incs=[("V", 1)])
            E("vector", "tensor_tensor", sig[:, ht:ht + TT], sig[:, ht:ht + TT],
              em_roll[:, r * TT:(r + 1) * TT], ALU.mult, incs=[("V", 1)])
            qktV = P.val("V")
            cons["epem"][r] = qktV
            E("tensor", "transpose", pbh[0][0:16, 0:128], er, ident[:, :],
              waits=[("V", elV)] + use_bank(0), incs=[("T", 1)])
            E("vector", "tensor_copy", pt16[:, r * 128:(r + 1) * 128],
              pbh[0][0:16, 0:128], waits=[("T", P.val("T"))], incs=[("V", 1)])
            ptr = pt16[:, r * 128:(r + 1) * 128]
            E("tensor", "matmul", pb[0][:, :], ptr, sel_fb[:, :], start=True,
              stop=True, waits=[("V", P.val("V"))], incs=[("T", 1)])
            E("vector", "tensor_tensor", kh_roll[:, r * TT:(r + 1) * TT],
              sig[:, ht:ht + TT], pb[0][:, :], ALU.mult,
              waits=[("T", P.val("T")), cons["kh"][r]], incs=[("V", 1)])
            khV = P.val("V")
            E("tensor", "matmul", pb[0][:, 0:TT // 2], ptr, sel_d0[:, :],
              start=True, stop=True, waits=[("V", khV)], incs=[("T", 1)])
            qtv = qt[:, ht:ht + TT].rearrange("p (j two c) -> p j two c",
                                              two=2, c=C)
            E("vector", "tensor_tensor",
              qh_roll[:, 0:TT // 2].rearrange("p (j c) -> p j c", c=C),
              qtv[:, :, 1, :],
              pb[0][:, 0:TT // 2].rearrange("p (j c) -> p j c", c=C), ALU.mult,
              waits=[("T", P.val("T")), cons["qh"]], incs=[("V", 1)])
            qhV = P.val("V")
            E("tensor", "matmul", pb[0][:, :], ptr, sel_dp[:, :], start=True,
              stop=True, waits=[("V", qhV)], incs=[("T", 1)])
            E("vector", "tensor_copy", dpb_roll[:, r * TT:(r + 1) * TT],
              pb[0][:, :], waits=[("T", P.val("T")), cons["dpb"][r]],
              incs=[("V", 1)])
            dpbV = P.val("V")
            for j in range(NPAIR):
                E("tensor", "transpose", pbh[0][:, j * 128:(j + 1) * 128],
                  vft[:, ht + j * 128:ht + (j + 1) * 128], ident[:, :],
                  waits=([("V", dpbV), ("V", vreadyV)]) if j == 0 else [],
                  incs=[("T", 1)] if j == NPAIR - 1 else [])
            E("vector", "tensor_copy", vtok_roll[:, r * TT:(r + 1) * TT],
              pbh[0][:, 0:TT], waits=[("T", P.val("T")), cons["vtok"][r]],
              incs=[("V", 1)])
            bank_done(0)
            vtokV = P.val("V")
            sfh = s_f[:, h * DKV:(h + 1) * DKV]
            sbh = s_bf[:, h * DKV:(h + 1) * DKV]
            chainV = P.val("V")
            for j in range(NPAIR):
                ab = 2 + j % 2
                kb = 4 + j % 2
                c0o = ht + 2 * j * C
                c1o = ht + (2 * j + 1) * C
                q0 = qt[:, c0o:c0o + C]
                q1h = qh_roll[:, j * C:(j + 1) * C]
                k0 = sig[:, c0o:c0o + C]
                k1 = sig[:, c1o:c1o + C]
                vp = vtok_roll[:, r * TT + j * DKV:r * TT + (j + 1) * DKV]
                ws = use_bank(ab) + [("V", qktV), ("V", qhV), ("V", vtokV)]
                E("tensor", "matmul", pb[ab][0:64, 0:C], k0, q0, start=True,
                  stop=False, tile_position=(0, 0), waits=ws)
                E("tensor", "matmul", pb[ab][0:64, C:2 * C], k0, q1h,
                  start=False, stop=True, tile_position=(0, 0))
                E("tensor", "matmul", pb[ab][64:128, C:2 * C], k1,
                  qt[:, c1o:c1o + C], start=True, stop=False,
                  tile_position=(0, 64))
                E("tensor", "matmul", pb[ab][64:128, 0:C], zer_b[:, 0:C], q0,
                  start=False, stop=True, tile_position=(0, 64),
                  incs=[("T", 1)])
                aT = P.val("T")
                E("tensor", "transpose", pbh[kb][:, 0:128],
                  kh_roll[:, r * TT + j * 128:r * TT + (j + 1) * 128],
                  ident[:, :], waits=use_bank(kb) + [("V", khV)],
                  incs=[("T", 1)])
                ktT = P.val("T")
                E("vector", "tensor_tensor", a_roll[:, (j % 2) * 128:
                  (j % 2 + 1) * 128], pb[ab][:, 0:128], trimask[:, :],
                  ALU.mult, waits=[("T", aT), cons["a"][j % 2]],
                  incs=[("V", 1)])
                bank_done(ab)
                aV = P.val("V")
                E("vector", "tensor_copy", ktok_roll[:, (j % 2) * 128:
                  (j % 2 + 1) * 128], pbh[kb][:, 0:128],
                  waits=[("T", ktT), cons["ktok"][j % 2]], incs=[("V", 1)])
                bank_done(kb)
                ktokV = P.val("V")
                ob = obank_rot[0]
                wso = (use_bank(ob) + [("V", chainV)]) if j == 0 else \
                    [("V", chainV)]
                E("tensor", "matmul", pb[ob][:, 2 * j * C:(2 * j + 1) * C],
                  sbh, q0, start=(j == 0), stop=False, waits=wso)
                E("tensor", "matmul", pb[ob][:, (2 * j + 1) * C:(2 * j + 2) * C],
                  sbh, q1h, start=False, stop=False)
                E("tensor", "matmul", pb[ob][:, 2 * j * C:(2 * j + 2) * C], vp,
                  a_roll[:, (j % 2) * 128:(j % 2 + 1) * 128], start=False,
                  stop=True, waits=[("V", aV)],
                  incs=[("T", 1)] if j == NPAIR - 1 else [])
                E("tensor", "matmul", pb[6][:, 0:128],
                  ktok_roll[:, (j % 2) * 128:(j % 2 + 1) * 128], vp,
                  start=True, stop=True,
                  waits=[("V", ktokV)] + use_bank(6), incs=[("T", 1)])
                uT = P.val("T")
                E("vector", "tensor_tensor", sfh, sfh,
                  dpb_roll[:, r * TT + j * DKV:r * TT + (j + 1) * DKV],
                  ALU.mult, waits=[("T", uT)])
                E("vector", "tensor_add", sfh, sfh, pb[6][:, 0:128],
                  incs=[("V", 1)])
                bank_done(6)
                E("vector", "tensor_copy", sbh, sfh, incs=[("V", 1)])
                chainV = P.val("V")
                cons["a"][j % 2] = ("T", P.val("T"))
                cons["ktok"][j % 2] = ("T", P.val("T"))
            oT = P.val("T")
            E("vector", "tensor_copy", o_s[:, ht:ht + TT],
              pb[obank_rot[0]][:, :], waits=[("T", oT)], incs=[("V", 1)])
            bank_done(obank_rot[0])
            obank_rot.reverse()
            cons["kh"][r] = ("T", P.val("T"))
            cons["qh"] = ("T", P.val("T"))
            cons["dpb"][r] = ("V", P.val("V"))
            cons["vtok"][r] = ("T", P.val("T"))

        def emit_tokout(w_key, nk, lhs_of_k, evac_cb, slab_ks=8):
            nhalf = nk // slab_ks
            first_mg = True
            for mg in range(4):
                dmavs = []
                for half in range(nhalf):
                    ks, ke = half * slab_ks, (half + 1) * slab_ks
                    tw = P.val("T") if first_mg \
                        else P.val("T") - (nhalf - 1 - half) * 4
                    E("sync", "dma_start",
                      out=slab[:, ks * TT:ke * TT].rearrange(
                          "p (k c) -> p k c", c=TT),
                      in_=wsc[w_key][ks:ke, :, mg * TT:(mg + 1) * TT].transpose(
                          [1, 0, 2]),
                      waits=[("T", tw)], incs=[("dma", 16)])
                    dmavs.append(P.val("dma"))
                first_mg = False
                for half in range(nhalf):
                    for g in range(NGRP):
                        bnk = 4 + g
                        for kk in range(slab_ks):
                            k = half * slab_ks + kk
                            ws = []
                            if kk == 0:
                                ws = [("dma", dmavs[half])]
                                if half == 0:
                                    ws += use_bank(bnk) + [("V", P.val("V"))]
                            E("tensor", "matmul", pb[bnk][:, :],
                              lhs_of_k(k, g), slab[:, k * TT:(k + 1) * TT],
                              start=(k == 0), stop=(k == nk - 1),
                              waits=ws,
                              incs=[("T", 1)] if kk == slab_ks - 1 else [])
                        if half == nhalf - 1:
                            evac_cb(mg, g, bnk)

        for i in range(n_tiles):
            E("sync", "dma_start",
              out=xt[:].rearrange("p (g d) -> p g d", g=NGRP),
              in_=x_d[i * TT:(i + 1) * TT, :].rearrange("(g p) d -> p g d",
                                                        p=128),
              waits=[("odma", P.val("odma")), ("V", P.val("V"))],
              incs=[("dma", 16)])
            emit_ln()
            if i == 0:
                DBG("xf", xf[:, :], [("V", P.val("V"))])
                DBG("xl", xl[:, :], [("V", P.val("V"))])

            def evac_q(m, bnk):
                E("scalar", "activation", qt[:, m * TT:(m + 1) * TT],
                  pb[bnk][:, :], AF.Relu, waits=[("T", P.val("T"))],
                  incs=[("S", 1)])
                bank_guard[bnk] = ("S", P.val("S"))

            emit_proj_feat("q", 0, NH, evac_q)
            qreadyS = P.val("S")
            if i == 0:
                DBG("qt0", qt, [("S", qreadyS)])

            def evac_v(m, bnk):
                E("vector", "tensor_copy", vft[:, m * TT:(m + 1) * TT],
                  pb[bnk][:, :], waits=[("T", P.val("T"))], incs=[("V", 1)])
                bank_done(bnk)

            emit_proj_feat("i", 0, NH, evac_v, slack_first=True)
            vreadyV = P.val("V")
            if i == 0:
                DBG("vft0", vft, [("V", vreadyV)])

            for hg in range(2):
                h_lo, h_hi = hg * 8, hg * 8 + 8

                def evac_k(m, bnk, _h_lo=h_lo):
                    E("scalar", "activation", sig[:, m * TT:(m + 1) * TT],
                      pb[bnk][:, :], AF.Sigmoid, waits=[("T", P.val("T"))],
                      incs=[("S", 1)])
                    E("scalar", "activation",
                      sgn[:, (m - _h_lo) * TT:(m - _h_lo + 1) * TT],
                      pb[bnk][:, :], AF.Sigmoid, scale=-1.0, incs=[("S", 1)])
                    bank_guard[bnk] = ("S", P.val("S"))

                emit_proj_feat("k", h_lo, h_hi, evac_k, slack_first=True)
                kreadyS = P.val("S")
                if i == 0 and hg == 0:
                    DBG("sig0", sig, [("S", kreadyS)])
                    DBG("sgn0", sgn[:, :], [("S", kreadyS)])
                for h in range(h_lo, h_hi):
                    emit_scan_head(h, h - h_lo, qreadyS, kreadyS, vreadyV)

            if i == 0:
                DBG("qt1", qt, [("V", P.val("V"))])
                DBG("sig1", sig, [("V", P.val("V"))])
                DBG("os0", o_s[:, :], [("V", P.val("V"))])
                DBG("sf0", s_f[:, :], [("V", P.val("V"))])
            sS0 = P.val("S")
            for h in range(NH):
                E("scalar", "activation",
                  sq_roll[:, (h % 2) * TT:(h % 2 + 1) * TT],
                  o_s[:, h * TT:(h + 1) * TT], AF.Square,
                  waits=[("V", P.val("V"))] if h == 0 else [], incs=[("S", 1)])
                E("tensor", "matmul", pb[2][0:1, :], ones1[:, :],
                  sq_roll[:, (h % 2) * TT:(h % 2 + 1) * TT],
                  start=(h == 0), stop=(h == NH - 1),
                  waits=(use_bank(2) if h == 0 else []) + [("S", P.val("S"))],
                  incs=[("T", 1)] if h == NH - 1 else [])
            E("vector", "tensor_scalar", rowf[:, 0:TT], pb[2][0:1, :], 1.0 / D,
              EPS_RMS, ALU.mult, ALU.add, waits=[("T", P.val("T"))],
              incs=[("V", 1)])
            bank_done(2)
            E("scalar", "activation", rowf[:, TT:2 * TT], rowf[:, 0:TT],
              AF.Ln, waits=[("V", P.val("V"))])
            E("scalar", "activation", rowb[:, 0:TT], rowf[:, TT:2 * TT],
              AF.Exp, scale=-0.5, incs=[("S", 1)])
            E("tensor", "matmul", pb[3][:, :], onesk1[:, :], rowb[:, 0:TT],
              start=True, stop=True,
              waits=use_bank(3) + [("S", P.val("S"))], incs=[("T", 1)])
            tT = P.val("T")
            for h in range(NH):
                E("vector", "tensor_tensor", o_s[:, h * TT:(h + 1) * TT],
                  o_s[:, h * TT:(h + 1) * TT], pb[3][:, :], ALU.mult,
                  waits=[("T", tT)] if h == 0 else [], incs=[("V", 1)])
            bank_done(3)

            if i == 0:
                DBG("orms", o_s[:, :], [("V", P.val("V"))])
            def evac_wo(mg, g, bnk):
                E("vector", "tensor_tensor",
                  xt[:, g * D + mg * TT:g * D + (mg + 1) * TT],
                  xt[:, g * D + mg * TT:g * D + (mg + 1) * TT],
                  pb[bnk][:, :], ALU.add, waits=[("T", P.val("T"))],
                  incs=[("V", 1)])
                bank_done(bnk)

            emit_tokout("o", NH,
                        lambda k, g: o_s[:, k * TT + g * 128:k * TT +
                                         (g + 1) * 128],
                        evac_wo)

            if i == 0:
                DBG("y0", xt[:, :], [("V", P.val("V"))])
            emit_ln()

            def ffn_waves(w_key, bnk0, first_dma_full):
                dmavs = []
                for half in range(2):
                    ks, ke = half * 8, half * 8 + 8
                    tw = P.val("T") if (first_dma_full or half == 1) \
                        else P.val("T") - 4
                    E("sync", "dma_start",
                      out=slab[:, ks * TT:ke * TT].rearrange(
                          "p (k c) -> p k c", c=TT),
                      in_=wsc[w_key][ks:ke, :, cg * TT:(cg + 1) * TT].transpose(
                          [1, 0, 2]),
                      waits=[("T", tw)], incs=[("dma", 16)])
                    dmavs.append(P.val("dma"))
                for half in range(2):
                    for m in range(4):
                        bnk = bnk0 + m
                        mo = m * 128
                        for kk in range(8):
                            k = half * 8 + kk
                            ws = []
                            if kk == 0:
                                ws = [("dma", dmavs[half])]
                                if half == 0:
                                    ws += use_bank(bnk) + [("V", P.val("V"))]
                            E("tensor", "matmul", pb[bnk][:, :],
                              slab[:, k * TT + mo:k * TT + mo + 128],
                              xl[:, k * TT:(k + 1) * TT],
                              start=(k == 0), stop=(k == NKD - 1),
                              waits=ws, incs=[("T", 1)] if kk == 7 else [])

            for cg in range(NKH // 4):
                ffn_waves("1", 0, cg == 0)
                ffn_waves("2", 4, False)
                x2T = P.val("T")
                for m in range(4):
                    hk = cg * 4 + m
                    E("scalar", "activation",
                      sq_roll[:, (m % 2) * TT:(m % 2 + 1) * TT],
                      pb[m][:, :], AF.Sigmoid,
                      waits=[("T", x2T - (3 - m))] +
                            ([("V", P.val("V"))] if m == 0 else []),
                      incs=[("S", 1)])
                    E("vector", "tensor_tensor", hid_b[:, hk * TT:(hk + 1) * TT],
                      sq_roll[:, (m % 2) * TT:(m % 2 + 1) * TT],
                      pb[4 + m][:, :], ALU.mult,
                      waits=[("S", P.val("S"))], incs=[("V", 1)])
                    bank_done(m, "S")
                    bank_done(4 + m)
            hidV = P.val("V")
            if i == 0:
                DBG("hid0", hid_b, [("V", hidV)])

            w3_grps = [(0, 8), (8, 16), (16, 24), (24, 32), (32, 40),
                       (40, 44)]
            phaseT = P.val("T")
            tafter = []
            for mg in range(4):
                for gi, (ks, ke) in enumerate(w3_grps):
                    buf = gi % 2
                    gflat = mg * len(w3_grps) + gi
                    tw = tafter[gflat - 2] if gflat >= 2 else phaseT
                    E("sync", "dma_start",
                      out=slab[:, buf * 8 * TT:buf * 8 * TT +
                               (ke - ks) * TT].rearrange(
                          "p (k c) -> p k c", c=TT),
                      in_=wsc["3"][ks:ke, :, mg * TT:(mg + 1) * TT].transpose(
                          [1, 0, 2]),
                      waits=[("T", tw)], incs=[("dma", 16)])
                    dmav = P.val("dma")
                    for g in range(NGRP):
                        bnk = 4 + g
                        ws = ((use_bank(bnk) if gi == 0 else []) +
                              [("dma", dmav), ("V", hidV)])
                        for kk in range(ke - ks):
                            k = ks + kk
                            E("tensor", "matmul", pb[bnk][:, :],
                              hid_b[:, k * TT + g * 128:
                                    k * TT + (g + 1) * 128],
                              slab[:, buf * 8 * TT + kk * TT:
                                   buf * 8 * TT + (kk + 1) * TT],
                              start=(k == 0), stop=(k == NKH - 1),
                              waits=ws if kk == 0 else [],
                              incs=([("T", 1)] if kk == ke - ks - 1 else []))
                    tafter.append(P.val("T"))
                for g in range(NGRP):
                    bnk = 4 + g
                    E("vector", "tensor_tensor",
                      xt[:, g * D + mg * TT:g * D + (mg + 1) * TT],
                      xt[:, g * D + mg * TT:g * D + (mg + 1) * TT],
                      pb[bnk][:, :], ALU.add,
                      waits=[("T", P.val("T"))] if g == 0 else [],
                      incs=[("V", 1)])
                    bank_done(bnk)
            E("sync", "dma_start",
              out=out_d[i * TT:(i + 1) * TT, :].rearrange("(g p) d -> p g d",
                                                          p=128),
              in_=xt[:].rearrange("p (g d) -> p g d", g=NGRP),
              waits=[("V", P.val("V"))], incs=[("odma", 16)])

        P.wait("sync", "odma", P.val("odma"))
        P.wait("gpsimd", "gdma", P.val("gdma"))

        @block.sync
        def _(sync):
            P.replay("sync", sync, sems)

        @block.gpsimd
        def _(gpsimd):
            P.replay("gpsimd", gpsimd, sems)

        @block.scalar
        def _(scalar):
            P.replay("scalar", scalar, sems)

        @block.vector
        def _(vector):
            P.replay("vector", vector, sems)

        @block.tensor
        def _(tensor):
            P.replay("tensor", tensor, sems)

    return nc, P


def host_inputs(inputs, core, T_core):
    x = inputs["L0_attn_hidden_states"][core]
    return {
        "x": np.ascontiguousarray(x[:T_core]),
        "wq": np.ascontiguousarray(inputs["L0_attn_Wq"]),
        "wk": np.ascontiguousarray(inputs["L0_attn_Wk"]),
        "wi": np.ascontiguousarray(inputs["L0_attn_Wi"]),
        "wo": np.ascontiguousarray(inputs["L0_attn_Wo"]),
        "gwt": np.ascontiguousarray(inputs["L0_attn_g_weight"]),
        "w1t": np.ascontiguousarray(inputs["L0_ffn_w1"].T),
        "w2t": np.ascontiguousarray(inputs["L0_ffn_w2"].T),
        "w3t": np.ascontiguousarray(inputs["L0_ffn_w3"].T),
    }



def kernel(**inputs):
    T_core = 8192
    nc, _ = build(T_core)
    in_maps = [host_inputs(inputs, c, T_core) for c in range(4)]
    res = run_bass_kernel_spmd(nc, in_maps, core_ids=[0, 1, 2, 3])
    out = np.stack([res.results[c]["out"] for c in range(4)], axis=0)
    return out.astype(np.float32)
